# version 04
# baseline (speedup 1.0000x reference)
import numpy as np
import ml_dtypes
import concourse.bass as bass
import concourse.mybir as mybir
from concourse.bass_utils import run_bass_kernel_spmd

F32 = mybir.dt.float32
BF16 = mybir.dt.bfloat16
AF = mybir.ActivationFunctionType
ALU = mybir.AluOpType
AX = mybir.AxisListType

NCORES = 8
T = 2048
D = 1024
NCH = 8
H, DH = 16, 64
S = 8192
LN0_EPS = 1e-5
PLN_EPS = 1e-5
SSD_EPS = 1e-6

SB_BASE = 16512
SB_TOP = 229344
OFF_STREAM = SB_BASE
OFF_SMALL = SB_BASE + 65536
P0 = SB_BASE + 69632


def _c(x):
    return np.ascontiguousarray(x)



def _rope_tables():
    inv_freq = 1.0 / (10000.0 ** (np.arange(0, DH, 2, dtype=np.float32) / DH))
    t = np.arange(S, dtype=np.float32)
    freqs = np.outer(t, inv_freq)
    cos = np.concatenate([np.cos(freqs), np.cos(freqs)], -1)[:, ::2]
    sin = np.concatenate([np.sin(freqs), np.sin(freqs)], -1)[:, ::2]
    return cos, sin


def _host_consts(core):
    pos0 = (core % 4) * T
    cos, sin = _rope_tables()
    cos_e = np.repeat(cos[pos0:pos0 + T].T, 2, axis=0)
    sin_e = np.repeat(sin[pos0:pos0 + T].T, 2, axis=0)
    cos2 = _c(np.concatenate([cos_e, cos_e], 0).astype(np.float32))
    sin2 = _c(np.concatenate([sin_e, sin_e], 0).astype(np.float32))
    P = np.zeros((64, 64), np.float32)
    for i in range(32):
        P[2 * i, 2 * i + 1] = -1.0
        P[2 * i + 1, 2 * i] = 1.0
    P2 = np.zeros((128, 128), np.float32)
    P2[:64, :64] = P
    P2[64:, 64:] = P
    meta = np.zeros((64, 4), np.float32)
    meta[:, core % 4] = 1.0
    hsum = np.zeros((128, 2), np.float32)
    hsum[:64, 0] = 1.0
    hsum[64:, 1] = 1.0
    return dict(cos2=cos2, sin2=sin2, p2t=_c(P2.T), hsum=hsum,
                ident=np.eye(128, dtype=np.float32), meta=meta)


def _prep_shared(inputs):
    g = {k: np.asarray(v) for k, v in inputs.items()}
    pre = {}
    pre["wdq_t"] = _c(g["L0_attn_w_down_q"].T.astype(np.float32))
    pre["wdkv_t"] = _c(g["L0_attn_w_down_kv"].T.astype(np.float32))
    pre["upq_t"] = _c(g["L0_attn_w_up_q"].T.astype(np.float32))
    pre["upk_t"] = _c(g["L0_attn_w_up_k"].T.astype(np.float32))
    pre["upv_t"] = _c(g["L0_attn_w_up_v"].T.astype(np.float32))
    pre["wout_t"] = _c(g["L0_attn_w_out"].T.astype(np.float32))
    gam = g["L0_attn_ln_scale"].astype(np.float32)
    bet = g["L0_attn_ln_bias"].astype(np.float32)
    pre["l0_gamma"] = _c(gam.reshape(NCH, 128).T)
    pre["l0_beta"] = _c(bet.reshape(NCH, 128).T)
    pre["W_t"] = _c(g["L0_ffn_W"].T.astype(np.float32))
    pre["wp_t"] = _c(g["L0_ffn_Wp"].T.astype(np.float32))
    bp = g["L0_ffn_bp"].astype(np.float32)
    pre["bp_col"] = _c(bp.reshape(2, 128).T)
    pre["bp_row"] = _c(bp.reshape(1, 256))
    pre["sim_fm"] = _c(g["L0_ffn_sim"].astype(np.float32).reshape(2, 128, 4)
                       .transpose(1, 0, 2).reshape(128, 8))
    pre["temp"] = _c(g["L0_ffn_temp"].astype(np.float32).reshape(1, 1))
    pre["wd_t"] = _c(np.concatenate(
        [g[f"L0_ffn_Wd{i}"].T.astype(np.float32) for i in range(4)], axis=1))
    pre["wu_t"] = _c(np.concatenate(
        [g[f"L0_ffn_Wu{i}"].T.astype(np.float32) for i in range(4)], axis=0))
    pre["wq"] = _c(g["L1_attn_Wq"].astype(np.float32))
    pre["wk"] = _c(g["L1_attn_Wk"].astype(np.float32))
    pre["wv"] = _c(g["L1_attn_Wv"].astype(np.float32))
    pre["wo"] = _c(g["L1_attn_Wo"].astype(np.float32))
    pre["wa"] = _c(g["L1_attn_Wa"].astype(np.float32))
    pre["ba_row"] = _c(g["L1_attn_ba"].astype(np.float32).reshape(1, H))
    pre["w_router"] = _c(g["L1_ffn_w_router"].astype(np.float32))
    pre["w_gating"] = _c(g["L1_ffn_w_gating"].astype(np.float32))
    pre["w_gate"] = _c(g["L1_ffn_w_gate"].astype(np.float32))
    pre["w_up"] = _c(g["L1_ffn_w_up"].astype(np.float32))
    pre["w_down"] = _c(g["L1_ffn_w_down"].astype(np.float32))
    return g, pre



class B:
    def __init__(self):
        self.nc = bass.Bass("TRN2", target_bir_lowering=False)
        self._sem_ctx = []
        self._sems = {}
        self.cnt = {}
        self.ins = {}
        self.outs = {}
        self._psum = {}
        self.ns = ""

    def din(self, name, shape, dtype=F32):
        t = self.nc.dram_tensor(name, list(shape), dtype, kind="ExternalInput")
        self.ins[name] = t
        return t

    def dout(self, name, shape, dtype=F32):
        t = self.nc.dram_tensor(name, list(shape), dtype, kind="ExternalOutput")
        self.outs[name] = t
        return t

    def dint(self, name, shape, dtype=F32, shared=False):
        if shared:
            return self.nc.dram_tensor(name, list(shape), dtype,
                                       kind="Internal", addr_space="Shared")
        return self.nc.dram_tensor(name, list(shape), dtype, kind="Internal")

    def sb(self, name, off, shape, dtype):
        elems = 1
        for s in shape[1:]:
            elems *= s
        nbytes = elems * (2 if dtype == BF16 else 4)
        assert off >= SB_BASE and off + nbytes <= SB_TOP, \
            f"{name}: off {off}..{off+nbytes} out of range"
        return self.nc.alloc_sbuf_tensor_at(name, list(shape), dtype,
                                            offset=off)

    def psum(self, name, shape=(128, 1024), dtype=F32):
        if name not in self._psum:
            self._psum[name] = self.nc.alloc_psum_tensor(name, list(shape),
                                                         dtype)
        return self._psum[name]

    def sem(self, name):
        name = self.ns + name
        if name not in self._sems:
            cm = self.nc.semaphore(name)
            self._sems[name] = cm.__enter__()
            self._sem_ctx.append(cm)
            self.cnt[name] = 0
        return self._sems[name]

    def inc(self, inst, name, k=1):
        inst.then_inc(self.sem(name), k)
        key = self.ns + name
        self.cnt[key] += k
        return self.cnt[key]

    def wait(self, eng, name, val=None):
        if val is None:
            val = self.cnt[self.ns + name]
        eng.wait_ge(self.sem(name), val)

    def getcnt(self, name):
        return self.cnt.get(self.ns + name, 0)

    def next_phase(self, ns):
        self.nc.all_engine_barrier()
        self.nc.bump_psum(banks=-(8 - self.nc.psum_banks_remaining))
        self.ns = ns

    def barrier(self):
        self.nc.all_engine_barrier()

    def setup_consts(self, tile, vals):
        self._const_tile = tile
        self._const_vals = list(vals)
        for j, v in enumerate(self._const_vals):
            self.nc.const_aps.register(F32, float(v), tile[:, j:j + 1])

    def emit_const_memsets(self, eng):
        for j, v in enumerate(self._const_vals):
            eng.memset(self._const_tile[:, j:j + 1], float(v))


def mm_chain(b, blk, pfx, tiles, psum, evac, pre_pe=None, pre_ve=None,
             extra_wait=None):
    s_mm, s_ev = pfx + "m", pfx + "e"

    def pe(t):
        if pre_pe:
            pre_pe(t)
        for i, (info, kl) in enumerate(tiles):
            bank = i % 2
            if i >= 2:
                b.wait(t, s_ev, i - 1)
            if extra_wait:
                extra_wait(t, i)
            for k, (lh, rh) in enumerate(kl):
                mm = t.matmul(psum[:, bank * 512:bank * 512 + 512][
                    :, :rh.shape[-1]], lh, rh,
                    start=(k == 0), stop=(k == len(kl) - 1))
            b.inc(mm, s_mm, 1)

    def ve(v):
        if pre_ve:
            pre_ve(v)
        for i, (info, kl) in enumerate(tiles):
            bank = i % 2
            b.wait(v, s_mm, i + 1)
            n = kl[0][1].shape[-1]
            inst = evac(v, info, psum[:, bank * 512:bank * 512 + n])
            b.inc(inst, s_ev, 1)

    blk.tensor(pe)
    blk.vector(ve)



def emit_phase_A(b, io):
    nc = b.nc

    stream = b.sb("stream", OFF_STREAM, [128, NCH, T], F32)
    so = OFF_SMALL
    onescol_f = b.sb("onescol_f", so, [128, 1], F32); so += 32
    onesrow_f = b.sb("onesrow_f", so, [1, 128], F32); so += 512
    gam_sb = b.sb("gam_sb", so, [128, NCH], F32); so += 32
    bet_sb = b.sb("bet_sb", so, [128, NCH], F32); so += 32
    qp_sb = b.sb("qp_sb", so, [128, 8], F32); so += 32
    mx_sb = b.sb("mx_sb", so, [128, 4], F32); so += 32
    kqn_sb = b.sb("kqn_sb", so, [1, NCH], F32); so += 32
    hsum_sb = b.sb("hsum_sb", so, [128, 2], BF16); so += 32
    ident_f = b.sb("ident_f", so, [128, 128], F32); so += 512
    cst = b.sb("cst", so, [128, 8], F32); so += 32
    b.setup_consts(cst, [LN0_EPS, SSD_EPS, 2.0, -1.0])
    assert so <= P0

    ybf = b.sb("ybf", P0, [128, NCH, T], BF16)
    q_fm = b.sb("q_fm", P0, [128, NCH, T], BF16)
    c_q = b.sb("c_q", P0 + 32768, [128, 4, T], BF16)
    c_kv = b.sb("c_kv", P0 + 49152, [128, 4, T], BF16)
    v_tm = b.sb("v_tm", P0 + 65536, [128, 16, D], BF16)
    k_fm = b.sb("k_fm", P0 + 98304, [128, NCH, T], BF16)
    stt1 = b.sb("stt1", P0 + 98304, [1, T], F32)
    stt2 = b.sb("stt2", P0 + 106496, [1, T], F32)
    sq = b.sb("sq", P0 + 65536, [128, 1024], F32)
    row2k = b.sb("row2k", P0 + 69632, [1, T], F32)
    wslab = b.sb("wslab", P0 + 131072, [128, NCH, 512], BF16)
    wup = b.sb("wup", P0 + 131072, [128, 4, D], BF16)
    cos_sb = b.sb("cos_sb", P0 + 32768, [128, T], BF16)
    sin_sb = b.sb("sin_sb", P0 + 36864, [128, T], BF16)
    p2t_sb = b.sb("p2t_sb", P0 + 40960, [128, 128], BF16)
    rsc = b.sb("rsc", P0 + 41216, [128, 512], BF16)
    ksq = b.sb("ksq", P0 + 42240, [128, 1024], BF16)
    rowq = b.sb("rowq", P0 + 44288, [1, 128], F32)

    pA = b.psum("pA")
    pB = b.psum("pB")
    pS = b.psum("pS", (2, 1024))

    with nc.Block() as blk:
        def sy(e):
            i = e.dma_start(out=stream[:, :, :],
                            in_=io["x_fm"].ap().rearrange("(c p) t -> p c t",
                                                          p=128))
            b.inc(i, "ldx", 16)
            i = e.dma_start(out=gam_sb[:, :], in_=io["l0_gamma"][:, :])
            b.inc(i, "ld", 16)
            i = e.dma_start(out=bet_sb[:, :], in_=io["l0_beta"][:, :])
            b.inc(i, "ld", 16)
            i = e.dma_start(out=ident_f[:, :], in_=io["ident"][:, :])
            b.inc(i, "ld", 16)

        def gp(e):
            e.memset(onescol_f[:, :], 1.0)
            e.memset(onesrow_f[:, :], 1.0)
            b.emit_const_memsets(e)
            i = e.dma_start(out=hsum_sb[:, :], in_=io["hsum"][:, :])
            b.inc(i, "ldh", 16)
            i = e.dma_start(out=wslab[:, :, :],
                            in_=io["wdq_t"].ap().rearrange(
                                "(c p) n -> p c n", p=128))
            b.inc(i, "wq", 16)

        blk.sync(sy)
        blk.gpsimd(gp)

    with nc.Block() as blk:
        def sc(s):
            b.wait(s, "ldx", 16)
            for i in range(32):
                tt, c = divmod(i, NCH)
                if i >= 2:
                    b.wait(s, "lsr", i - 1)
                b.inc(s.activation(sq[:, (i % 2) * 512:(i % 2) * 512 + 512],
                                   stream[:, c, tt * 512:tt * 512 + 512],
                                   AF.Square), "lsq", 1)

        def pe(t):
            b.wait(t, "ldx", 16)
            for i in range(32):
                tt, c = divmod(i, NCH)
                b.wait(t, "lsq", i + 1)
                if c == 0 and tt >= 1:
                    b.wait(t, "lse", tt)
                t.matmul(pS[0:1, 0:512], onescol_f[:, 0:1],
                         stream[:, c, tt * 512:tt * 512 + 512],
                         start=(c == 0), stop=(c == NCH - 1))
                mm = t.matmul(pS[0:1, 512:1024], onescol_f[:, 0:1],
                              sq[:, (i % 2) * 512:(i % 2) * 512 + 512],
                              start=(c == 0), stop=(c == NCH - 1))
                b.inc(mm, "lsr", 1)

        def ve(v):
            for tt in range(4):
                b.wait(v, "lsr", (tt + 1) * NCH)
                v.tensor_scalar_mul(stt1[:1, tt * 512:tt * 512 + 512],
                                    pS[0:1, 0:512], 1.0 / D)
                i = v.tensor_scalar_mul(stt2[:1, tt * 512:tt * 512 + 512],
                                        pS[0:1, 512:1024], 1.0 / D)
                b.inc(i, "lse", 1)

        blk.scalar(sc)
        blk.tensor(pe)
        blk.vector(ve)
    b.barrier()

    with nc.Block() as blk:
        def ve(v):
            b.inc(v.tensor_tensor(row2k[:1, :], stt1[:1, :], stt1[:1, :],
                                  ALU.mult), "fv", 1)
            b.wait(v, "fv", 1)
            b.inc(v.tensor_tensor(stt2[:1, :], stt2[:1, :], row2k[:1, :],
                                  ALU.subtract), "fv", 1)

        def sc(s):
            b.wait(s, "fv", 2)
            b.inc(s.activation(stt2[:1, :], stt2[:1, :], AF.Ln,
                               bias=LN0_EPS), "fs", 1)
            b.wait(s, "fs", 1)
            b.inc(s.activation(stt2[:1, :], stt2[:1, :], AF.Exp, scale=-0.5),
                  "fs", 1)

        blk.vector(ve)
        blk.scalar(sc)
    b.barrier()

    with nc.Block() as blk:
        def pe(t):
            for tt in range(4):
                sl = slice(tt * 512, tt * 512 + 512)
                if tt >= 1:
                    b.wait(t, "nrm", tt * NCH)
                t.matmul(pB[:, 0:512], onesrow_f[:1, :], stt1[:1, sl],
                         start=True, stop=True)
                mm = t.matmul(pB[:, 512:1024], onesrow_f[:1, :],
                              stt2[:1, sl], start=True, stop=True)
                b.inc(mm, "nb", 1)

        def ve(v):
            b.wait(v, "ld", 48)
            for tt in range(4):
                sl = slice(tt * 512, tt * 512 + 512)
                b.wait(v, "nb", tt + 1)
                for c in range(NCH):
                    b.inc(v.tensor_tensor(sq[:, 0:512], stream[:, c, sl],
                                          pB[:, 0:512], ALU.subtract),
                          "n0", 1)
                    b.wait(v, "n0")
                    b.inc(v.tensor_tensor(sq[:, 512:1024], sq[:, 0:512],
                                          pB[:, 512:1024], ALU.mult), "n0", 1)
                    b.wait(v, "n0")
                    b.inc(v.tensor_scalar(ybf[:, c, sl], sq[:, 512:1024],
                                          gam_sb[:, c:c + 1],
                                          bet_sb[:, c:c + 1],
                                          ALU.mult, ALU.add), "nrm", 1)

        blk.tensor(pe)
        blk.vector(ve)
    b.barrier()

    def ctiles(dst):
        tiles = []
        for lt in range(4):
            for tt in range(4):
                kl = [(wslab[:, c, lt * 128:lt * 128 + 128],
                       ybf[:, c, tt * 512:tt * 512 + 512]) for c in range(NCH)]
                tiles.append(((dst, lt, tt), kl))
        return tiles

    def cevac(v, info, ps):
        dst, lt, tt = info
        return v.tensor_copy(dst[:, lt, tt * 512:tt * 512 + 512], ps)

    with nc.Block() as blk:
        def pre_pe(t):
            b.wait(t, "wq", 16)
        mm_chain(b, blk, "cq", ctiles(c_q), pA, cevac, pre_pe=pre_pe)
    b.barrier()
    with nc.Block() as blk:
        def gp(e):
            i = e.dma_start(out=wslab[:, :, :],
                            in_=io["wdkv_t"].ap().rearrange(
                                "(c p) n -> p c n", p=128))
            b.inc(i, "wq", 16)

        blk.gpsimd(gp)
        def pre_pe(t):
            b.wait(t, "wq", 32)
        mm_chain(b, blk, "ckv", ctiles(c_kv), pA, cevac, pre_pe=pre_pe)
    b.barrier()

    with nc.Block() as blk:
        def gp(e):
            i = e.dma_start(out=wup[:, :, :],
                            in_=io["upv_t"].ap().rearrange(
                                "(c p) n -> p c n", p=128))
            b.inc(i, "wu", 16)
        blk.gpsimd(gp)
        tiles = []
        for tt in range(16):
            for ph in range(2):
                kl = [(c_kv[:, lc, tt * 128:tt * 128 + 128],
                       wup[:, lc, ph * 512:ph * 512 + 512]) for lc in range(4)]
                tiles.append(((tt, ph), kl))

        def vevac(v, info, ps):
            tt, ph = info
            return v.tensor_copy(v_tm[:, tt, ph * 512:ph * 512 + 512], ps)

        def pre_pe(t):
            b.wait(t, "wu", 16)
        mm_chain(b, blk, "vt", tiles, pA, vevac, pre_pe=pre_pe)
    b.barrier()

    def emit_up(name, wname, src_c, dst, wsem):
        with nc.Block() as blk:
            def gp(e):
                i = e.dma_start(out=wup[:, :, :],
                                in_=io[wname].ap().rearrange(
                                    "(c p) n -> p c n", p=128))
                b.inc(i, wsem, 16)
            blk.gpsimd(gp)
            tiles = []
            for pc in range(NCH):
                for tt in range(4):
                    kl = [(wup[:, lc, pc * 128:pc * 128 + 128],
                           src_c[:, lc, tt * 512:tt * 512 + 512])
                          for lc in range(4)]
                    tiles.append(((pc, tt), kl))

            def uevac(v, info, ps):
                pc, tt = info
                return v.tensor_copy(dst[:, pc, tt * 512:tt * 512 + 512], ps)

            def pre_pe(t):
                b.wait(t, wsem)
            mm_chain(b, blk, name, tiles, pA, uevac, pre_pe=pre_pe)
        b.barrier()

    def emit_rope(name, dst):
        s_s, s_r = name + "s", name + "r"
        with nc.Block() as blk:
            def pe(t):
                for i in range(32):
                    pc, tt = divmod(i, 4)
                    bank = i % 2
                    if i >= 2:
                        b.wait(t, s_r, 3 * (i - 1))
                    mm = t.matmul(pB[:, bank * 512:bank * 512 + 512],
                                  p2t_sb[:, :],
                                  dst[:, pc, tt * 512:tt * 512 + 512],
                                  start=True, stop=True)
                    b.inc(mm, s_s, 1)

            def ve(v):
                for i in range(32):
                    pc, tt = divmod(i, 4)
                    bank = i % 2
                    sl = slice(tt * 512, tt * 512 + 512)
                    b.wait(v, s_s, i + 1)
                    b.inc(v.tensor_tensor(rsc[:, :], dst[:, pc, sl],
                                          cos_sb[:, sl], ALU.mult), s_r, 1)
                    b.wait(v, s_r)
                    b.inc(v.tensor_tensor(dst[:, pc, sl],
                                          pB[:, bank * 512:bank * 512 + 512],
                                          sin_sb[:, sl], ALU.mult), s_r, 1)
                    b.wait(v, s_r)
                    b.inc(v.tensor_tensor(dst[:, pc, sl], dst[:, pc, sl],
                                          rsc[:, :], ALU.add), s_r, 1)

            blk.tensor(pe)
            blk.vector(ve)
        b.barrier()

    emit_up("upq", "upq_t", c_q, q_fm, "wuq")
    with nc.Block() as blk:
        def gp(e):
            i = e.dma_start(out=cos_sb[:, :], in_=io["cos2"][:, :])
            b.inc(i, "rl", 16)
            i = e.dma_start(out=sin_sb[:, :], in_=io["sin2"][:, :])
            b.inc(i, "rl", 16)
            i = e.dma_start(out=p2t_sb[:, :], in_=io["p2t"][:, :])
            b.inc(i, "rl", 16)

        def ve(v):
            b.wait(v, "rl", 48)
        blk.gpsimd(gp)
        blk.vector(ve)
    b.barrier()
    emit_rope("rpq", q_fm)
    emit_up("upk", "upk_t", c_kv, k_fm, "wuk")
    emit_rope("rpk", k_fm)

    with nc.Block() as blk:
        def sc(s):
            for i in range(32):
                pc, tt = divmod(i, 4)
                if i >= 2:
                    b.wait(s, "kn1", i - 1)
                b.inc(s.activation(ksq[:, (i % 2) * 512:(i % 2) * 512 + 512],
                                   k_fm[:, pc, tt * 512:tt * 512 + 512],
                                   AF.Square), "kn0", 1)

        def pe(t):
            b.wait(t, "ld", 48)
            b.wait(t, "ldh", 16)
            for i in range(32):
                b.wait(t, "kn0", i + 1)
                if i >= 2:
                    b.wait(t, "kn2", i - 1)
                mm = t.matmul(pS[0:2, (i % 2) * 512:(i % 2) * 512 + 512],
                              hsum_sb[:, :],
                              ksq[:, (i % 2) * 512:(i % 2) * 512 + 512],
                              start=True, stop=True)
                b.inc(mm, "kn1", 1)

        def ve(v):
            for i in range(32):
                pc, tt = divmod(i, 4)
                b.wait(v, "kn1", i + 1)
                if tt == 0 and pc >= 1:
                    b.wait(v, "kn3", pc)
                b.inc(v.reduce_max(mx_sb[0:2, tt:tt + 1],
                                   pS[0:2, (i % 2) * 512:(i % 2) * 512 + 512],
                                   axis=AX.X), "kn2", 1)
                if tt == 3:
                    b.wait(v, "kn2")
                    b.inc(v.reduce_max(qp_sb[0:2, pc:pc + 1], mx_sb[0:2, :],
                                       axis=AX.X), "kn3", 1)

        blk.scalar(sc)
        blk.tensor(pe)
        blk.vector(ve)
    b.barrier()
    with nc.Block() as blk:
        def pe(t):
            b.inc(t.transpose(pB[0:8, 0:2], qp_sb[0:2, 0:8],
                              ident_f[0:2, 0:2]), "kn4", 1)

        def ve(v):
            b.wait(v, "kn4", 1)
            b.inc(v.tensor_copy(mx_sb[0:8, 0:2], pB[0:8, 0:2]), "kn5", 1)
            b.wait(v, "kn5", 1)
            b.inc(v.reduce_max(qp_sb[0:8, 0:1], mx_sb[0:8, 0:2], axis=AX.X),
                  "kn5", 1)

        blk.tensor(pe)
        blk.vector(ve)
    b.barrier()
    with nc.Block() as blk:
        def pe(t):
            b.inc(t.transpose(pB[0:1, 0:8], qp_sb[0:8, 0:1],
                              ident_f[0:8, 0:8]), "kn6", 1)

        def ve(v):
            b.wait(v, "kn6", 1)
            b.inc(v.tensor_copy(kqn_sb[0:1, 0:8], pB[0:1, 0:8]), "kn7", 1)

        blk.tensor(pe)
        blk.vector(ve)
    b.barrier()

    with nc.Block() as blk:
        def sy(e):
            i = e.dma_start(out=io["qf"].ap().rearrange("(c p) t -> p c t",
                                                        p=128),
                            in_=q_fm[:, :, :])
            b.inc(i, "stA", 16)
            i = e.dma_start(out=io["kf"].ap().rearrange("(c p) t -> p c t",
                                                        p=128),
                            in_=k_fm[:, :, :])
            b.inc(i, "stA", 16)
            i = e.dma_start(out=io["vt"].ap().rearrange("(tt p) P -> p tt P",
                                                        p=128),
                            in_=v_tm[:, :, :])
            b.inc(i, "stA", 16)
            i = e.dma_start(out=io["kqn"][:, :], in_=kqn_sb[:1, :])
            b.inc(i, "stA", 16)
            b.wait(e, "stA")

        blk.sync(sy)
    return dict(stream=stream, q_fm=q_fm, k_fm=k_fm, v_tm=v_tm)


def build_PA():
    b = B()
    io = {}
    io["x_fm"] = b.din("x_fm", [D, T])
    for n, sh in [("wdq_t", [D, 512]), ("wdkv_t", [D, 512]),
                  ("upq_t", [512, D]), ("upk_t", [512, D]),
                  ("upv_t", [512, D]), ("l0_gamma", [128, NCH]),
                  ("l0_beta", [128, NCH]), ("cos2", [128, T]),
                  ("sin2", [128, T]), ("p2t", [128, 128]),
                  ("ident", [128, 128]), ("hsum", [128, 2])]:
        io[n] = b.din(n, sh)
    io["qf"] = b.dout("qf", [D, T], BF16)
    io["kf"] = b.dout("kf", [D, T], BF16)
    io["vt"] = b.dout("vt", [T, D], BF16)
    io["kqn"] = b.dout("kqn", [1, NCH])
    emit_phase_A(b, io)
    b.nc.finalize()
    return b



def emit_phase_B(b, io):
    nc = b.nc
    stream = b.sb("streamB", OFF_STREAM, [128, NCH, T], F32)
    so = OFF_SMALL
    onescol_f = b.sb("onescol_fB", so, [128, 1], F32); so += 32
    onesrow_f = b.sb("onesrow_fB", so, [1, 128], F32); so += 512
    ident_f = b.sb("ident_fB", so, [128, 128], F32); so += 512
    kqn4 = b.sb("kqn4", so, [4, NCH], F32); so += 32
    rmax8 = b.sb("rmax8", so, [NCH, 4], F32); so += 32
    kq8 = b.sb("kq8", so, [NCH, 1], F32); so += 32
    nsh = b.sb("nsh", so, [1, NCH], F32); so += 32
    nsh2 = b.sb("nsh2", so, [1, NCH], F32); so += 32
    biasp = b.sb("biasp", so, [128, NCH], F32); so += 32
    qp_sb = b.sb("qp_sbB", so, [2, NCH], F32); so += 32
    mx_sb = b.sb("mx_sbB", so, [2, 4], F32); so += 32
    hsum_sb = b.sb("hsum_sbB", so, [128, 2], BF16); so += 32
    cstB = b.sb("cstB", so, [128, 8], F32); so += 32
    assert so <= P0

    q_fm = b.sb("q_fmB", P0, [128, NCH, T], BF16)
    attn_out = b.sb("attn_out", P0 + 32768, [128, NCH, T], BF16)
    kpair = b.sb("kpair", P0 + 65536, [128, 2, S], BF16)
    v_aug = b.sb("v_aug", P0 + 98304, [128, 4, 64, 65], BF16)
    probsT = b.sb("probsT", P0 + 131584, [128, 2, 1024], BF16)
    rb_sb = b.sb("rb_sb", P0 + 135680, [64, 512], F32)
    rrow = b.sb("rrow", P0 + 137728, [1, 1024], F32)
    ksq = b.sb("ksqB", P0 + 65536, [128, 1024], BF16)
    wout_sb = b.sb("wout_sb", P0 + 65536, [128, NCH, D], BF16)
    stt1 = b.sb("stt1B", P0 + 81920, [1, T], F32)
    stt2 = b.sb("stt2B", P0 + 90112, [1, T], F32)
    sq = b.sb("sqB", P0 + 98304, [128, 1024], F32)
    row2k = b.sb("row2kB", P0 + 102400, [1, T], F32)

    pS4 = b.psum("pS4", (128, 2048))
    po = b.psum("po", (65, 1024))
    pB = b.psum("pBB", (128, 1024))
    b.setup_consts(cstB, [PLN_EPS])

    with nc.Block() as blk:
        def sy(e):
            i = e.dma_start(out=stream[:, :, :],
                            in_=io["x_fm"].ap().rearrange("(c p) t -> p c t",
                                                          p=128))
            b.inc(i, "ld", 16)
            i = e.dma_start(out=q_fm[:, :, :],
                            in_=io["qf"].ap().rearrange("(c p) t -> p c t",
                                                        p=128))
            b.inc(i, "ldq", 16)
            i = e.dma_start(out=kqn4[:, :], in_=io["kqn_all"][:, :])
            b.inc(i, "ldk", 16)
            i = e.dma_start(out=ident_f[:, :], in_=io["ident"][:, :])
            b.inc(i, "ldk", 16)

        def gp(e):
            e.memset(onescol_f[:, :], 1.0)
            e.memset(onesrow_f[:, :], 1.0)
            b.emit_const_memsets(e)
            e.memset(v_aug[:, :, :, :], 1.0)
            i = e.dma_start(out=hsum_sb[:, :], in_=io["hsum"][:, :])
            b.inc(i, "ldh", 16)

        blk.sync(sy)
        blk.gpsimd(gp)
    b.barrier()

    with nc.Block() as blk:
        def pe(t):
            b.wait(t, "ldk", 32)
            b.inc(t.transpose(pB[0:NCH, 0:4], kqn4[0:4, :], ident_f[0:4, 0:4]),
                  "km0", 1)

        def ve(v):
            b.wait(v, "km0", 1)
            b.inc(v.tensor_copy(rmax8[:, :], pB[0:NCH, 0:4]), "km1", 1)
            b.wait(v, "km1", 1)
            b.inc(v.reduce_max(kq8[:, 0:1], rmax8[:, :], axis=AX.X), "km1", 1)

        blk.tensor(pe)
        blk.vector(ve)
    b.barrier()

    with nc.Block() as blk:
        def sc(s):
            b.wait(s, "ldq", 16)
            for i in range(32):
                pc, tt = divmod(i, 4)
                if i >= 2:
                    b.wait(s, "qn1", i - 1)
                b.inc(s.activation(ksq[:, (i % 2) * 512:(i % 2) * 512 + 512],
                                   q_fm[:, pc, tt * 512:tt * 512 + 512],
                                   AF.Square), "qn0", 1)

        def pe(t):
            b.wait(t, "ldq", 16)
            b.wait(t, "ldh", 16)
            for i in range(32):
                b.wait(t, "qn0", i + 1)
                if i >= 2:
                    b.wait(t, "qn2", i - 1)
                mm = t.matmul(pS4[0:2, (i % 2) * 512:(i % 2) * 512 + 512],
                              hsum_sb[:, :],
                              ksq[:, (i % 2) * 512:(i % 2) * 512 + 512],
                              start=True, stop=True)
                b.inc(mm, "qn1", 1)

        def ve(v):
            for i in range(32):
                pc, tt = divmod(i, 4)
                b.wait(v, "qn1", i + 1)
                if tt == 0 and pc >= 1:
                    b.wait(v, "qn3", pc)
                b.inc(v.reduce_max(mx_sb[0:2, tt:tt + 1],
                                   pS4[0:2, (i % 2) * 512:(i % 2) * 512 + 512],
                                   axis=AX.X), "qn2", 1)
                if tt == 3:
                    b.wait(v, "qn2")
                    b.inc(v.reduce_max(qp_sb[0:2, pc:pc + 1], mx_sb[0:2, :],
                                       axis=AX.X), "qn3", 1)

        blk.scalar(sc)
        blk.tensor(pe)
        blk.vector(ve)
    b.barrier()
    with nc.Block() as blk:
        def pe(t):
            b.inc(t.transpose(pB[0:NCH, 0:2], qp_sb[0:2, 0:NCH],
                              ident_f[0:2, 0:2]), "qm0", 1)

        def ve(v):
            b.wait(v, "qm0", 1)
            b.inc(v.tensor_copy(rmax8[:, 0:2], pB[0:NCH, 0:2]), "qm1", 1)
            b.wait(v, "qm1", 1)
            b.inc(v.reduce_max(rmax8[:, 2:3], rmax8[:, 0:2], axis=AX.X),
                  "qm1", 1)
            b.wait(v, "qm1", 2)
            b.inc(v.tensor_tensor(rmax8[:, 3:4], rmax8[:, 2:3], kq8[:, 0:1],
                                  ALU.mult), "qm1", 1)

        blk.tensor(pe)
        blk.vector(ve)
    b.barrier()
    with nc.Block() as blk:
        def sc(s):
            b.inc(s.activation(rmax8[:, 3:4], rmax8[:, 3:4], AF.Ln), "qs0", 1)
            b.wait(s, "qs0", 1)
            b.inc(s.activation(rmax8[:, 3:4], rmax8[:, 3:4], AF.Exp,
                               scale=0.5), "qs0", 1)

        def ve(v):
            b.wait(v, "qs0", 2)
            b.inc(v.tensor_scalar_mul(rmax8[:, 3:4], rmax8[:, 3:4], -0.125),
                  "qs1", 1)

        def pe(t):
            b.wait(t, "qs1", 1)
            b.inc(t.transpose(pB[0:1, 0:NCH], rmax8[0:NCH, 3:4],
                              ident_f[0:NCH, 0:NCH]), "qs2", 1)

        blk.scalar(sc)
        blk.vector(ve)
        blk.tensor(pe)
    b.barrier()
    with nc.Block() as blk:
        def ve(v):
            b.inc(v.tensor_copy(nsh[0:1, :], pB[0:1, 0:NCH]), "qs3", 1)

        def pe(t):
            b.wait(t, "qs3", 1)
            for hp in range(NCH):
                mm = t.matmul(pB[:, hp * 128:hp * 128 + 1],
                              onesrow_f[:1, :], nsh[0:1, hp:hp + 1],
                              start=True, stop=True)
                b.inc(mm, "qs4", 1)

        def ve2(v):
            for hp in range(NCH):
                b.wait(v, "qs4", hp + 1)
                b.inc(v.tensor_copy(biasp[:, hp:hp + 1],
                                    pB[:, hp * 128:hp * 128 + 1]), "qs5", 1)

        blk.vector(ve)
        blk.tensor(pe)
    b.barrier()
    with nc.Block() as blk:
        blk.vector(ve2)
    b.barrier()

    with nc.Block() as blk:
        def sy(e):
            for hp in range(NCH):
                buf = hp % 2
                sname = "kvl%d" % buf
                if hp >= 2:
                    b.wait(e, "pav", (hp - 1) * 4 * 64)
                i = e.dma_start(out=kpair[:, buf, :],
                                in_=io["k_b"].ap()[hp * 128:hp * 128 + 128, :])
                b.inc(i, sname, 16)
                for hh in range(2):
                    srcv = io["v_b"].ap()[:, (2 * hp + hh) * 64:
                                          (2 * hp + hh) * 64 + 64]
                    i = e.dma_start(
                        out=v_aug[:, buf * 2 + hh, :, 0:64],
                        in_=srcv.rearrange("(kt p) d -> p kt d", p=128))
                    b.inc(i, sname, 16)

        def pe(t):
            for hp in range(NCH):
                buf = hp % 2
                b.wait(t, "kvl%d" % buf, (hp // 2 + 1) * 48)
                for qt in range(4):
                    rnd = hp * 4 + qt
                    qsl = slice(qt * 512, qt * 512 + 512)
                    if rnd >= 1:
                        b.wait(t, "fin", rnd)
                    for j in range(64):
                        p = j % 2
                        if rnd * 64 + j - 1 >= 1:
                            b.wait(t, "act", rnd * 64 + j - 1)
                        ksl = slice(j * 128, j * 128 + 128)
                        t.matmul(pS4[:, p * 1024:p * 1024 + 512],
                                 kpair[0:64, buf, ksl],
                                 q_fm[0:64, hp, qsl],
                                 start=True, stop=True)
                        mm = t.matmul(pS4[:, p * 1024 + 512:p * 1024 + 1024],
                                      kpair[64:128, buf, ksl],
                                      q_fm[64:128, hp, qsl],
                                      start=True, stop=True,
                                      tile_position=(64, 0))
                        b.inc(mm, "pes", 1)
                        if j >= 1:
                            pm = (j - 1) % 2
                            b.wait(t, "act", rnd * 64 + j)
                            t.matmul(po[:, 0:512],
                                     v_aug[:, buf * 2, j - 1, :],
                                     probsT[:, pm, 0:512],
                                     start=(j - 1 == 0), stop=False)
                            mm = t.matmul(po[:, 512:1024],
                                          v_aug[:, buf * 2 + 1, j - 1, :],
                                          probsT[:, pm, 512:1024],
                                          start=(j - 1 == 0), stop=False)
                            b.inc(mm, "pav", 1)
                    b.wait(t, "act", rnd * 64 + 64)
                    t.matmul(po[:, 0:512], v_aug[:, buf * 2, 63, :],
                             probsT[:, 1, 0:512], start=False, stop=True)
                    mm = t.matmul(po[:, 512:1024], v_aug[:, buf * 2 + 1, 63, :],
                                  probsT[:, 1, 512:1024],
                                  start=False, stop=True)
                    b.inc(mm, "pav", 1)
                    b.wait(t, "fr0", rnd + 1)
                    t.matmul(pB[0:64, 0:512], onesrow_f[:1, 0:64],
                             rrow[0:1, 0:512], start=True, stop=True)
                    mm = t.matmul(pB[0:64, 512:1024], onesrow_f[:1, 0:64],
                                  rrow[0:1, 512:1024], start=True, stop=True)
                    b.inc(mm, "fr1", 1)

        def sc(s):
            for hp in range(NCH):
                for qt in range(4):
                    rnd = hp * 4 + qt
                    for j in range(64):
                        p = j % 2
                        b.wait(s, "pes", rnd * 64 + j + 1)
                        if rnd * 64 + j - 1 >= 1:
                            b.wait(s, "pav", rnd * 64 + j - 1)
                        i = s.activation(probsT[:, p, :],
                                         pS4[:, p * 1024:p * 1024 + 1024],
                                         AF.Exp, bias=biasp[:, hp:hp + 1],
                                         scale=0.125)
                        b.inc(i, "act", 1)

        def ve(v):
            for hp in range(NCH):
                for qt in range(4):
                    rnd = hp * 4 + qt
                    qsl = slice(qt * 512, qt * 512 + 512)
                    b.wait(v, "pav", (rnd + 1) * 64)
                    v.reciprocal(rrow[0:1, 0:512], po[64:65, 0:512])
                    b.inc(v.reciprocal(rrow[0:1, 512:1024], po[64:65,
                                                               512:1024]),
                          "fr0", 1)
                    b.wait(v, "fr1", rnd + 1)
                    b.inc(v.tensor_copy(rb_sb[:, :], pB[0:64, 0:512]),
                          "fe", 1)
                    b.wait(v, "fe")
                    b.inc(v.tensor_tensor(attn_out[0:64, hp, qsl],
                                          po[0:64, 0:512], rb_sb[:, :],
                                          ALU.mult), "fe", 1)
                    b.wait(v, "fe")
                    b.inc(v.tensor_copy(rb_sb[:, :], pB[0:64, 512:1024]),
                          "fe", 1)
                    b.wait(v, "fe")
                    b.inc(v.tensor_tensor(attn_out[64:128, hp, qsl],
                                          po[0:64, 512:1024], rb_sb[:, :],
                                          ALU.mult), "fin", 1)

        blk.sync(sy)
        blk.tensor(pe)
        blk.scalar(sc)
        blk.vector(ve)
    b.barrier()

    with nc.Block() as blk:
        def gp(e):
            i = e.dma_start(out=wout_sb[:, :, :],
                            in_=io["wout_t"].ap().rearrange(
                                "(c p) n -> p c n", p=128))
            b.inc(i, "wol", 16)
        blk.gpsimd(gp)

        tiles = []
        for ot in range(NCH):
            for tt in range(4):
                kl = [(wout_sb[:, pc, ot * 128:ot * 128 + 128],
                       attn_out[:, pc, tt * 512:tt * 512 + 512])
                      for pc in range(NCH)]
                tiles.append(((ot, tt), kl))

        def wevac(v, info, ps):
            ot, tt = info
            sl = slice(tt * 512, tt * 512 + 512)
            return v.scalar_tensor_tensor(stream[:, ot, sl],
                                          stream[:, ot, sl], 2.0, ps,
                                          ALU.mult, ALU.add)

        def pre_pe(t):
            b.wait(t, "wol", 16)

        def pre_ve(v):
            b.wait(v, "ld", 16)
        mm_chain(b, blk, "wo", tiles, pB, wevac, pre_pe=pre_pe, pre_ve=pre_ve)
    b.barrier()

    emit_postln(b, stream, sq, row2k, stt1, stt2, onescol_f, onesrow_f,
                pS4, pB[:, 0:512], pB[:, 512:1024], "pln1")
    with nc.Block() as blk:
        def sy(e):
            i = e.dma_start(out=io["s1"].ap().rearrange("(c p) t -> p c t",
                                                        p=128),
                            in_=stream[:, :, :])
            b.inc(i, "stB", 16)
            b.wait(e, "stB", 16)

        blk.sync(sy)


def emit_postln(b, stream, sq, row2k, stt1, stt2, onescol_f, onesrow_f,
                psA, psB_mu, psB_rs, pfx):
    nc = b.nc
    s_sq, s_mm, s_ev = pfx + "q", pfx + "m", pfx + "e"
    with nc.Block() as blk:
        def sc(s):
            for i in range(32):
                tt, c = divmod(i, NCH)
                if i >= 2:
                    b.wait(s, s_mm, i - 1)
                b.inc(s.activation(sq[:, (i % 2) * 512:(i % 2) * 512 + 512],
                                   stream[:, c, tt * 512:tt * 512 + 512],
                                   AF.Square), s_sq, 1)

        def pe(t):
            for i in range(32):
                tt, c = divmod(i, NCH)
                b.wait(t, s_sq, i + 1)
                if c == 0 and tt >= 1:
                    b.wait(t, s_ev, tt)
                t.matmul(psA[0:1, 0:512], onescol_f[:, 0:1],
                         stream[:, c, tt * 512:tt * 512 + 512],
                         start=(c == 0), stop=(c == NCH - 1))
                mm = t.matmul(psA[0:1, 512:1024], onescol_f[:, 0:1],
                              sq[:, (i % 2) * 512:(i % 2) * 512 + 512],
                              start=(c == 0), stop=(c == NCH - 1))
                b.inc(mm, s_mm, 1)

        def ve(v):
            for tt in range(4):
                b.wait(v, s_mm, (tt + 1) * NCH)
                v.tensor_scalar_mul(stt1[:1, tt * 512:tt * 512 + 512],
                                    psA[0:1, 0:512], 1.0 / D)
                i = v.tensor_scalar_mul(stt2[:1, tt * 512:tt * 512 + 512],
                                        psA[0:1, 512:1024], 1.0 / D)
                b.inc(i, s_ev, 1)

        blk.scalar(sc)
        blk.tensor(pe)
        blk.vector(ve)
    b.barrier()
    with nc.Block() as blk:
        def ve(v):
            b.inc(v.tensor_tensor(row2k[:1, :], stt1[:1, :], stt1[:1, :],
                                  ALU.mult), pfx + "v", 1)
            b.wait(v, pfx + "v", 1)
            b.inc(v.tensor_tensor(stt2[:1, :], stt2[:1, :], row2k[:1, :],
                                  ALU.subtract), pfx + "v", 1)

        def sc(s):
            b.wait(s, pfx + "v", 2)
            b.inc(s.activation(stt2[:1, :], stt2[:1, :], AF.Ln,
                               bias=PLN_EPS), pfx + "s", 1)
            b.wait(s, pfx + "s", 1)
            b.inc(s.activation(stt2[:1, :], stt2[:1, :], AF.Exp, scale=-0.5),
                  pfx + "s", 1)

        blk.vector(ve)
        blk.scalar(sc)
    b.barrier()
    with nc.Block() as blk:
        def pe(t):
            for tt in range(4):
                sl = slice(tt * 512, tt * 512 + 512)
                if tt >= 1:
                    b.wait(t, pfx + "n", tt * NCH)
                t.matmul(psB_mu[:, 0:512], onesrow_f[:1, :], stt1[:1, sl],
                         start=True, stop=True)
                mm = t.matmul(psB_rs[:, 0:512], onesrow_f[:1, :],
                              stt2[:1, sl], start=True, stop=True)
                b.inc(mm, pfx + "b", 1)

        def ve(v):
            for tt in range(4):
                sl = slice(tt * 512, tt * 512 + 512)
                b.wait(v, pfx + "b", tt + 1)
                for c in range(NCH):
                    b.inc(v.tensor_tensor(stream[:, c, sl], stream[:, c, sl],
                                          psB_mu[:, 0:512], ALU.subtract),
                          pfx + "0", 1)
                    b.wait(v, pfx + "0")
                    b.inc(v.tensor_tensor(stream[:, c, sl], stream[:, c, sl],
                                          psB_rs[:, 0:512], ALU.mult),
                          pfx + "n", 1)

        blk.tensor(pe)
        blk.vector(ve)
    b.barrier()


def build_PB():
    b = B()
    io = {}
    io["x_fm"] = b.din("x_fm", [D, T])
    io["qf"] = b.din("qf", [D, T], BF16)
    io["k_b"] = b.din("k_b", [D, S], BF16)
    io["v_b"] = b.din("v_b", [S, D], BF16)
    io["kqn_all"] = b.din("kqn_all", [4, NCH])
    io["wout_t"] = b.din("wout_t", [D, D])
    io["ident"] = b.din("ident", [128, 128])
    io["hsum"] = b.din("hsum", [128, 2])
    io["s1"] = b.dout("s1", [D, T])
    emit_phase_B(b, io)
    b.nc.finalize()
    return b



CLAMP_MAX = 4.605170185988092


def emit_phase_C(b, io):
    nc = b.nc
    stream = b.sb("streamC", OFF_STREAM, [128, NCH, T], F32)
    so = OFF_SMALL
    onescol_f = b.sb("onescol_fC", so, [128, 1], F32); so += 32
    onesrow_f = b.sb("onesrow_fC", so, [1, 128], F32); so += 512
    onesrow_bf = b.sb("onesrow_bfC", so, [1, 128], BF16); so += 256
    ident_f = b.sb("ident_fC", so, [128, 128], F32); so += 512
    bp_col = b.sb("bp_colC", so, [128, 2], F32); so += 32
    sim_sb = b.sb("sim_sbC", so, [128, NCH], F32); so += 32
    simn_sb = b.sb("simn_sbC", so, [128, NCH], F32); so += 32
    sqs_sb = b.sb("sqs_sbC", so, [128, NCH], F32); so += 32
    temp_sb = b.sb("temp_sbC", so, [1, 1], F32); so += 32
    s_col = b.sb("s_colC", so, [128, 1], F32); so += 32
    scn_sb = b.sb("scn_sbC", so, [1, 4], F32); so += 32
    scnb_sb = b.sb("scnb_sbC", so, [128, 4], F32); so += 32
    rinv = b.sb("rinvC", so, [128, 16], F32); so += 64
    nrm2 = b.sb("nrm2C", so, [128, 1], F32); so += 32
    negm = b.sb("negmC", so, [128, 1], F32); so += 32
    den_c = b.sb("den_cC", so, [128, 1], F32); so += 32
    logit_sb = b.sb("logit_sbC", so, [128, 4], F32); so += 32
    lw_sb = b.sb("lw_sbC", so, [128, 4], F32); so += 32
    msk_sb = b.sb("msk_sbC", so, [128, 4], F32); so += 32
    ret_sb = b.sb("ret_sbC", so, [128, 4], F32); so += 32
    selmap = b.sb("selmapC", so, [4, 15], BF16); so += 32
    bprow_sb = b.sb("bprow_sbC", so, [1, 256], F32); so += 1024
    cstC = b.sb("cstC", so, [128, 8], F32); so += 32
    assert so <= P0, so

    p = P0
    ybf = b.sb("ybfC", p, [128, NCH, T], BF16); p += 32768
    proj_fm = b.sb("proj_fmC", p, [128, 2, T], F32); p += 16384
    retT_sb = b.sb("retTC", p, [4, T], BF16); p += 4096
    dq_sb = b.sb("dqC", p, [15, T], BF16); p += 4096
    sc_sb = b.sb("scC", p, [15, 512], BF16); p += 1024
    sq256 = b.sb("sq256C", p, [128, 256], F32); p += 1024
    W_sb = b.sb("W_sbC", p, [128, NCH, D], BF16); p += 16384
    wp_sb = b.sb("wp_sbC", p, [128, NCH, 256], F32); p += 8192
    wd_sb = b.sb("wd_sbC", p, [128, NCH, 15], BF16); p += 256
    wu_sb = b.sb("wu_sbC", p, [15, D], BF16); p += 2048
    stt1 = b.sb("stt1C", p, [1, T], F32); p += 8192
    stt2 = b.sb("stt2C", p, [1, T], F32); p += 8192
    sq = b.sb("sqC", p, [128, 1024], F32); p += 4096
    row2k = b.sb("row2kC", p, [1, T], F32); p += 8192
    assert p <= SB_TOP, p

    pA = b.psum("pAC")
    pB = b.psum("pBC")
    pSt = b.psum("pStC", (2, 1024))

    b.setup_consts(cstC, [PLN_EPS])

    with nc.Block() as blk:
        def sy(e):
            i = e.dma_start(out=stream[:, :, :],
                            in_=io["s1"].ap().rearrange("(c p) t -> p c t",
                                                        p=128))
            b.inc(i, "ldx", 16)
            i = e.dma_start(out=ident_f[:, :], in_=io["ident"][:, :])
            b.inc(i, "ld", 16)
            i = e.dma_start(out=bp_col[:, :], in_=io["bp_col"][:, :])
            b.inc(i, "ld", 16)
            i = e.dma_start(out=sim_sb[:, :], in_=io["sim_fm"][:, :])
            b.inc(i, "ld", 16)
            i = e.dma_start(out=temp_sb[:, :], in_=io["temp"][:, :])
            b.inc(i, "ld", 16)

        def gp(e):
            e.memset(onescol_f[:, :], 1.0)
            e.memset(onesrow_f[:, :], 1.0)
            e.memset(onesrow_bf[:, :], 1.0)
            b.emit_const_memsets(e)
            i = e.dma_start(out=W_sb[:, :, :],
                            in_=io["W_t"].ap().rearrange("(c p) n -> p c n",
                                                         p=128))
            b.inc(i, "lw", 16)
            i = e.dma_start(out=wp_sb[:, :, :],
                            in_=io["wp_t"].ap().rearrange("(c p) n -> p c n",
                                                          p=128))
            b.inc(i, "lw", 16)
            i = e.dma_start(out=wd_sb[:, :, :],
                            in_=io["wd_t"].ap().rearrange("(c p) n -> p c n",
                                                          p=128))
            b.inc(i, "lw", 16)
            i = e.dma_start(out=wu_sb[:, :], in_=io["wu_t"][:, :])
            b.inc(i, "lw", 16)
            i = e.dma_start(out=selmap[:, :], in_=io["selmap"][:, :])
            b.inc(i, "lw", 16)
            i = e.dma_start(out=bprow_sb[:, :], in_=io["bp_row"][:, :])
            b.inc(i, "lw", 16)

        def ve(v):
            b.wait(v, "ldx", 16)
            for i in range(32):
                tt, c = divmod(i, NCH)
                b.inc(v.tensor_copy(ybf[:, c, tt * 512:tt * 512 + 512],
                                    stream[:, c, tt * 512:tt * 512 + 512]),
                      "yc", 1)

        blk.sync(sy)
        blk.gpsimd(gp)
        blk.vector(ve)
    b.barrier()

    with nc.Block() as blk:
        def pe(t):
            b.wait(t, "lw", 96)
            b.wait(t, "yc", 32)
            for i in range(16):
                bank = i % 2
                if i >= 2:
                    b.wait(t, "c1s", i - 1)
                for c in range(NCH):
                    t.matmul(pA[0:128, bank * 512:bank * 512 + 256],
                             stream[:, c, i * 128:i * 128 + 128],
                             wp_sb[:, c, :],
                             start=(c == 0), stop=False)
                mm = t.matmul(pA[0:128, bank * 512:bank * 512 + 256],
                              onesrow_f[:1, :], bprow_sb[:1, :],
                              start=False, stop=True)
                b.inc(mm, "c1m", 1)

        def sc(s):
            for i in range(16):
                bank = i % 2
                b.wait(s, "c1m", i + 1)
                if i >= 1:
                    b.wait(s, "c1n", i)
                b.inc(s.activation(sq256[:, :],
                                   pA[0:128, bank * 512:bank * 512 + 256],
                                   AF.Square), "c1s", 1)
                b.wait(s, "c1n", i + 1)
                b.inc(s.activation(nrm2[:, 0:1], nrm2[:, 0:1], AF.Ln),
                      "c1l", 1)
                b.wait(s, "c1l")
                b.inc(s.activation(rinv[:, i:i + 1], nrm2[:, 0:1], AF.Exp,
                                   scale=-0.5), "c1r", 1)

        def ve(v):
            for i in range(16):
                b.wait(v, "c1s", i + 1)
                if i >= 1:
                    b.wait(v, "c1r", i)
                b.inc(v.reduce_sum(nrm2[:, 0:1], sq256[:, :], axis=AX.X),
                      "c1n", 1)

        blk.tensor(pe)
        blk.scalar(sc)
        blk.vector(ve)
    b.barrier()

    with nc.Block() as blk:
        tiles = []
        for pt in range(2):
            for tt in range(4):
                kl = [(wp_sb[:, c, pt * 128:pt * 128 + 128],
                       stream[:, c, tt * 512:tt * 512 + 512])
                      for c in range(NCH)]
                tiles.append(((pt, tt), kl))

        def pevac(v, info, ps):
            pt, tt = info
            return v.tensor_scalar(proj_fm[:, pt, tt * 512:tt * 512 + 512],
                                   ps, bp_col[:, pt:pt + 1], None, ALU.add)

        def c2_pre_ve(v):
            b.wait(v, "ld", 64)
        mm_chain(b, blk, "c2", tiles, pB, pevac, pre_ve=c2_pre_ve)

        def sc(s):
            b.wait(s, "ld", 64)
            b.inc(s.activation(sqs_sb[:, :], sim_sb[:, :], AF.Square),
                  "sm0", 1)

        def pe(t):
            b.wait(t, "sm0", 1)
            for c in range(2):
                mm = t.matmul(pSt[0:1, 0:4], onescol_f[:, 0:1],
                              sqs_sb[:, c * 4:c * 4 + 4],
                              start=(c == 0), stop=(c == 1))
            b.inc(mm, "sm1", 1)

        blk.scalar(sc)
        blk.tensor(pe)
    b.barrier()
    with nc.Block() as blk:
        def sc2(s):
            b.inc(s.activation(scn_sb[:1, :], pSt[0:1, 0:4], AF.Ln),
                  "sm2", 1)
            b.wait(s, "sm2", 1)
            b.inc(s.activation(scn_sb[:1, :], scn_sb[:1, :], AF.Exp,
                               scale=-0.5), "sm2", 1)
        blk.scalar(sc2)
    b.barrier()
    with nc.Block() as blk:
        def pe2(t):
            b.inc(t.matmul(pB[0:128, 0:4], onesrow_f[:1, :], scn_sb[:1, :],
                           start=True, stop=True), "sm3", 1)

        def ve_a(v):
            b.wait(v, "sm3", 1)
            b.inc(v.tensor_copy(scnb_sb[:, :], pB[0:128, 0:4]), "sm4", 1)
            b.wait(v, "sm4")
            for c in range(2):
                b.inc(v.tensor_tensor(simn_sb[:, c * 4:c * 4 + 4],
                                      sim_sb[:, c * 4:c * 4 + 4],
                                      scnb_sb[:, :], ALU.mult), "sm4", 1)
            b.wait(v, "ld", 64)
            b.inc(v.tensor_scalar_min(temp_sb[:1, :], temp_sb[:1, :],
                                      CLAMP_MAX), "sm5", 1)

        blk.tensor(pe2)
        blk.vector(ve_a)
    b.barrier()
    with nc.Block() as blk:
        def sc3(s):
            b.inc(s.activation(temp_sb[:1, :], temp_sb[:1, :], AF.Exp),
                  "sm6", 1)
        blk.scalar(sc3)
    b.barrier()
    with nc.Block() as blk:
        def pe3(t):
            b.inc(t.matmul(pB[0:128, 4:5], onesrow_f[:1, :], temp_sb[:1, :],
                           start=True, stop=True), "sm7", 1)

        def ve2(v):
            b.wait(v, "sm7", 1)
            b.inc(v.tensor_copy(s_col[:, :], pB[0:128, 4:5]), "sm8", 1)

        blk.tensor(pe3)
        blk.vector(ve2)
    b.barrier()

    with nc.Block() as blk:
        def pe(t):
            for i in range(16):
                bank = i % 2
                for c in range(2):
                    mm = t.matmul(pA[0:128, bank * 512:bank * 512 + 4],
                                  proj_fm[:, c, i * 128:i * 128 + 128],
                                  simn_sb[:, c * 4:c * 4 + 4],
                                  start=(c == 0), stop=(c == 1))
                b.inc(mm, "c3m", 1)
                b.wait(t, "c3r", i + 1)
                b.inc(t.transpose(pB[0:4, bank * 512:bank * 512 + 128],
                                  ret_sb[:, :], ident_f[:, :]), "c3p", 1)

        def ve(v):
            for i in range(16):
                bank = i % 2
                b.wait(v, "c3m", i + 1)
                b.inc(v.tensor_tensor(nrm2[:, 0:1], rinv[:, i:i + 1],
                                      s_col[:, 0:1], ALU.mult), "c3v", 1)
                b.wait(v, "c3v")
                b.inc(v.tensor_scalar_mul(logit_sb[:, :],
                                          pA[0:128,
                                             bank * 512:bank * 512 + 4],
                                          nrm2[:, 0:1]), "c3v", 1)
                b.wait(v, "c3v")
                b.inc(v.reduce_max(negm[:, 0:1], logit_sb[:, :], axis=AX.X,
                                   negate=True), "c3v", 1)
                b.wait(v, "c3v")
                b.inc(v.tensor_scalar(msk_sb[:, :], logit_sb[:, :],
                                      negm[:, 0:1], 0.0, ALU.add, ALU.is_ge),
                      "c3k", 1)
                b.wait(v, "c3x", i + 1)
                b.inc(v.reduce_sum(den_c[:, 0:1], lw_sb[:, :], axis=AX.X),
                      "c3w", 1)
                b.wait(v, "c3w")
                b.inc(v.reciprocal(den_c[:, 0:1], den_c[:, 0:1]), "c3w", 1)
                b.wait(v, "c3w")
                b.inc(v.tensor_scalar_mul(lw_sb[:, :], lw_sb[:, :],
                                          den_c[:, 0:1]), "c3w", 1)
                b.wait(v, "c3w")
                b.inc(v.tensor_tensor(ret_sb[:, :], lw_sb[:, :],
                                      msk_sb[:, :], ALU.mult), "c3r", 1)
                b.wait(v, "c3p", i + 1)
                b.inc(v.tensor_copy(retT_sb[:, i * 128:i * 128 + 128],
                                    pB[0:4, bank * 512:bank * 512 + 128]),
                      "c3t", 1)

        def sc(s):
            for i in range(16):
                b.wait(s, "c3k", i + 1)
                b.inc(s.activation(lw_sb[:, :], logit_sb[:, :], AF.Exp,
                                   bias=negm[:, 0:1]), "c3x", 1)

        blk.tensor(pe)
        blk.vector(ve)
        blk.scalar(sc)
    b.barrier()

    with nc.Block() as blk:
        def pe(t):
            for tt in range(4):
                bank = tt % 2
                if tt >= 2:
                    b.wait(t, "c4e", 2 * (tt - 1))
                for c in range(NCH):
                    mm = t.matmul(pA[0:15, bank * 512:bank * 512 + 512],
                                  wd_sb[:, c, :],
                                  ybf[:, c, tt * 512:tt * 512 + 512],
                                  start=(c == 0), stop=(c == NCH - 1))
                b.inc(mm, "c4m", 1)
                mm = t.matmul(pB[0:15, bank * 512:bank * 512 + 512],
                              selmap[:, :],
                              retT_sb[:, tt * 512:tt * 512 + 512],
                              start=True, stop=True)
                b.inc(mm, "c4s", 1)

        def ve(v):
            for tt in range(4):
                bank = tt % 2
                b.wait(v, "c4s", tt + 1)
                if tt >= 1:
                    b.wait(v, "c4e", 2 * tt)
                b.inc(v.tensor_copy(sc_sb[:, :],
                                    pB[0:15, bank * 512:bank * 512 + 512]),
                      "c4e", 1)
                b.wait(v, "c4m", tt + 1)
                b.wait(v, "c4e")
                b.inc(v.tensor_tensor(dq_sb[:, tt * 512:tt * 512 + 512],
                                      pA[0:15, bank * 512:bank * 512 + 512],
                                      sc_sb[:, :], ALU.mult), "c4e", 1)

        blk.tensor(pe)
        blk.vector(ve)
    b.barrier()

    with nc.Block() as blk:
        tiles = []
        for ot in range(NCH):
            for tt in range(4):
                kl = [(W_sb[:, c, ot * 128:ot * 128 + 128],
                       ybf[:, c, tt * 512:tt * 512 + 512])
                      for c in range(NCH)]
                kl.append((wu_sb[0:15, ot * 128:ot * 128 + 128],
                           dq_sb[0:15, tt * 512:tt * 512 + 512]))
                tiles.append(((ot, tt), kl))

        def cevac(v, info, ps):
            ot, tt = info
            sl = slice(tt * 512, tt * 512 + 512)
            return v.tensor_tensor(stream[:, ot, sl], stream[:, ot, sl],
                                   ps, ALU.add)

        mm_chain(b, blk, "c5", tiles, pB, cevac)
    b.barrier()

    emit_postln(b, stream, sq, row2k, stt1, stt2, onescol_f, onesrow_f,
                pSt, pB[:, 0:512], pB[:, 512:1024], "pln2")
    with nc.Block() as blk:
        def sy(e):
            i = e.dma_start(out=io["s2"].ap().rearrange("(c p) t -> p c t",
                                                        p=128),
                            in_=stream[:, :, :])
            b.inc(i, "stC", 16)
            b.wait(e, "stC", 16)

        blk.sync(sy)


def build_PC():
    b = B()
    io = {}
    io["s1"] = b.din("s1", [D, T])
    io["W_t"] = b.din("W_t", [D, D])
    io["wp_t"] = b.din("wp_t", [D, 256])
    io["bp_col"] = b.din("bp_col", [128, 2])
    io["bp_row"] = b.din("bp_row", [1, 256])
    io["sim_fm"] = b.din("sim_fm", [128, NCH])
    io["temp"] = b.din("temp", [1, 1])
    io["wd_t"] = b.din("wd_t", [D, 15])
    io["wu_t"] = b.din("wu_t", [15, D])
    io["selmap"] = b.din("selmap", [4, 15])
    io["ident"] = b.din("ident", [128, 128])
    io["s2"] = b.dout("s2", [D, T])
    emit_phase_C(b, io)
    b.nc.finalize()
    return b



NHALF = 2
NCK = 8
_NCAP = None
_X = set()


class SSDCtx:
    pass


def alloc_ssd(b, with_pass2):
    c = SSDCtx()
    so = OFF_SMALL
    c.onescol_f = b.sb("onescol_fD", so, [128, 1], F32); so += 32
    c.onesrow_f = b.sb("onesrow_fD", so, [1, 128], F32); so += 512
    c.ident_f = b.sb("ident_fD", so, [128, 128], F32); so += 512
    c.ident_bf = b.sb("ident_bfD", so, [128, 128], BF16); so += 256
    c.ba2 = b.sb("ba2D", so, [16, 1], F32); so += 32
    c.tot_la = b.sb("tot_laD", so, [16, 1], F32); so += 32
    c.lrow = b.sb("lrowD", so, [1, 16], F32); so += 32
    c.b1_sb = b.sb("b1_sbD", so, [128, 2, 128], F32); so += 1024
    c.tmp_sb = b.sb("tmp_sbD", so, [128, 2], F32); so += 32
    c.e_part = b.sb("e_partD", so, [128, 2], F32); so += 32
    c.lamC = b.sb("lamCD", so, [64, 8], F32); so += 32
    c.k1_sb = b.sb("k1_sbD", so, [128, 2, 64], BF16); so += 256
    c.dl2_sb = b.sb("dl2_sbD", so, [128, 2, 128], F32); so += 1024
    c.d_sb = b.sb("d_sbD", so, [128, 2, 128], BF16); so += 512
    c.at_sb = b.sb("at_sbD", so, [128, 2, 128], BF16); so += 512
    c.qc_sb = b.sb("qc_sbD", so, [64, 4, 128], BF16); so += 1024
    c.q1_sb = b.sb("q1_sbD", so, [64, 2, 128], BF16); so += 512
    c.rinv = b.sb("rinvD", so, [128, 1], F32); so += 32
    c.cstD = b.sb("cstD", so, [128, 8], F32); so += 32
    c.lam_sb = b.sb("lam_sbD", so, [64, 4, 128], F32); so += 2048
    assert so <= P0 + 8192, so

    p = P0 + 8192
    c.ybf = b.sb("ybfD", p, [128, NCH, 1024], BF16); p += 16384
    c.v_aug = b.sb("v_augD", p, [128, NCK, 16, 65], BF16); p += 16640
    c.slab = b.sb("slabD", p, [128, NCH, 1024], BF16)
    c.kTall = b.sb("kTallD", p, [128, NCK, 8, 128], BF16)
    p += 16384
    c.wa_sb = b.sb("wa_sbD", p, [128, NCH, 16], BF16); p += 256
    c.a_sb_off = p
    c.a_sb = b.sb("a_sbD", p, [16, 1024], F32)
    c.elu1 = b.sb("elu1D", p, [128, 512], F32)
    c.elu2 = b.sb("elu2D", p + 2048, [128, 512], F32)
    p += 4096
    c.cum_fm = b.sb("cum_fmD", p, [16, 1024], F32); p += 4096
    c.cum_tm = b.sb("cum_tmD", p, [128, NCK, 16], F32); p += 512
    c.stt = [None, None]
    c.stt[0] = b.sb("stt0D", p, [64, 16, 65], F32); p += 4160
    c.stt[1] = b.sb("stt1D_", p, [64, 16, 65], F32); p += 4160
    c.stt_bf = b.sb("sttbfD", p, [64, 2, 65], BF16); p += 288
    c.lamp = b.sb("lampD", p, [64, 16], F32); p += 64
    c.lamrep = b.sb("lamrepD", p, [64, 16], F32); p += 64
    c.meta_sb = b.sb("meta_sbD", p, [64, 4], F32); p += 64
    c.maskneg = b.sb("masknegD", c.a_sb_off, [128, 128], F32)
    if with_pass2:
        c.scanout = b.sb("scanoutD", p, [128, 16, 1024], BF16)
        c.sacc = b.sb("saccD", p, [64, 16, 65], F32)
        c.sacc2 = b.sb("sacc2D", p + 4160, [64, 16, 65], F32)
        c.sldp = b.sb("sldpD", p + 8320, [64, 16, 65], F32)
        c.sel2 = b.sb("sel2D", p + 12480, [64, 16, 65], F32)
        p += 32768
        c.k_tm = b.sb("k_tmD", p, [128, NCK, 1024], BF16)
        c.scanout_fm = b.sb("scanout_fmD", p, [128, NCH, T], BF16)
        p += 16384
        c.q_fm = b.sb("q_fmD", p, [128, NCH, 1024], BF16); p += 16384
    else:
        c.k_tm = b.sb("k_tmD", p, [128, NCK, 1024], BF16); p += 16384
        c.sacc = c.sldp = None
    assert p <= SB_TOP, p

    c.pM2 = b.psum("pM2D", (128, 1536))
    c.pAT = b.psum("pATD", (128, 1024))
    c.pNS = b.psum("pNSD", (128, 1024))
    c.pX = b.psum("pXD", (128, 512))
    return c


def emit_ssd_qkva(b, c, io, stream, half, with_q):
    nc = b.nc
    hof = half * 1024
    _names = ["yc", "wq", "qm", "qa", "qb", "qx", "qd", "wk", "km", "ka",
              "kb", "kx", "kd", "wv", "vm", "vd", "wa", "am", "ae", "al",
              "cs", "ts", "tt", "ct", "cte"]
    _base = {n: b.getcnt("qk_" + n) for n in _names}

    class _HX:
        def __add__(self, other):
            return "qk_" + other
    hx = _HX()

    _orig_wait = b.wait

    def _qwait(eng, name, val=None):
        if name.startswith("qk_") and val is not None:
            val = _base[name[3:]] + val
        _orig_wait(eng, name, val)

    pQ = c.pM2

    with nc.Block() as blk:
        def ve(v):
            _qwait(v, "ldx", 16)
            for i in range(16):
                tt, cch = divmod(i, NCH)
                sl = slice(hof + tt * 512, hof + tt * 512 + 512)
                b.inc(v.tensor_copy(c.ybf[:, cch, tt * 512:tt * 512 + 512],
                                    stream[:, cch, sl]), hx + "yc", 1)
        blk.vector(ve)
    b.barrier()

    def load_slab(wname, sem):
        def gp(e):
            i = e.dma_start(out=c.slab[:, :, :],
                            in_=io[wname].ap().rearrange("(c p) n -> p c n",
                                                         p=128))
            b.inc(i, sem, 16)
        return gp

    if with_q:
        with nc.Block() as blk:
            blk.gpsimd(load_slab("wq", hx + "wq"))

            def pe(t):
                _qwait(t, hx + "wq", 16)
                _qwait(t, hx + "yc", 16)
                for i in range(16):
                    pc, tt = divmod(i, 2)
                    bank = i % 2
                    if i >= 2:
                        _qwait(t, hx + "qd", i - 1)
                    for cch in range(NCH):
                        mm = t.matmul(pQ[:, bank * 512:bank * 512 + 512],
                                      c.slab[:, cch, pc * 128:pc * 128 + 128],
                                      c.ybf[:, cch, tt * 512:tt * 512 + 512],
                                      start=(cch == 0), stop=(cch == NCH - 1))
                    b.inc(mm, hx + "qm", 1)

            def ve(v):
                for i in range(16):
                    pc, tt = divmod(i, 2)
                    bank = i % 2
                    ps = pQ[:, bank * 512:bank * 512 + 512]
                    _qwait(v, hx + "qm", i + 1)
                    if i >= 1:
                        _qwait(v, hx + "qd", i)
                    b.inc(v.tensor_scalar_max(c.elu1[:, :], ps, 0.0),
                          hx + "qa", 1)
                    _qwait(v, hx + "qa")
                    b.inc(v.tensor_scalar_min(c.elu2[:, :], ps, 0.0),
                          hx + "qb", 1)
                    _qwait(v, hx + "qx", i + 1)
                    b.inc(v.tensor_tensor(
                        c.q_fm[:, pc, tt * 512:tt * 512 + 512],
                        c.elu1[:, :], c.elu2[:, :], ALU.add), hx + "qd", 1)

            def sc(s):
                for i in range(16):
                    _qwait(s, hx + "qb", i + 1)
                    b.inc(s.activation(c.elu2[:, :], c.elu2[:, :], AF.Exp),
                          hx + "qx", 1)

            blk.tensor(pe)
            blk.vector(ve)
            blk.scalar(sc)
        b.barrier()

    with nc.Block() as blk:
        blk.gpsimd(load_slab("wk", hx + "wk"))

        def pe(t):
            _qwait(t, hx + "wk", 16)
            _qwait(t, hx + "yc", 16)
            for i in range(16):
                ck, ph = divmod(i, 2)
                bank = i % 2
                if i >= 2:
                    _qwait(t, hx + "kd", i - 1)
                for cch in range(NCH):
                    mm = t.matmul(pQ[:, bank * 512:bank * 512 + 512],
                                  c.ybf[:, cch, ck * 128:ck * 128 + 128],
                                  c.slab[:, cch, ph * 512:ph * 512 + 512],
                                  start=(cch == 0), stop=(cch == NCH - 1))
                b.inc(mm, hx + "km", 1)

        def ve(v):
            for i in range(16):
                ck, ph = divmod(i, 2)
                bank = i % 2
                ps = pQ[:, bank * 512:bank * 512 + 512]
                _qwait(v, hx + "km", i + 1)
                if i >= 1:
                    _qwait(v, hx + "kd", i)
                b.inc(v.tensor_scalar_max(c.elu1[:, :], ps, 0.0), hx + "ka", 1)
                _qwait(v, hx + "ka")
                b.inc(v.tensor_scalar_min(c.elu2[:, :], ps, 0.0), hx + "kb", 1)
                _qwait(v, hx + "kx", i + 1)
                b.inc(v.tensor_tensor(
                    c.k_tm[:, ck, ph * 512:ph * 512 + 512],
                    c.elu1[:, :], c.elu2[:, :], ALU.add), hx + "kd", 1)

        def sc(s):
            for i in range(16):
                _qwait(s, hx + "kb", i + 1)
                b.inc(s.activation(c.elu2[:, :], c.elu2[:, :], AF.Exp),
                      hx + "kx", 1)

        blk.tensor(pe)
        blk.vector(ve)
        blk.scalar(sc)
    b.barrier()

    with nc.Block() as blk:
        blk.gpsimd(load_slab("wv", hx + "wv"))

        def pe(t):
            _qwait(t, hx + "wv", 16)
            for i in range(16):
                ck, ph = divmod(i, 2)
                bank = i % 2
                if i >= 2:
                    _qwait(t, hx + "vd", i - 1)
                for cch in range(NCH):
                    mm = t.matmul(pQ[:, bank * 512:bank * 512 + 512],
                                  c.ybf[:, cch, ck * 128:ck * 128 + 128],
                                  c.slab[:, cch, ph * 512:ph * 512 + 512],
                                  start=(cch == 0), stop=(cch == NCH - 1))
                b.inc(mm, hx + "vm", 1)

        def ve(v):
            for i in range(16):
                ck, ph = divmod(i, 2)
                bank = i % 2
                ps = pQ[:, bank * 512:bank * 512 + 512]
                _qwait(v, hx + "vm", i + 1)
                dst = c.v_aug[:, ck, ph * 8:ph * 8 + 8, 0:64]
                b.inc(v.tensor_copy(dst, ps.rearrange("p (h d) -> p h d",
                                                      d=64)), hx + "vd", 1)

        blk.tensor(pe)
        blk.vector(ve)
    b.barrier()

    with nc.Block() as blk:
        def gp(e):
            i = e.dma_start(out=c.wa_sb[:, :, :],
                            in_=io["wa"].ap().rearrange("(c p) n -> p c n",
                                                        p=128))
            b.inc(i, hx + "wa", 16)

        def pe(t):
            _qwait(t, hx + "wa", 16)
            _qwait(t, hx + "yc", 16)
            for tt in range(2):
                if tt == 1:
                    _qwait(t, hx + "ae", 1)
                for cch in range(NCH):
                    mm = t.matmul(pQ[0:16, 0:512],
                                  c.wa_sb[:, cch, :],
                                  c.ybf[:, cch, tt * 512:tt * 512 + 512],
                                  start=(cch == 0), stop=(cch == NCH - 1))
                b.inc(mm, hx + "am", 1)

        def sc(s):
            b.wait(s, "ldba", 1)
            for tt in range(2):
                _qwait(s, hx + "am", tt + 1)
                b.inc(s.activation(c.a_sb[:, tt * 512:tt * 512 + 512],
                                   pQ[0:16, 0:512], AF.Sigmoid,
                                   bias=c.ba2[0:16, 0:1]), hx + "ae", 1)
            _qwait(s, hx + "ae", 2)
            b.inc(s.activation(c.a_sb[:, :], c.a_sb[:, :], AF.Ln),
                  hx + "al", 1)

        def ve(v):
            _qwait(v, hx + "al", 1)
            for ck in range(NCK):
                b.inc(v.tensor_tensor_scan(
                    c.cum_fm[:, ck * 128:ck * 128 + 128],
                    c.a_sb[:, ck * 128:ck * 128 + 128],
                    c.a_sb[:, ck * 128:ck * 128 + 128],
                    0.0, ALU.add, ALU.bypass), hx + "cs", 1)
            _qwait(v, hx + "cs", NCK)
            b.inc(v.reduce_sum(c.tmp_sb[0:16, 0:1], c.a_sb[:, :], axis=AX.X),
                  hx + "ts", 1)
            _qwait(v, hx + "ts", 1)
            b.inc(v.tensor_tensor(c.tot_la[:, 0:1], c.tot_la[:, 0:1],
                                  c.tmp_sb[0:16, 0:1], ALU.add), hx + "tt", 1)

        blk.gpsimd(gp)
        blk.tensor(pe)
        blk.scalar(sc)
        blk.vector(ve)
    b.barrier()

    with nc.Block() as blk:
        def pe(t):
            _qwait(t, "ldi", None)
            for ck in range(NCK):
                if ck >= 1:
                    _qwait(t, hx + "cte", ck)
                mm = t.transpose(c.pX[0:128, 0:16],
                                 c.cum_fm[:, ck * 128:ck * 128 + 128],
                                 c.ident_f[0:16, 0:16])
                b.inc(mm, hx + "ct", 1)

        def ve(v):
            for ck in range(NCK):
                _qwait(v, hx + "ct", ck + 1)
                b.inc(v.tensor_copy(c.cum_tm[:, ck, :], c.pX[0:128, 0:16]),
                      hx + "cte", 1)

        blk.tensor(pe)
        blk.vector(ve)
    b.barrier()


def emit_kT_pre(b, c, half):
    nc = b.nc
    base = b.getcnt("ktp"), b.getcnt("kte")
    with nc.Block() as blk:
        def pe(t):
            b.wait(t, "ldb", None)
            for j in range(128):
                ck, h = divmod(j, 16)
                if j >= 1:
                    b.wait(t, "kte", base[1] + j)
                mm = t.matmul(c.pX[0:64, 0:128],
                              c.k_tm[:, ck, h * 64:h * 64 + 64],
                              c.ident_bf[:, :], start=True, stop=True)
                b.inc(mm, "ktp", 1)

        def ve(v):
            for j in range(128):
                ck, h = divmod(j, 16)
                b.wait(v, "ktp", base[0] + j + 1)
                hb = 64 * (h % 2)
                b.inc(v.tensor_copy(c.kTall[hb:hb + 64, ck, h // 2, :],
                                    c.pX[0:64, 0:128]), "kte", 1)

        blk.tensor(pe)
        blk.vector(ve)
    b.barrier()


def emit_ssd_scan(b, c, io, half, pass2):
    nc = b.nc
    if pass2:
        with nc.Block() as blk0:
            def gpm(e):
                i = e.dma_start(out=c.maskneg[:, :], in_=io["maskneg"][:, :])
                b.inc(i, "mkl", 16)
            blk0.gpsimd(gpm)

    nstep = NCK * 16 if _NCAP is None else _NCAP
    nround = nstep + 7
    pe_base = b.getcnt("rpe")
    dv_base = b.getcnt("rdv")
    ac_base = b.getcnt("rac")

    def dv_ops(r):
        n = 0
        if 0 <= r - 6 < nstep:
            n += 4 if pass2 else 1
        if 0 <= r - 4 < nstep:
            n += 4 if pass2 else 1
        if 0 <= r - 2 < nstep:
            n += 4 if pass2 else 1
        if 0 <= r < nstep:
            n += 1
        return max(n, 1)

    def ac_ops(r):
        n = 0
        if 0 <= r - 3 < nstep:
            n += 4 if pass2 else 2
        return max(n, 1)

    dv_cum = [0]
    ac_cum = [0]
    for r in range(nround):
        dv_cum.append(dv_cum[-1] + dv_ops(r))
        ac_cum.append(ac_cum[-1] + ac_ops(r))

    def M2(s):
        return c.pM2[:, (s % 3) * 512:(s % 3) * 512 + 128]

    def AT(s):
        return c.pAT[:, (s % 2) * 512:(s % 2) * 512 + 128]

    def NUM(s):
        return c.pNS[0:128, (s % 2) * 512:(s % 2) * 512 + 65]

    def PST(s):
        return c.pNS[0:64, (s % 2) * 512 + 128:(s % 2) * 512 + 193]

    def chh(s):
        return divmod(s, 16)

    with nc.Block() as blk:
        def pe(t):
            b.wait(t, "ldi", None)
            if pass2:
                b.wait(t, "ldb", None)
            for r in range(nround):
                b.wait(t, "rdv", dv_base + dv_cum[r])
                b.wait(t, "rac", ac_base + ac_cum[r])
                last = None
                s = r - 5
                if 0 <= s < nstep:
                    ck, h = chh(s)
                    if pass2 and "num" not in _X:
                        t.matmul(NUM(s), c.at_sb[:, s % 2, :],
                                 c.v_aug[:, ck, h, :],
                                 start=True, stop=False)
                        t.matmul(NUM(s), c.q1_sb[:, s % 2, :],
                                 c.stt_bf[:, s % 2, :],
                                 start=False, stop=True)
                    last = t.matmul(PST(s), c.k1_sb[:, s % 2, :],
                                    c.v_aug[:, ck, h, :],
                                    start=True, stop=True)
                s = r - 3
                if pass2 and 0 <= s < nstep and "atm" not in _X:
                    ck, h = chh(s)
                    hb = 64 * (h % 2)
                    last = t.matmul(AT(s), c.kTall[hb:hb + 64, ck, h // 2, :],
                                    c.q_fm[hb:hb + 64, h // 2,
                                           ck * 128:ck * 128 + 128],
                                    start=True, stop=True,
                                    tile_position=(hb, 0))
                s = r - 1
                if 0 <= s < nstep:
                    last = t.matmul(M2(s), c.b1_sb[:, s % 2, :],
                                    c.ident_f[:, :], start=True, stop=True)
                if last is None:
                    b.wait(t, "rpe", pe_base + r)
                    last = t.nop()
                b.inc(last, "rpe", 1)
            assert b.getcnt("rpe") - pe_base == nround

        def ve(v):
            if pass2:
                b.wait(v, "mkl", None)

            def I(op):
                b.inc(op, "rdv", 1)

            for r in range(nround):
                b.wait(v, "rpe", pe_base + r)
                b.wait(v, "rac", ac_base + ac_cum[r])
                did = False
                s = r - 6
                if 0 <= s < nstep:
                    did = True
                    ck, h = chh(s)
                    g = half * NCK + ck
                    pin, pout = (g + 1) % 2, g % 2
                    if pass2 and "out" not in _X:
                        b.wait(v, "rdv")
                        I(v.tensor_scalar_add(c.rinv[:, 0:1],
                                              NUM(s)[0:128, 64:65], SSD_EPS))
                        b.wait(v, "rdv")
                        I(v.reciprocal(c.rinv[:, 0:1], c.rinv[:, 0:1]))
                        b.wait(v, "rdv")
                        I(v.tensor_scalar_mul(
                            c.scanout[:, half * 8 + ck, h * 64:h * 64 + 64],
                            NUM(s)[0:128, 0:64], c.rinv[:, 0:1]))
                    elif pass2:
                        I(v.tensor_copy(c.rinv[:, 0:1], c.onescol_f[:, :]))
                        I(v.tensor_copy(c.rinv[:, 0:1], c.onescol_f[:, :]))
                        I(v.tensor_copy(c.scanout[:, half * 8 + ck,
                                                  h * 64:h * 64 + 1],
                                        c.onescol_f[:, :]))
                    I(v.scalar_tensor_tensor(
                        c.stt[pout][:, h, :], c.stt[pin][:, h, :],
                        c.lamC[:, s % 8:s % 8 + 1], PST(s),
                        ALU.mult, ALU.add))
                s = r - 4
                if 0 <= s < nstep:
                    did = True
                    ck, h = chh(s)
                    g = half * NCK + ck
                    pin = (g + 1) % 2
                    if pass2:
                        b.wait(v, "rdv")
                        if "att" not in _X:
                            I(v.tensor_tensor(c.at_sb[:, s % 2, :], AT(s),
                                              c.d_sb[:, s % 2, :], ALU.mult))
                        else:
                            I(v.tensor_copy(c.at_sb[:, s % 2, 0:1],
                                            c.onescol_f[:, :]))
                        I(v.tensor_tensor(c.q1_sb[:, s % 2, :],
                                          c.qc_sb[:, s % 4, :],
                                          c.lam_sb[:, s % 4, :], ALU.mult))
                        I(v.tensor_copy(c.stt_bf[:, s % 2, :],
                                        c.stt[pin][:, h, :]))
                    I(v.tensor_scalar_mul(
                        c.k1_sb[:, s % 2, :],
                        c.k_tm[:, ck, h * 64:h * 64 + 64],
                        c.e_part[:, s % 2:s % 2 + 1]))
                s = r - 2
                if 0 <= s < nstep:
                    did = True
                    ck, h = chh(s)
                    I(v.tensor_scalar(c.tmp_sb[:, s % 2:s % 2 + 1],
                                      M2(s)[0:128, 127:128],
                                      c.cum_tm[:, ck, h:h + 1], None,
                                      ALU.subtract))
                    if pass2:
                        I(v.tensor_scalar(c.dl2_sb[:, s % 2, :], M2(s),
                                          c.cum_tm[:, ck, h:h + 1], None,
                                          ALU.subtract))
                        b.wait(v, "rdv")
                        I(v.tensor_tensor(c.dl2_sb[:, s % 2, :],
                                          c.dl2_sb[:, s % 2, :],
                                          c.maskneg[:, :], ALU.min))
                        hb = 64 * (h % 2)
                        I(v.tensor_copy(
                            c.qc_sb[:, s % 4, :],
                            c.q_fm[hb:hb + 64, h // 2,
                                   ck * 128:ck * 128 + 128]))
                s = r
                if 0 <= s < nstep:
                    did = True
                    ck, h = chh(s)
                    I(v.tensor_copy(
                        c.b1_sb[:, s % 2, :],
                        c.cum_tm[:, ck, h:h + 1].broadcast_to([128, 128])))
                if not did:
                    b.wait(v, "rdv")
                    I(v.nop())
            assert b.getcnt("rdv") - dv_base == dv_cum[nround], \
                (b.getcnt("rdv") - dv_base, dv_cum[nround])

        def sc(s_):
            def I(op):
                b.inc(op, "rac", 1)

            for r in range(nround):
                b.wait(s_, "rpe", pe_base + r)
                b.wait(s_, "rdv", dv_base + dv_cum[r])
                did = False
                s = r - 3
                if 0 <= s < nstep:
                    did = True
                    if pass2:
                        I(s_.activation(c.d_sb[:, s % 2, :],
                                        c.dl2_sb[:, s % 2, :], AF.Exp))
                    I(s_.activation(c.e_part[:, s % 2:s % 2 + 1],
                                    c.tmp_sb[:, s % 2:s % 2 + 1], AF.Exp))
                    I(s_.activation(c.lamC[:, s % 8:s % 8 + 1],
                                    M2(s)[0:64, 127:128], AF.Exp))
                    if pass2:
                        I(s_.activation(c.lam_sb[:, s % 4, :],
                                        M2(s)[0:64, :], AF.Exp))
                if not did:
                    b.wait(s_, "rac")
                    I(s_.nop())
            assert b.getcnt("rac") - ac_base == ac_cum[nround], \
                (b.getcnt("rac") - ac_base, ac_cum[nround])

        blk.tensor(pe)
        blk.vector(ve)
        blk.scalar(sc)
    b.barrier()


def emit_scanout_tr(b, c):
    nc = b.nc
    with nc.Block() as blk:
        def pe(t):
            for j in range(128):
                tt, pc = divmod(j, 8)
                if j >= 1:
                    b.wait(t, "sote", j)
                mm = t.matmul(c.pX[0:128, 0:128],
                              c.scanout[:, tt, pc * 128:pc * 128 + 128],
                              c.ident_bf[:, :], start=True, stop=True)
                b.inc(mm, "sot", 1)

        def ve(v):
            for j in range(128):
                tt, pc = divmod(j, 8)
                b.wait(v, "sot", j + 1)
                b.inc(v.tensor_copy(
                    c.scanout_fm[:, pc, tt * 128:tt * 128 + 128],
                    c.pX[0:128, 0:128]), "sote", 1)

        blk.tensor(pe)
        blk.vector(ve)
    b.barrier()


def emit_phase_D(b, io):
    nc = b.nc
    stream = b.sb("streamD", OFF_STREAM, [128, NCH, T], F32)
    c = alloc_ssd(b, with_pass2=False)
    b.setup_consts(c.cstD, [PLN_EPS])
    with nc.Block() as blk:
        def sy(e):
            i = e.dma_start(out=stream[:, :, :],
                            in_=io["s2"].ap().rearrange("(c p) t -> p c t",
                                                        p=128))
            b.inc(i, "ldx", 16)
            i = e.dma_start(out=c.ident_f[:, :], in_=io["ident"][:, :])
            b.inc(i, "ldi", 16)

        def gp(e):
            e.memset(c.onescol_f[:, :], 1.0)
            e.memset(c.onesrow_f[:, :], 1.0)
            b.emit_const_memsets(e)
            e.memset(c.v_aug[:, :, :, :], 1.0)
            e.memset(c.tot_la[:, :], 0.0)
            e.memset(c.stt[0][:, :, :], 0.0)
            e.memset(c.stt[1][:, :, :], 0.0)
            i = e.dma_start(out=c.ba2[:, :], in_=io["ba_col"][:, :])
            b.inc(i, "ldb", 16)

        def ve(v):
            b.wait(v, "ldb", 16)
            b.inc(v.tensor_scalar_add(c.ba2[:, :], c.ba2[:, :], 2.0),
                  "ldba", 1)

        blk.sync(sy)
        blk.gpsimd(gp)
        blk.vector(ve)
    b.barrier()

    for half in range(NHALF):
        emit_ssd_qkva(b, c, io, stream, half, with_q=False)
        emit_ssd_scan(b, c, io, half, pass2=False)

    with nc.Block() as blk:
        def sc(s):
            b.inc(s.activation(c.tot_la[:, 0:1], c.tot_la[:, 0:1], AF.Exp),
                  "py0", 1)

        def pe(t):
            b.wait(t, "py0", 1)
            b.wait(t, "ldi", 16)
            b.inc(t.transpose(c.pM2[0:1, 0:16], c.tot_la[0:16, 0:1],
                              c.ident_f[0:16, 0:16]), "py1", 1)

        def ve(v):
            b.wait(v, "py1", 1)
            b.inc(v.tensor_copy(c.lrow[0:1, :], c.pM2[0:1, 0:16]), "py2", 1)

        blk.scalar(sc)
        blk.tensor(pe)
        blk.vector(ve)
    b.barrier()
    with nc.Block() as blk:
        def pe(t):
            b.inc(t.matmul(c.pM2[0:64, 512:528], c.onesrow_f[:1, 0:64],
                           c.lrow[0:1, :], start=True, stop=True), "py3", 1)

        def ve(v):
            b.wait(v, "py3", 1)
            b.inc(v.tensor_copy(c.lamrep[:, :], c.pM2[0:64, 512:528]),
                  "py4", 1)

        blk.tensor(pe)
        blk.vector(ve)
    b.barrier()
    with nc.Block() as blk:
        def sy(e):
            i = e.dma_start(out=io["summ"].ap(),
                            in_=c.stt[1][:, :, :].rearrange(
                                "p h n -> p (h n)"))
            b.inc(i, "stD", 16)
            i = e.dma_start(out=io["lamr"][:, :], in_=c.lamrep[:, :])
            b.inc(i, "stD", 16)
            b.wait(e, "stD", 32)

        blk.sync(sy)


def emit_phase_E(b, io):
    nc = b.nc
    stream = b.sb("streamE", OFF_STREAM, [128, NCH, T], F32)
    c = alloc_ssd(b, with_pass2=True)
    b.setup_consts(c.cstD, [PLN_EPS])
    with nc.Block() as blk:
        def sy(e):
            i = e.dma_start(out=stream[:, :, :],
                            in_=io["s2"].ap().rearrange("(c p) t -> p c t",
                                                        p=128))
            b.inc(i, "ldx", 16)
            i = e.dma_start(out=c.ident_f[:, :], in_=io["ident"][:, :])
            b.inc(i, "ldi", 16)
            i = e.dma_start(out=c.meta_sb[:, :], in_=io["meta"][:, :])
            b.inc(i, "ldi", 16)

        def gp(e):
            e.memset(c.onescol_f[:, :], 1.0)
            e.memset(c.onesrow_f[:, :], 1.0)
            b.emit_const_memsets(e)
            e.memset(c.v_aug[:, :, :, :], 1.0)
            e.memset(c.tot_la[:, :], 0.0)
            e.memset(c.stt[1][:, :, :], 0.0)
            e.memset(c.sacc[:, :, :], 0.0)
            i = e.dma_start(out=c.ba2[:, :], in_=io["ba_col"][:, :])
            b.inc(i, "ldb", 16)
            i = e.dma_start(out=c.ident_bf[:, :], in_=io["ident"][:, :])
            b.inc(i, "ldb", 16)

        def ve(v):
            b.wait(v, "ldb", 32)
            b.inc(v.tensor_scalar_add(c.ba2[:, :], c.ba2[:, :], 2.0),
                  "ldba", 1)

        blk.sync(sy)
        blk.gpsimd(gp)
        blk.vector(ve)
    b.barrier()

    if True:
      with nc.Block() as blk:
        def gp(e):
            for p_ in range(4):
                b.wait(e, "pcv", p_ * 17)
                i = e.dma_start(out=c.sldp[:, :, :].rearrange(
                                    "p h n -> p (h n)"),
                                in_=io["summ_all"].ap()[p_ * 64:p_ * 64 + 64,
                                                        :])
                b.inc(i, "pcl", 16)
                i = e.dma_start(out=c.lamp[:, :],
                                in_=io["lam_all"].ap()[p_ * 64:p_ * 64 + 64,
                                                       :])
                b.inc(i, "pcl", 16)

        def ve(v):
            b.wait(v, "ldi", 32)
            for p_ in range(4):
                b.wait(v, "pcl", (p_ + 1) * 32)
                sel_src = c.stt[1] if p_ % 2 == 0 else c.sel2
                sel_dst = c.sel2 if p_ % 2 == 0 else c.stt[1]
                acc_src = c.sacc if p_ % 2 == 0 else c.sacc2
                acc_dst = c.sacc2 if p_ % 2 == 0 else c.sacc
                b.inc(v.scalar_tensor_tensor(
                    sel_dst[:, :, :].rearrange("p h n -> p (h n)"),
                    acc_src[:, :, :].rearrange("p h n -> p (h n)"),
                    c.meta_sb[:, p_:p_ + 1],
                    sel_src[:, :, :].rearrange("p h n -> p (h n)"),
                    ALU.mult, ALU.add), "pcv", 1)
                b.wait(v, "pcv")
                for h in range(16):
                    b.inc(v.scalar_tensor_tensor(
                        acc_dst[:, h, :], acc_src[:, h, :],
                        c.lamp[:, h:h + 1], c.sldp[:, h, :],
                        ALU.mult, ALU.add), "pcv", 1)
                    b.wait(v, "pcv")

        blk.gpsimd(gp)
        blk.vector(ve)
      b.barrier()

    for half in range(NHALF):
        emit_ssd_qkva(b, c, io, stream, half, with_q=True)
        emit_kT_pre(b, c, half)
        emit_ssd_scan(b, c, io, half, pass2=True)
    if "tails" not in _X:
        emit_scanout_tr(b, c)

    if "tails" not in _X:
      with nc.Block() as blk:
        def gp(e):
            i = e.dma_start(out=c.slab[:, :, :],
                            in_=io["wo"].ap().rearrange("(c p) n -> p c n",
                                                        p=128))
            b.inc(i, "wol", 16)
        blk.gpsimd(gp)

        tiles = []
        for ot in range(NCH):
            for tt in range(4):
                kl = [(c.slab[:, pc, ot * 128:ot * 128 + 128],
                       c.scanout_fm[:, pc, tt * 512:tt * 512 + 512])
                      for pc in range(NCH)]
                tiles.append(((ot, tt), kl))

        def wevac(v, info, ps):
            ot, tt = info
            sl = slice(tt * 512, tt * 512 + 512)
            return v.tensor_tensor(stream[:, ot, sl], stream[:, ot, sl],
                                   ps, ALU.add)

        def pre_pe(t):
            b.wait(t, "wol", 16)

        def pre_ve(v):
            b.wait(v, "ldx", 16)
        mm_chain(b, blk, "ew", tiles, c.pM2, wevac, pre_pe=pre_pe,
                 pre_ve=pre_ve)
      b.barrier()

    if "tails" not in _X:
        sq = b.sb("sqE", P0 + 8192 + 32768, [128, 1024], F32)
        stt1 = b.sb("stt1E_", P0 + 8192, [1, T], F32)
        stt2 = b.sb("stt2E_", P0 + 16384, [1, T], F32)
        row2k = b.sb("row2kE", P0 + 24576, [1, T], F32)
        emit_postln(b, stream, sq, row2k, stt1, stt2, c.onescol_f,
                    c.onesrow_f, c.pNS, c.pAT[:, 0:512], c.pM2[:, 0:512],
                    "pln3")
    with nc.Block() as blk:
        def sy(e):
            i = e.dma_start(out=io["s3"].ap().rearrange("(c p) t -> p c t",
                                                        p=128),
                            in_=stream[:, :, :])
            b.inc(i, "stE", 16)
            b.wait(e, "stE", 16)

        blk.sync(sy)


def _mk_maskneg():
    m = np.zeros((128, 128), np.float32)
    for j in range(128):
        m[j, :j] = -1e30
    return m


def build_PD():
    b = B()
    io = {}
    io["s2"] = b.din("s2", [D, T])
    io["wk"] = b.din("wk", [D, D])
    io["wv"] = b.din("wv", [D, D])
    io["wa"] = b.din("wa", [D, H])
    io["ba_col"] = b.din("ba_col", [H, 1])
    io["ident"] = b.din("ident", [128, 128])
    io["summ"] = b.dout("summ", [64, 16 * 65])
    io["lamr"] = b.dout("lamr", [64, 16])
    emit_phase_D(b, io)
    b.nc.finalize()
    return b


def build_PE():
    b = B()
    io = {}
    io["s2"] = b.din("s2", [D, T])
    io["wq"] = b.din("wq", [D, D])
    io["wk"] = b.din("wk", [D, D])
    io["wv"] = b.din("wv", [D, D])
    io["wa"] = b.din("wa", [D, H])
    io["ba_col"] = b.din("ba_col", [H, 1])
    io["wo"] = b.din("wo", [D, D])
    io["ident"] = b.din("ident", [128, 128])
    io["maskneg"] = b.din("maskneg", [128, 128])
    io["meta"] = b.din("meta", [64, 4])
    io["summ_all"] = b.din("summ_all", [256, 16 * 65])
    io["lam_all"] = b.din("lam_all", [256, 16])
    io["s3"] = b.dout("s3", [D, T])
    emit_phase_E(b, io)
    b.nc.finalize()
    return b



def emit_phase_F(b, io):
    nc = b.nc
    stream = b.sb("streamF", OFF_STREAM, [128, NCH, T], F32)
    so = OFF_SMALL
    onescol_f = b.sb("onescol_fF", so, [128, 1], F32); so += 32
    onesrow_f = b.sb("onesrow_fF", so, [1, 128], F32); so += 512
    ident_f = b.sb("ident_fF", so, [128, 128], F32); so += 512
    wr_sb = b.sb("wr_sbF", so, [128, NCH, 8], F32); so += 256
    wg_sb = b.sb("wg_sbF", so, [128, NCH, 8], F32); so += 256
    negm = b.sb("negmF", so, [128, 1], F32); so += 32
    den_c = b.sb("den_cF", so, [128, 1], F32); so += 32
    rp_sb = b.sb("rp_sbF", so, [128, 8], F32); so += 32
    gw_sb = b.sb("gw_sbF", so, [128, 8], F32); so += 32
    cw_sb = b.sb("cw_sbF", so, [128, 8], F32); so += 32
    cwt = b.sb("cwtF", so, [128, 8], F32); so += 32
    cstF = b.sb("cstF", so, [128, 8], F32); so += 32
    assert so <= P0, so

    p = P0
    ybf = b.sb("ybfF", p, [128, NCH, T], BF16); p += 32768
    hsl = b.sb("hslF", p, [128, NCH, T], BF16); p += 32768
    cw_tm = b.sb("cw_tmF", p, [128, 16, 8], F32); p += 512
    cwrow = b.sb("cwrowF", p, [1, T], F32); p += 8192
    cwb = b.sb("cwbF", p, [128, T], F32); p += 8192
    gs_sb = b.sb("gs_sbF", p, [128, 2, 512], BF16); p += 2048
    uc_sb = b.sb("uc_sbF", p, [128, 2, 512], BF16); p += 2048
    wgate = b.sb("wgateF", p, [128, 2, NCH, 512], BF16)
    stt1 = b.sb("stt1F", p, [1, T], F32)
    stt2 = b.sb("stt2F", p + 8192, [1, T], F32)
    p += 16384
    wup = b.sb("wupF", p, [128, 2, NCH, 512], BF16)
    row2k = b.sb("row2kF", p, [1, T], F32)
    sq = b.sb("sqF", p + 8192, [128, 1024], F32)
    p += 16384
    wdown = b.sb("wdownF", p, [128, 2, 4, D], BF16); p += 16384
    assert p <= SB_TOP, p

    pA = b.psum("pAF")
    pB = b.psum("pBF")
    pC = b.psum("pCF")
    pD = b.psum("pDF", (128, 512))
    b.setup_consts(cstF, [PLN_EPS])

    with nc.Block() as blk:
        def sy(e):
            i = e.dma_start(out=stream[:, :, :],
                            in_=io["s3"].ap().rearrange("(c p) t -> p c t",
                                                        p=128))
            b.inc(i, "ldx", 16)
            i = e.dma_start(out=ident_f[:, :], in_=io["ident"][:, :])
            b.inc(i, "ldk", 16)

        def gp(e):
            e.memset(onescol_f[:, :], 1.0)
            e.memset(onesrow_f[:, :], 1.0)
            b.emit_const_memsets(e)
            i = e.dma_start(out=wr_sb[:, :, :],
                            in_=io["w_router"].ap().rearrange(
                                "(c p) n -> p c n", p=128))
            b.inc(i, "ldr", 16)
            i = e.dma_start(out=wg_sb[:, :, :],
                            in_=io["w_gating"].ap().rearrange(
                                "(c p) n -> p c n", p=128))
            b.inc(i, "ldr", 16)

        def ve(v):
            b.wait(v, "ldx", 16)
            for i in range(32):
                tt, cc = divmod(i, NCH)
                b.inc(v.tensor_copy(ybf[:, cc, tt * 512:tt * 512 + 512],
                                    stream[:, cc, tt * 512:tt * 512 + 512]),
                      "yc", 1)

        blk.sync(sy)
        blk.gpsimd(gp)
        blk.vector(ve)
    b.barrier()

    with nc.Block() as blk:
        def pe(t):
            b.wait(t, "ldr", 32)
            b.wait(t, "ldk", 16)
            for i in range(16):
                bank = i % 2
                for cc in range(NCH):
                    t.matmul(pA[0:128, bank * 512:bank * 512 + 8],
                             stream[:, cc, i * 128:i * 128 + 128],
                             wr_sb[:, cc, :],
                             start=(cc == 0), stop=(cc == NCH - 1))
                for cc in range(NCH):
                    mm = t.matmul(pA[0:128, bank * 512 + 8:bank * 512 + 16],
                                  stream[:, cc, i * 128:i * 128 + 128],
                                  wg_sb[:, cc, :],
                                  start=(cc == 0), stop=(cc == NCH - 1))
                b.inc(mm, "f0m", 1)
                b.wait(t, "f0c", i + 1)

        def ve(v):
            def chain(op):
                b.inc(op, "f0v", 1)
                b.wait(v, "f0v")
            for i in range(16):
                bank = i % 2
                b.wait(v, "f0m", i + 1)
                chain(v.reduce_max(negm[:, 0:1],
                                   pA[0:128, bank * 512:bank * 512 + 8],
                                   axis=AX.X, negate=True))
                b.inc(v.tensor_scalar(rp_sb[:, :],
                                      pA[0:128, bank * 512:bank * 512 + 8],
                                      negm[:, 0:1], None, ALU.add), "f0k", 1)
                b.wait(v, "f0x", 2 * i + 1)
                chain(v.reduce_sum(den_c[:, 0:1], rp_sb[:, :], axis=AX.X))
                chain(v.reciprocal(den_c[:, 0:1], den_c[:, 0:1]))
                chain(v.tensor_scalar_mul(rp_sb[:, :], rp_sb[:, :],
                                          den_c[:, 0:1]))
                chain(v.reduce_max(negm[:, 0:1],
                                   pA[0:128,
                                      bank * 512 + 8:bank * 512 + 16],
                                   axis=AX.X, negate=True))
                b.inc(v.tensor_scalar(gw_sb[:, :],
                                      pA[0:128,
                                         bank * 512 + 8:bank * 512 + 16],
                                      negm[:, 0:1], None, ALU.add), "f0k", 1)
                b.wait(v, "f0x", 2 * i + 2)
                chain(v.reduce_sum(den_c[:, 0:1], gw_sb[:, :], axis=AX.X))
                chain(v.reciprocal(den_c[:, 0:1], den_c[:, 0:1]))
                chain(v.tensor_scalar_mul(gw_sb[:, :], gw_sb[:, :],
                                          den_c[:, 0:1]))
                chain(v.tensor_tensor(cw_sb[:, :], rp_sb[:, :], gw_sb[:, :],
                                      ALU.mult))
                chain(v.reduce_sum(den_c[:, 0:1], cw_sb[:, :], axis=AX.X))
                chain(v.tensor_scalar_add(den_c[:, 0:1], den_c[:, 0:1],
                                          1e-9))
                chain(v.reciprocal(den_c[:, 0:1], den_c[:, 0:1]))
                b.inc(v.tensor_scalar_mul(cw_tm[:, i, :], cw_sb[:, :],
                                          den_c[:, 0:1]), "f0c", 1)

        def sc(s):
            for i in range(16):
                b.wait(s, "f0k", 2 * i + 1)
                b.inc(s.activation(rp_sb[:, :], rp_sb[:, :], AF.Exp),
                      "f0x", 1)
                b.wait(s, "f0k", 2 * i + 2)
                b.inc(s.activation(gw_sb[:, :], gw_sb[:, :], AF.Exp,
                                   scale=10.0), "f0x", 1)

        blk.tensor(pe)
        blk.vector(ve)
        blk.scalar(sc)
    b.barrier()

    NSLAB = 4
    with nc.Block() as blk:
        def gp(e_):
            for e in range(8):
                for fs in range(NSLAB):
                    for half in range(2):
                        idx = (e * NSLAB + fs) * 2 + half
                        buf = idx % 2
                        if idx >= 2:
                            b.wait(e_, "fhm", (idx - 1) * 32)
                        f0 = fs * 1024 + half * 512
                        i = e_.dma_start(
                            out=wgate[:, buf, :, :],
                            in_=io["w_gate"].ap()[e, :, f0:f0 + 512]
                            .rearrange("(c p) n -> p c n", p=128))
                        b.inc(i, "fwl%d" % buf, 16)
                        i = e_.dma_start(
                            out=wup[:, buf, :, :],
                            in_=io["w_up"].ap()[e, :, f0:f0 + 512]
                            .rearrange("(c p) n -> p c n", p=128))
                        b.inc(i, "fwl%d" % buf, 16)
                        if e * NSLAB + fs >= 1:
                            b.wait(e_, "fde", (e * NSLAB + fs) * 32)
                        i = e_.dma_start(
                            out=wdown[:, half, :, :],
                            in_=io["w_down"].ap()[e, f0:f0 + 512, :]
                            .rearrange("(c p) n -> p c n", p=128))
                        b.inc(i, "fwd%d" % half, 16)

        def pe(t):
            b.wait(t, "ldk", 16)
            for e in range(8):
                for i in range(16):
                    j = e * 16 + i
                    if j >= 1:
                        b.wait(t, "cre", j)
                    mm = t.transpose(pD[0:1, (j % 2) * 128:(j % 2) * 128 + 128],
                                     cw_tm[:, i, e:e + 1], ident_f[:, :])
                    b.inc(mm, "crt", 1)
                b.wait(t, "cre", (e + 1) * 16)
                b.wait(t, "fde", e * NSLAB * 32)
                for tt in range(4):
                    j = e * 4 + tt
                    if j >= 2:
                        b.wait(t, "cbv", j - 1)
                    mm = t.matmul(pC[:, (tt % 2) * 512:(tt % 2) * 512 + 512],
                                  onesrow_f[:1, :],
                                  cwrow[0:1, tt * 512:tt * 512 + 512],
                                  start=True, stop=True)
                    b.inc(mm, "cbm", 1)
                for fs in range(NSLAB):
                    for half in range(2):
                        idx = (e * NSLAB + fs) * 2 + half
                        buf = idx % 2
                        b.wait(t, "fwl%d" % buf, (idx // 2 + 1) * 32)
                        for ft in range(4):
                            for tt in range(4):
                                for gu in range(2):
                                    j = idx * 32 + (ft * 4 + tt) * 2 + gu
                                    j2 = j // 2
                                    wt = wgate if gu == 0 else wup
                                    ps = pA if j2 % 2 == 0 else pB
                                    bank = gu
                                    if gu == 0 and j2 >= 2:
                                        b.wait(t, "fhe", j2 - 1)
                                    for cc in range(NCH):
                                        mm = t.matmul(
                                            ps[:, bank * 512:bank * 512 + 512],
                                            wt[:, buf, cc,
                                               ft * 128:ft * 128 + 128],
                                            ybf[:, cc,
                                                tt * 512:tt * 512 + 512],
                                            start=(cc == 0),
                                            stop=(cc == NCH - 1))
                                    b.inc(mm, "fhm", 1)
                    sidx = e * NSLAB + fs
                    b.wait(t, "fhe", (sidx + 1) * 32)
                    b.wait(t, "fwd0", (sidx + 1) * 16)
                    b.wait(t, "fwd1", (sidx + 1) * 16)
                    if fs == 0:
                        b.wait(t, "cbv", (e + 1) * 4)
                    for ot in range(NCH):
                        for tt in range(4):
                            j = sidx * 32 + ot * 4 + tt
                            bank = j % 2
                            if j >= 2:
                                b.wait(t, "fde", j - 1)
                            for cc in range(NCH):
                                mm = t.matmul(
                                    pC[:, bank * 512:bank * 512 + 512],
                                    wdown[:, cc // 4, cc % 4,
                                          ot * 128:ot * 128 + 128],
                                    hsl[:, cc, tt * 512:tt * 512 + 512],
                                    start=(cc == 0), stop=(cc == NCH - 1))
                            b.inc(mm, "fdm", 1)

        def ve(v):
            b.wait(v, "ldx", 16)
            for e in range(8):
                for i in range(16):
                    j = e * 16 + i
                    b.wait(v, "crt", j + 1)
                    b.inc(v.tensor_copy(cwrow[0:1, i * 128:i * 128 + 128],
                                        pD[0:1, (j % 2) * 128:
                                           (j % 2) * 128 + 128]), "cre", 1)
                for tt in range(4):
                    j = e * 4 + tt
                    b.wait(v, "cbm", j + 1)
                    b.inc(v.tensor_copy(cwb[:, tt * 512:tt * 512 + 512],
                                        pC[:, (tt % 2) * 512:
                                           (tt % 2) * 512 + 512]), "cbv", 1)
                for fs in range(NSLAB):
                    for half in range(2):
                        idx = (e * NSLAB + fs) * 2 + half
                        for ft in range(4):
                            for tt in range(4):
                                j2 = idx * 16 + ft * 4 + tt
                                jg = idx * 32 + (ft * 4 + tt) * 2
                                ps = pA if (jg // 2) % 2 == 0 else pB
                                par = j2 % 2
                                fc = half * 4 + ft
                                b.wait(v, "fhm", jg + 2)
                                b.wait(v, "fhx", j2 + 1)
                                b.inc(v.tensor_tensor(
                                    uc_sb[:, par, :],
                                    ps[:, 512:1024],
                                    cwb[:, tt * 512:tt * 512 + 512],
                                    ALU.mult), "fhv", 1)
                                b.wait(v, "fhv")
                                b.inc(v.tensor_tensor(
                                    hsl[:, fc, tt * 512:tt * 512 + 512],
                                    gs_sb[:, par, :], uc_sb[:, par, :],
                                    ALU.mult), "fhe", 1)
                    sidx = e * NSLAB + fs
                    for ot in range(NCH):
                        for tt in range(4):
                            j = sidx * 32 + ot * 4 + tt
                            bank = j % 2
                            b.wait(v, "fdm", j + 1)
                            sl = slice(tt * 512, tt * 512 + 512)
                            b.inc(v.tensor_tensor(
                                stream[:, ot, sl], stream[:, ot, sl],
                                pC[:, bank * 512:bank * 512 + 512],
                                ALU.add), "fde", 1)

        def sc(s):
            for e in range(8):
                for fs in range(NSLAB):
                    for half in range(2):
                        idx = (e * NSLAB + fs) * 2 + half
                        for ft in range(4):
                            for tt in range(4):
                                j2 = idx * 16 + ft * 4 + tt
                                jg = idx * 32 + (ft * 4 + tt) * 2
                                ps = pA if (jg // 2) % 2 == 0 else pB
                                par = j2 % 2
                                b.wait(s, "fhm", jg + 1)
                                if j2 >= 2:
                                    b.wait(s, "fhe", j2 - 1)
                                b.inc(s.activation(gs_sb[:, par, :],
                                                   ps[:, 0:512], AF.Silu),
                                      "fhx", 1)

        blk.gpsimd(gp)
        blk.tensor(pe)
        blk.vector(ve)
        blk.scalar(sc)
    b.barrier()

    emit_postln(b, stream, sq, row2k, stt1, stt2, onescol_f, onesrow_f,
                pA, pB[:, 0:512], pB[:, 512:1024], "pln4")
    with nc.Block() as blk:
        def sy(e):
            i = e.dma_start(out=io["out_fm"].ap().rearrange(
                                "(c p) t -> p c t", p=128),
                            in_=stream[:, :, :])
            b.inc(i, "stF", 16)
            b.wait(e, "stF", 16)

        blk.sync(sy)


def build_PF():
    b = B()
    io = {}
    io["s3"] = b.din("s3", [D, T])
    io["w_router"] = b.din("w_router", [D, 8])
    io["w_gating"] = b.din("w_gating", [D, 8])
    io["w_gate"] = b.din("w_gate", [8, D, 4096])
    io["w_up"] = b.din("w_up", [8, D, 4096])
    io["w_down"] = b.din("w_down", [8, 4096, D])
    io["ident"] = b.din("ident", [128, 128])
    io["out_fm"] = b.dout("out_fm", [D, T])
    emit_phase_F(b, io)
    b.nc.finalize()
    return b


def build_BC():
    b = B()
    io = {}
    io["x_fm"] = b.din("x_fm", [D, T])
    io["qf"] = b.din("qf", [D, T], BF16)
    io["k_b"] = b.din("k_b", [D, S], BF16)
    io["v_b"] = b.din("v_b", [S, D], BF16)
    io["kqn_all"] = b.din("kqn_all", [4, NCH])
    io["wout_t"] = b.din("wout_t", [D, D])
    io["ident"] = b.din("ident", [128, 128])
    io["hsum"] = b.din("hsum", [128, 2])
    io["s1"] = b.dint("s1", [D, T])
    for n, sh in [("W_t", [D, D]), ("wp_t", [D, 256]), ("bp_col", [128, 2]),
                  ("bp_row", [1, 256]), ("sim_fm", [128, NCH]),
                  ("temp", [1, 1]), ("wd_t", [D, 15]), ("wu_t", [15, D]),
                  ("selmap", [4, 15])]:
        io[n] = b.din(n, sh)
    io["s2"] = b.dout("s2", [D, T])
    b.ns = "B_"
    emit_phase_B(b, io)
    b.next_phase("C_")
    emit_phase_C(b, io)
    b.nc.finalize()
    return b


def build_EF():
    b = B()
    io = {}
    io["s2"] = b.din("s2", [D, T])
    for n, sh in [("wq", [D, D]), ("wk", [D, D]), ("wv", [D, D]),
                  ("wa", [D, H]), ("ba_col", [H, 1]), ("wo", [D, D]),
                  ("ident", [128, 128]), ("maskneg", [128, 128]),
                  ("meta", [64, 4]), ("summ_all", [256, 16 * 65]),
                  ("lam_all", [256, 16])]:
        io[n] = b.din(n, sh)
    io["s3"] = b.dint("s3", [D, T])
    for n, sh in [("w_router", [D, 8]), ("w_gating", [D, 8]),
                  ("w_gate", [8, D, 4096]), ("w_up", [8, D, 4096]),
                  ("w_down", [8, 4096, D])]:
        io[n] = b.din(n, sh)
    io["out_fm"] = b.dout("out_fm", [D, T])
    b.ns = "E_"
    emit_phase_E(b, io)
    b.next_phase("F_")
    emit_phase_F(b, io)
    b.nc.finalize()
    return b


def _mk_selmap():
    sel = np.zeros((4, 15), np.float32)
    r0 = 0
    for e, r in enumerate([1, 2, 4, 8]):
        sel[e, r0:r0 + r] = 1.0
        r0 += r
    return sel


def kernel(**inputs):
    g, pre = _prep_shared(inputs)
    consts = [_host_consts(core) for core in range(8)]
    x = g["L0_attn_x"].astype(np.float32)
    maskneg = _mk_maskneg()
    selmap = _mk_selmap()
    ba_col = _c(g["L1_attn_ba"].astype(np.float32).reshape(H, 1))

    def shard(arr):
        out = []
        for core in range(NCORES):
            bcore, pos0 = core // 4, (core % 4) * T
            out.append(_c(arr[bcore, pos0:pos0 + T, :].T))
        return out

    x_fm = shard(x)

    bA = build_PA()
    mapsA = []
    for core in range(NCORES):
        m = {"x_fm": x_fm[core]}
        for n in ["wdq_t", "wdkv_t", "upq_t", "upk_t", "upv_t",
                  "l0_gamma", "l0_beta"]:
            m[n] = pre[n]
        for n in ["cos2", "sin2", "p2t", "ident", "hsum"]:
            m[n] = consts[core][n]
        mapsA.append(m)
    resA = run_bass_kernel_spmd(bA.nc, mapsA, core_ids=list(range(NCORES)))
    rA = resA.results

    bBC = build_BC()
    mapsBC = []
    for core in range(NCORES):
        bcore = core // 4
        grp = range(4 * bcore, 4 * bcore + 4)
        mapsBC.append(dict(
            x_fm=x_fm[core], qf=_c(rA[core]["qf"]),
            k_b=_c(np.concatenate([rA[i]["kf"] for i in grp], axis=1)),
            v_b=_c(np.concatenate([rA[i]["vt"] for i in grp], axis=0)),
            kqn_all=_c(np.concatenate([rA[i]["kqn"] for i in grp], 0)),
            wout_t=pre["wout_t"], ident=consts[core]["ident"],
            hsum=consts[core]["hsum"],
            W_t=pre["W_t"], wp_t=pre["wp_t"], bp_col=pre["bp_col"],
            bp_row=pre["bp_row"], sim_fm=pre["sim_fm"], temp=pre["temp"],
            wd_t=pre["wd_t"], wu_t=pre["wu_t"], selmap=selmap))
    resBC = run_bass_kernel_spmd(bBC.nc, mapsBC, core_ids=list(range(NCORES)))
    s2 = [_c(resBC.results[core]["s2"]) for core in range(NCORES)]

    bD = build_PD()
    mapsD = []
    for core in range(NCORES):
        mapsD.append(dict(
            s2=s2[core], wk=pre["wk"], wv=pre["wv"], wa=pre["wa"],
            ba_col=ba_col, ident=consts[core]["ident"]))
    resD = run_bass_kernel_spmd(bD.nc, mapsD, core_ids=list(range(NCORES)))

    bEF = build_EF()
    mapsEF = []
    for core in range(NCORES):
        grp = range(4 * (core // 4), 4 * (core // 4) + 4)
        mapsEF.append(dict(
            s2=s2[core], wq=pre["wq"], wk=pre["wk"], wv=pre["wv"],
            wa=pre["wa"], ba_col=ba_col, wo=pre["wo"],
            ident=consts[core]["ident"], maskneg=maskneg,
            meta=consts[core]["meta"],
            summ_all=_c(np.concatenate(
                [resD.results[i]["summ"] for i in grp], 0)),
            lam_all=_c(np.concatenate(
                [resD.results[i]["lamr"] for i in grp], 0)),
            w_router=pre["w_router"], w_gating=pre["w_gating"],
            w_gate=pre["w_gate"], w_up=pre["w_up"], w_down=pre["w_down"]))
    resEF = run_bass_kernel_spmd(bEF.nc, mapsEF, core_ids=list(range(NCORES)))

    out = np.empty((2, S, D), np.float32)
    for core in range(NCORES):
        bcore, pos0 = core // 4, (core % 4) * T
        out[bcore, pos0:pos0 + T, :] = resEF.results[core]["out_fm"].T
    return out


# version 05 (final)
# speedup vs baseline: 1.0098x; 1.0098x over previous
import numpy as np
import ml_dtypes
import concourse.bass as bass
import concourse.mybir as mybir
from concourse.bass_utils import run_bass_kernel_spmd

F32 = mybir.dt.float32
BF16 = mybir.dt.bfloat16
AF = mybir.ActivationFunctionType
ALU = mybir.AluOpType
AX = mybir.AxisListType

NCORES = 8
T = 2048
D = 1024
NCH = 8
H, DH = 16, 64
S = 8192
LN0_EPS = 1e-5
PLN_EPS = 1e-5
SSD_EPS = 1e-6

SB_BASE = 16512
SB_TOP = 229344
OFF_STREAM = SB_BASE
OFF_SMALL = SB_BASE + 65536
P0 = SB_BASE + 69632


def _c(x):
    return np.ascontiguousarray(x)



def _rope_tables():
    inv_freq = 1.0 / (10000.0 ** (np.arange(0, DH, 2, dtype=np.float32) / DH))
    t = np.arange(S, dtype=np.float32)
    freqs = np.outer(t, inv_freq)
    cos = np.concatenate([np.cos(freqs), np.cos(freqs)], -1)[:, ::2]
    sin = np.concatenate([np.sin(freqs), np.sin(freqs)], -1)[:, ::2]
    return cos, sin


def _host_consts(core):
    pos0 = (core % 4) * T
    cos, sin = _rope_tables()
    cos_e = np.repeat(cos[pos0:pos0 + T].T, 2, axis=0)
    sin_e = np.repeat(sin[pos0:pos0 + T].T, 2, axis=0)
    cos2 = _c(np.concatenate([cos_e, cos_e], 0).astype(np.float32))
    sin2 = _c(np.concatenate([sin_e, sin_e], 0).astype(np.float32))
    P = np.zeros((64, 64), np.float32)
    for i in range(32):
        P[2 * i, 2 * i + 1] = -1.0
        P[2 * i + 1, 2 * i] = 1.0
    P2 = np.zeros((128, 128), np.float32)
    P2[:64, :64] = P
    P2[64:, 64:] = P
    meta = np.zeros((64, 4), np.float32)
    meta[:, core % 4] = 1.0
    hsum = np.zeros((128, 2), np.float32)
    hsum[:64, 0] = 1.0
    hsum[64:, 1] = 1.0
    return dict(cos2=cos2, sin2=sin2, p2t=_c(P2.T), hsum=hsum,
                ident=np.eye(128, dtype=np.float32), meta=meta)


def _prep_shared(inputs):
    g = {k: np.asarray(v) for k, v in inputs.items()}
    pre = {}
    pre["wdq_t"] = _c(g["L0_attn_w_down_q"].T.astype(np.float32))
    pre["wdkv_t"] = _c(g["L0_attn_w_down_kv"].T.astype(np.float32))
    pre["upq_t"] = _c(g["L0_attn_w_up_q"].T.astype(np.float32))
    pre["upk_t"] = _c(g["L0_attn_w_up_k"].T.astype(np.float32))
    pre["upv_t"] = _c(g["L0_attn_w_up_v"].T.astype(np.float32))
    pre["wout_t"] = _c(g["L0_attn_w_out"].T.astype(np.float32))
    gam = g["L0_attn_ln_scale"].astype(np.float32)
    bet = g["L0_attn_ln_bias"].astype(np.float32)
    pre["l0_gamma"] = _c(gam.reshape(NCH, 128).T)
    pre["l0_beta"] = _c(bet.reshape(NCH, 128).T)
    pre["W_t"] = _c(g["L0_ffn_W"].T.astype(np.float32))
    pre["wp_t"] = _c(g["L0_ffn_Wp"].T.astype(np.float32))
    bp = g["L0_ffn_bp"].astype(np.float32)
    pre["bp_col"] = _c(bp.reshape(2, 128).T)
    pre["bp_row"] = _c(bp.reshape(1, 256))
    pre["sim_fm"] = _c(g["L0_ffn_sim"].astype(np.float32).reshape(2, 128, 4)
                       .transpose(1, 0, 2).reshape(128, 8))
    pre["temp"] = _c(g["L0_ffn_temp"].astype(np.float32).reshape(1, 1))
    pre["wd_t"] = _c(np.concatenate(
        [g[f"L0_ffn_Wd{i}"].T.astype(np.float32) for i in range(4)], axis=1))
    pre["wu_t"] = _c(np.concatenate(
        [g[f"L0_ffn_Wu{i}"].T.astype(np.float32) for i in range(4)], axis=0))
    pre["wq"] = _c(g["L1_attn_Wq"].astype(np.float32))
    pre["wk"] = _c(g["L1_attn_Wk"].astype(np.float32))
    pre["wv"] = _c(g["L1_attn_Wv"].astype(np.float32))
    pre["wo"] = _c(g["L1_attn_Wo"].astype(np.float32))
    pre["wa"] = _c(g["L1_attn_Wa"].astype(np.float32))
    pre["ba_row"] = _c(g["L1_attn_ba"].astype(np.float32).reshape(1, H))
    pre["w_router"] = _c(g["L1_ffn_w_router"].astype(np.float32))
    pre["w_gating"] = _c(g["L1_ffn_w_gating"].astype(np.float32))
    pre["w_gate"] = _c(g["L1_ffn_w_gate"].astype(np.float32))
    pre["w_up"] = _c(g["L1_ffn_w_up"].astype(np.float32))
    pre["w_down"] = _c(g["L1_ffn_w_down"].astype(np.float32))
    return g, pre



class B:
    def __init__(self):
        self.nc = bass.Bass("TRN2", target_bir_lowering=False)
        self._sem_ctx = []
        self._sems = {}
        self.cnt = {}
        self.ins = {}
        self.outs = {}
        self._psum = {}
        self.ns = ""
        self._bind = {}
        self._basev = {}
        self._physlist = []
        self._physcnt = []
        self._free = []

    def din(self, name, shape, dtype=F32):
        t = self.nc.dram_tensor(name, list(shape), dtype, kind="ExternalInput")
        self.ins[name] = t
        return t

    def dout(self, name, shape, dtype=F32):
        t = self.nc.dram_tensor(name, list(shape), dtype, kind="ExternalOutput")
        self.outs[name] = t
        return t

    def dint(self, name, shape, dtype=F32, shared=False):
        if shared:
            return self.nc.dram_tensor(name, list(shape), dtype,
                                       kind="Internal", addr_space="Shared")
        return self.nc.dram_tensor(name, list(shape), dtype, kind="Internal")

    def sb(self, name, off, shape, dtype):
        elems = 1
        for s in shape[1:]:
            elems *= s
        nbytes = elems * (2 if dtype == BF16 else 4)
        assert off >= SB_BASE and off + nbytes <= SB_TOP, \
            f"{name}: off {off}..{off+nbytes} out of range"
        return self.nc.alloc_sbuf_tensor_at(name, list(shape), dtype,
                                            offset=off)

    def psum(self, name, shape=(128, 1024), dtype=F32):
        if name not in self._psum:
            self._psum[name] = self.nc.alloc_psum_tensor(name, list(shape),
                                                         dtype)
        return self._psum[name]

    def sem(self, name):
        key = self.ns + name
        if key not in self._bind:
            if self._free:
                idx = self._free.pop()
            else:
                idx = len(self._physlist)
                cm = self.nc.semaphore(f"ps{idx}")
                self._physlist.append(cm.__enter__())
                self._sem_ctx.append(cm)
                self._physcnt.append(0)
            self._bind[key] = idx
            self._basev[key] = self._physcnt[idx]
            self.cnt[key] = 0
        return self._physlist[self._bind[key]]

    def inc(self, inst, name, k=1):
        inst.then_inc(self.sem(name), k)
        key = self.ns + name
        self.cnt[key] += k
        self._physcnt[self._bind[key]] += k
        return self.cnt[key]

    def wait(self, eng, name, val=None):
        key = self.ns + name
        if val is None:
            val = self.cnt[key]
        eng.wait_ge(self.sem(name), self._basev[key] + val)

    def getcnt(self, name):
        return self.cnt.get(self.ns + name, 0)

    def next_phase(self, ns):
        self.nc.all_engine_barrier()
        self.nc.bump_psum(banks=-(8 - self.nc.psum_banks_remaining))
        self._free = list(range(len(self._physlist)))
        self.ns = ns

    def barrier(self):
        self.nc.all_engine_barrier()

    def setup_consts(self, tile, vals):
        self._const_tile = tile
        self._const_vals = list(vals)
        for j, v in enumerate(self._const_vals):
            self.nc.const_aps.register(F32, float(v), tile[:, j:j + 1])

    def emit_const_memsets(self, eng):
        for j, v in enumerate(self._const_vals):
            eng.memset(self._const_tile[:, j:j + 1], float(v))


def mm_chain(b, blk, pfx, tiles, psum, evac, pre_pe=None, pre_ve=None,
             extra_wait=None):
    s_mm, s_ev = pfx + "m", pfx + "e"

    def pe(t):
        if pre_pe:
            pre_pe(t)
        for i, (info, kl) in enumerate(tiles):
            bank = i % 2
            if i >= 2:
                b.wait(t, s_ev, i - 1)
            if extra_wait:
                extra_wait(t, i)
            for k, (lh, rh) in enumerate(kl):
                mm = t.matmul(psum[:, bank * 512:bank * 512 + 512][
                    :, :rh.shape[-1]], lh, rh,
                    start=(k == 0), stop=(k == len(kl) - 1))
            b.inc(mm, s_mm, 1)

    def ve(v):
        if pre_ve:
            pre_ve(v)
        for i, (info, kl) in enumerate(tiles):
            bank = i % 2
            b.wait(v, s_mm, i + 1)
            n = kl[0][1].shape[-1]
            inst = evac(v, info, psum[:, bank * 512:bank * 512 + n])
            b.inc(inst, s_ev, 1)

    blk.tensor(pe)
    blk.vector(ve)



def emit_phase_A(b, io):
    nc = b.nc

    stream = b.sb("stream", OFF_STREAM, [128, NCH, T], F32)
    so = OFF_SMALL
    onescol_f = b.sb("onescol_f", so, [128, 1], F32); so += 32
    onesrow_f = b.sb("onesrow_f", so, [1, 128], F32); so += 512
    gam_sb = b.sb("gam_sb", so, [128, NCH], F32); so += 32
    bet_sb = b.sb("bet_sb", so, [128, NCH], F32); so += 32
    qp_sb = b.sb("qp_sb", so, [128, 8], F32); so += 32
    mx_sb = b.sb("mx_sb", so, [128, 4], F32); so += 32
    kqn_sb = b.sb("kqn_sb", so, [1, NCH], F32); so += 32
    hsum_sb = b.sb("hsum_sb", so, [128, 2], BF16); so += 32
    ident_f = b.sb("ident_f", so, [128, 128], F32); so += 512
    cst = b.sb("cst", so, [128, 8], F32); so += 32
    b.setup_consts(cst, [LN0_EPS, SSD_EPS, 2.0, -1.0])
    assert so <= P0

    ybf = b.sb("ybf", P0, [128, NCH, T], BF16)
    q_fm = b.sb("q_fm", P0, [128, NCH, T], BF16)
    c_q = b.sb("c_q", P0 + 32768, [128, 4, T], BF16)
    c_kv = b.sb("c_kv", P0 + 49152, [128, 4, T], BF16)
    v_tm = b.sb("v_tm", P0 + 65536, [128, 16, D], BF16)
    k_fm = b.sb("k_fm", P0 + 98304, [128, NCH, T], BF16)
    stt1 = b.sb("stt1", P0 + 98304, [1, T], F32)
    stt2 = b.sb("stt2", P0 + 106496, [1, T], F32)
    sq = b.sb("sq", P0 + 65536, [128, 1024], F32)
    row2k = b.sb("row2k", P0 + 69632, [1, T], F32)
    wslab = b.sb("wslab", P0 + 131072, [128, NCH, 512], BF16)
    wup = b.sb("wup", P0 + 131072, [128, 4, D], BF16)
    cos_sb = b.sb("cos_sb", P0 + 32768, [128, T], BF16)
    sin_sb = b.sb("sin_sb", P0 + 36864, [128, T], BF16)
    p2t_sb = b.sb("p2t_sb", P0 + 40960, [128, 128], BF16)
    rsc = b.sb("rsc", P0 + 41216, [128, 512], BF16)
    ksq = b.sb("ksq", P0 + 42240, [128, 1024], BF16)
    rowq = b.sb("rowq", P0 + 44288, [1, 128], F32)

    pA = b.psum("pA")
    pB = b.psum("pB")
    pS = b.psum("pS", (2, 1024))

    with nc.Block() as blk:
        def sy(e):
            i = e.dma_start(out=stream[:, :, :],
                            in_=io["x_fm"].ap().rearrange("(c p) t -> p c t",
                                                          p=128))
            b.inc(i, "ldx", 16)
            i = e.dma_start(out=gam_sb[:, :], in_=io["l0_gamma"][:, :])
            b.inc(i, "ld", 16)
            i = e.dma_start(out=bet_sb[:, :], in_=io["l0_beta"][:, :])
            b.inc(i, "ld", 16)
            i = e.dma_start(out=ident_f[:, :], in_=io["ident"][:, :])
            b.inc(i, "ld", 16)

        def gp(e):
            e.memset(onescol_f[:, :], 1.0)
            e.memset(onesrow_f[:, :], 1.0)
            b.emit_const_memsets(e)
            i = e.dma_start(out=hsum_sb[:, :], in_=io["hsum"][:, :])
            b.inc(i, "ldh", 16)
            i = e.dma_start(out=wslab[:, :, :],
                            in_=io["wdq_t"].ap().rearrange(
                                "(c p) n -> p c n", p=128))
            b.inc(i, "wq", 16)

        blk.sync(sy)
        blk.gpsimd(gp)

    with nc.Block() as blk:
        def sc(s):
            b.wait(s, "ldx", 16)
            for i in range(32):
                tt, c = divmod(i, NCH)
                if i >= 2:
                    b.wait(s, "lsr", i - 1)
                b.inc(s.activation(sq[:, (i % 2) * 512:(i % 2) * 512 + 512],
                                   stream[:, c, tt * 512:tt * 512 + 512],
                                   AF.Square), "lsq", 1)

        def pe(t):
            b.wait(t, "ldx", 16)
            for i in range(32):
                tt, c = divmod(i, NCH)
                b.wait(t, "lsq", i + 1)
                if c == 0 and tt >= 1:
                    b.wait(t, "lse", tt)
                t.matmul(pS[0:1, 0:512], onescol_f[:, 0:1],
                         stream[:, c, tt * 512:tt * 512 + 512],
                         start=(c == 0), stop=(c == NCH - 1))
                mm = t.matmul(pS[0:1, 512:1024], onescol_f[:, 0:1],
                              sq[:, (i % 2) * 512:(i % 2) * 512 + 512],
                              start=(c == 0), stop=(c == NCH - 1))
                b.inc(mm, "lsr", 1)

        def ve(v):
            for tt in range(4):
                b.wait(v, "lsr", (tt + 1) * NCH)
                v.tensor_scalar_mul(stt1[:1, tt * 512:tt * 512 + 512],
                                    pS[0:1, 0:512], 1.0 / D)
                i = v.tensor_scalar_mul(stt2[:1, tt * 512:tt * 512 + 512],
                                        pS[0:1, 512:1024], 1.0 / D)
                b.inc(i, "lse", 1)

        blk.scalar(sc)
        blk.tensor(pe)
        blk.vector(ve)
    b.barrier()

    with nc.Block() as blk:
        def ve(v):
            b.inc(v.tensor_tensor(row2k[:1, :], stt1[:1, :], stt1[:1, :],
                                  ALU.mult), "fv", 1)
            b.wait(v, "fv", 1)
            b.inc(v.tensor_tensor(stt2[:1, :], stt2[:1, :], row2k[:1, :],
                                  ALU.subtract), "fv", 1)

        def sc(s):
            b.wait(s, "fv", 2)
            b.inc(s.activation(stt2[:1, :], stt2[:1, :], AF.Ln,
                               bias=LN0_EPS), "fs", 1)
            b.wait(s, "fs", 1)
            b.inc(s.activation(stt2[:1, :], stt2[:1, :], AF.Exp, scale=-0.5),
                  "fs", 1)

        blk.vector(ve)
        blk.scalar(sc)
    b.barrier()

    with nc.Block() as blk:
        def pe(t):
            for tt in range(4):
                sl = slice(tt * 512, tt * 512 + 512)
                if tt >= 1:
                    b.wait(t, "nrm", tt * NCH)
                t.matmul(pB[:, 0:512], onesrow_f[:1, :], stt1[:1, sl],
                         start=True, stop=True)
                mm = t.matmul(pB[:, 512:1024], onesrow_f[:1, :],
                              stt2[:1, sl], start=True, stop=True)
                b.inc(mm, "nb", 1)

        def ve(v):
            b.wait(v, "ld", 48)
            for tt in range(4):
                sl = slice(tt * 512, tt * 512 + 512)
                b.wait(v, "nb", tt + 1)
                for c in range(NCH):
                    b.inc(v.tensor_tensor(sq[:, 0:512], stream[:, c, sl],
                                          pB[:, 0:512], ALU.subtract),
                          "n0", 1)
                    b.wait(v, "n0")
                    b.inc(v.tensor_tensor(sq[:, 512:1024], sq[:, 0:512],
                                          pB[:, 512:1024], ALU.mult), "n0", 1)
                    b.wait(v, "n0")
                    b.inc(v.tensor_scalar(ybf[:, c, sl], sq[:, 512:1024],
                                          gam_sb[:, c:c + 1],
                                          bet_sb[:, c:c + 1],
                                          ALU.mult, ALU.add), "nrm", 1)

        blk.tensor(pe)
        blk.vector(ve)
    b.barrier()

    def ctiles(dst):
        tiles = []
        for lt in range(4):
            for tt in range(4):
                kl = [(wslab[:, c, lt * 128:lt * 128 + 128],
                       ybf[:, c, tt * 512:tt * 512 + 512]) for c in range(NCH)]
                tiles.append(((dst, lt, tt), kl))
        return tiles

    def cevac(v, info, ps):
        dst, lt, tt = info
        return v.tensor_copy(dst[:, lt, tt * 512:tt * 512 + 512], ps)

    with nc.Block() as blk:
        def pre_pe(t):
            b.wait(t, "wq", 16)
        mm_chain(b, blk, "cq", ctiles(c_q), pA, cevac, pre_pe=pre_pe)
    b.barrier()
    with nc.Block() as blk:
        def gp(e):
            i = e.dma_start(out=wslab[:, :, :],
                            in_=io["wdkv_t"].ap().rearrange(
                                "(c p) n -> p c n", p=128))
            b.inc(i, "wq", 16)

        blk.gpsimd(gp)
        def pre_pe(t):
            b.wait(t, "wq", 32)
        mm_chain(b, blk, "ckv", ctiles(c_kv), pA, cevac, pre_pe=pre_pe)
    b.barrier()

    with nc.Block() as blk:
        def gp(e):
            i = e.dma_start(out=wup[:, :, :],
                            in_=io["upv_t"].ap().rearrange(
                                "(c p) n -> p c n", p=128))
            b.inc(i, "wu", 16)
        blk.gpsimd(gp)
        tiles = []
        for tt in range(16):
            for ph in range(2):
                kl = [(c_kv[:, lc, tt * 128:tt * 128 + 128],
                       wup[:, lc, ph * 512:ph * 512 + 512]) for lc in range(4)]
                tiles.append(((tt, ph), kl))

        def vevac(v, info, ps):
            tt, ph = info
            return v.tensor_copy(v_tm[:, tt, ph * 512:ph * 512 + 512], ps)

        def pre_pe(t):
            b.wait(t, "wu", 16)
        mm_chain(b, blk, "vt", tiles, pA, vevac, pre_pe=pre_pe)
    b.barrier()

    def emit_up(name, wname, src_c, dst, wsem):
        with nc.Block() as blk:
            def gp(e):
                i = e.dma_start(out=wup[:, :, :],
                                in_=io[wname].ap().rearrange(
                                    "(c p) n -> p c n", p=128))
                b.inc(i, wsem, 16)
            blk.gpsimd(gp)
            tiles = []
            for pc in range(NCH):
                for tt in range(4):
                    kl = [(wup[:, lc, pc * 128:pc * 128 + 128],
                           src_c[:, lc, tt * 512:tt * 512 + 512])
                          for lc in range(4)]
                    tiles.append(((pc, tt), kl))

            def uevac(v, info, ps):
                pc, tt = info
                return v.tensor_copy(dst[:, pc, tt * 512:tt * 512 + 512], ps)

            def pre_pe(t):
                b.wait(t, wsem)
            mm_chain(b, blk, name, tiles, pA, uevac, pre_pe=pre_pe)
        b.barrier()

    def emit_rope(name, dst):
        s_s, s_r = name + "s", name + "r"
        with nc.Block() as blk:
            def pe(t):
                for i in range(32):
                    pc, tt = divmod(i, 4)
                    bank = i % 2
                    if i >= 2:
                        b.wait(t, s_r, 3 * (i - 1))
                    mm = t.matmul(pB[:, bank * 512:bank * 512 + 512],
                                  p2t_sb[:, :],
                                  dst[:, pc, tt * 512:tt * 512 + 512],
                                  start=True, stop=True)
                    b.inc(mm, s_s, 1)

            def ve(v):
                for i in range(32):
                    pc, tt = divmod(i, 4)
                    bank = i % 2
                    sl = slice(tt * 512, tt * 512 + 512)
                    b.wait(v, s_s, i + 1)
                    b.inc(v.tensor_tensor(rsc[:, :], dst[:, pc, sl],
                                          cos_sb[:, sl], ALU.mult), s_r, 1)
                    b.wait(v, s_r)
                    b.inc(v.tensor_tensor(dst[:, pc, sl],
                                          pB[:, bank * 512:bank * 512 + 512],
                                          sin_sb[:, sl], ALU.mult), s_r, 1)
                    b.wait(v, s_r)
                    b.inc(v.tensor_tensor(dst[:, pc, sl], dst[:, pc, sl],
                                          rsc[:, :], ALU.add), s_r, 1)

            blk.tensor(pe)
            blk.vector(ve)
        b.barrier()

    emit_up("upq", "upq_t", c_q, q_fm, "wuq")
    with nc.Block() as blk:
        def gp(e):
            i = e.dma_start(out=cos_sb[:, :], in_=io["cos2"][:, :])
            b.inc(i, "rl", 16)
            i = e.dma_start(out=sin_sb[:, :], in_=io["sin2"][:, :])
            b.inc(i, "rl", 16)
            i = e.dma_start(out=p2t_sb[:, :], in_=io["p2t"][:, :])
            b.inc(i, "rl", 16)

        def ve(v):
            b.wait(v, "rl", 48)
        blk.gpsimd(gp)
        blk.vector(ve)
    b.barrier()
    emit_rope("rpq", q_fm)
    emit_up("upk", "upk_t", c_kv, k_fm, "wuk")
    emit_rope("rpk", k_fm)

    with nc.Block() as blk:
        def sc(s):
            for i in range(32):
                pc, tt = divmod(i, 4)
                if i >= 2:
                    b.wait(s, "kn1", i - 1)
                b.inc(s.activation(ksq[:, (i % 2) * 512:(i % 2) * 512 + 512],
                                   k_fm[:, pc, tt * 512:tt * 512 + 512],
                                   AF.Square), "kn0", 1)

        def pe(t):
            b.wait(t, "ld", 48)
            b.wait(t, "ldh", 16)
            for i in range(32):
                b.wait(t, "kn0", i + 1)
                if i >= 2:
                    b.wait(t, "kn2", i - 1)
                mm = t.matmul(pS[0:2, (i % 2) * 512:(i % 2) * 512 + 512],
                              hsum_sb[:, :],
                              ksq[:, (i % 2) * 512:(i % 2) * 512 + 512],
                              start=True, stop=True)
                b.inc(mm, "kn1", 1)

        def ve(v):
            for i in range(32):
                pc, tt = divmod(i, 4)
                b.wait(v, "kn1", i + 1)
                if tt == 0 and pc >= 1:
                    b.wait(v, "kn3", pc)
                b.inc(v.reduce_max(mx_sb[0:2, tt:tt + 1],
                                   pS[0:2, (i % 2) * 512:(i % 2) * 512 + 512],
                                   axis=AX.X), "kn2", 1)
                if tt == 3:
                    b.wait(v, "kn2")
                    b.inc(v.reduce_max(qp_sb[0:2, pc:pc + 1], mx_sb[0:2, :],
                                       axis=AX.X), "kn3", 1)

        blk.scalar(sc)
        blk.tensor(pe)
        blk.vector(ve)
    b.barrier()
    with nc.Block() as blk:
        def pe(t):
            b.inc(t.transpose(pB[0:8, 0:2], qp_sb[0:2, 0:8],
                              ident_f[0:2, 0:2]), "kn4", 1)

        def ve(v):
            b.wait(v, "kn4", 1)
            b.inc(v.tensor_copy(mx_sb[0:8, 0:2], pB[0:8, 0:2]), "kn5", 1)
            b.wait(v, "kn5", 1)
            b.inc(v.reduce_max(qp_sb[0:8, 0:1], mx_sb[0:8, 0:2], axis=AX.X),
                  "kn5", 1)

        blk.tensor(pe)
        blk.vector(ve)
    b.barrier()
    with nc.Block() as blk:
        def pe(t):
            b.inc(t.transpose(pB[0:1, 0:8], qp_sb[0:8, 0:1],
                              ident_f[0:8, 0:8]), "kn6", 1)

        def ve(v):
            b.wait(v, "kn6", 1)
            b.inc(v.tensor_copy(kqn_sb[0:1, 0:8], pB[0:1, 0:8]), "kn7", 1)

        blk.tensor(pe)
        blk.vector(ve)
    b.barrier()

    with nc.Block() as blk:
        def sy(e):
            i = e.dma_start(out=io["qf"].ap().rearrange("(c p) t -> p c t",
                                                        p=128),
                            in_=q_fm[:, :, :])
            b.inc(i, "stA", 16)
            i = e.dma_start(out=io["kf"].ap().rearrange("(c p) t -> p c t",
                                                        p=128),
                            in_=k_fm[:, :, :])
            b.inc(i, "stA", 16)
            i = e.dma_start(out=io["vt"].ap().rearrange("(tt p) P -> p tt P",
                                                        p=128),
                            in_=v_tm[:, :, :])
            b.inc(i, "stA", 16)
            i = e.dma_start(out=io["kqn"][:, :], in_=kqn_sb[:1, :])
            b.inc(i, "stA", 16)
            b.wait(e, "stA")

        blk.sync(sy)
    return dict(stream=stream, q_fm=q_fm, k_fm=k_fm, v_tm=v_tm)


def build_PA():
    b = B()
    io = {}
    io["x_fm"] = b.din("x_fm", [D, T])
    for n, sh in [("wdq_t", [D, 512]), ("wdkv_t", [D, 512]),
                  ("upq_t", [512, D]), ("upk_t", [512, D]),
                  ("upv_t", [512, D]), ("l0_gamma", [128, NCH]),
                  ("l0_beta", [128, NCH]), ("cos2", [128, T]),
                  ("sin2", [128, T]), ("p2t", [128, 128]),
                  ("ident", [128, 128]), ("hsum", [128, 2])]:
        io[n] = b.din(n, sh)
    io["qf"] = b.dout("qf", [D, T], BF16)
    io["kf"] = b.dout("kf", [D, T], BF16)
    io["vt"] = b.dout("vt", [T, D], BF16)
    io["kqn"] = b.dout("kqn", [1, NCH])
    emit_phase_A(b, io)
    b.nc.finalize()
    return b



def emit_phase_B(b, io):
    nc = b.nc
    stream = b.sb("streamB", OFF_STREAM, [128, NCH, T], F32)
    so = OFF_SMALL
    onescol_f = b.sb("onescol_fB", so, [128, 1], F32); so += 32
    onesrow_f = b.sb("onesrow_fB", so, [1, 128], F32); so += 512
    ident_f = b.sb("ident_fB", so, [128, 128], F32); so += 512
    kqn4 = b.sb("kqn4", so, [4, NCH], F32); so += 32
    rmax8 = b.sb("rmax8", so, [NCH, 4], F32); so += 32
    kq8 = b.sb("kq8", so, [NCH, 1], F32); so += 32
    nsh = b.sb("nsh", so, [1, NCH], F32); so += 32
    nsh2 = b.sb("nsh2", so, [1, NCH], F32); so += 32
    biasp = b.sb("biasp", so, [128, NCH], F32); so += 32
    qp_sb = b.sb("qp_sbB", so, [2, NCH], F32); so += 32
    mx_sb = b.sb("mx_sbB", so, [2, 4], F32); so += 32
    hsum_sb = b.sb("hsum_sbB", so, [128, 2], BF16); so += 32
    cstB = b.sb("cstB", so, [128, 8], F32); so += 32
    assert so <= P0

    q_fm = b.sb("q_fmB", P0, [128, NCH, T], BF16)
    attn_out = b.sb("attn_out", P0 + 32768, [128, NCH, T], BF16)
    kpair = b.sb("kpair", P0 + 65536, [128, 2, S], BF16)
    v_aug = b.sb("v_aug", P0 + 98304, [128, 4, 64, 65], BF16)
    probsT = b.sb("probsT", P0 + 131584, [128, 2, 1024], BF16)
    rb_sb = b.sb("rb_sb", P0 + 135680, [64, 512], F32)
    rrow = b.sb("rrow", P0 + 137728, [1, 1024], F32)
    ksq = b.sb("ksqB", P0 + 65536, [128, 1024], BF16)
    wout_sb = b.sb("wout_sb", P0 + 65536, [128, NCH, D], BF16)
    stt1 = b.sb("stt1B", P0 + 81920, [1, T], F32)
    stt2 = b.sb("stt2B", P0 + 90112, [1, T], F32)
    sq = b.sb("sqB", P0 + 98304, [128, 1024], F32)
    row2k = b.sb("row2kB", P0 + 102400, [1, T], F32)

    pS4 = b.psum("pS4", (128, 2048))
    po = b.psum("po", (65, 1024))
    pB = b.psum("pBB", (128, 1024))
    b.setup_consts(cstB, [PLN_EPS])

    with nc.Block() as blk:
        def sy(e):
            i = e.dma_start(out=stream[:, :, :],
                            in_=io["x_fm"].ap().rearrange("(c p) t -> p c t",
                                                          p=128))
            b.inc(i, "ld", 16)
            i = e.dma_start(out=q_fm[:, :, :],
                            in_=io["qf"].ap().rearrange("(c p) t -> p c t",
                                                        p=128))
            b.inc(i, "ldq", 16)
            i = e.dma_start(out=kqn4[:, :], in_=io["kqn_all"][:, :])
            b.inc(i, "ldk", 16)
            i = e.dma_start(out=ident_f[:, :], in_=io["ident"][:, :])
            b.inc(i, "ldk", 16)

        def gp(e):
            e.memset(onescol_f[:, :], 1.0)
            e.memset(onesrow_f[:, :], 1.0)
            b.emit_const_memsets(e)
            e.memset(v_aug[:, :, :, :], 1.0)
            i = e.dma_start(out=hsum_sb[:, :], in_=io["hsum"][:, :])
            b.inc(i, "ldh", 16)

        blk.sync(sy)
        blk.gpsimd(gp)
    b.barrier()

    with nc.Block() as blk:
        def pe(t):
            b.wait(t, "ldk", 32)
            b.inc(t.transpose(pB[0:NCH, 0:4], kqn4[0:4, :], ident_f[0:4, 0:4]),
                  "km0", 1)

        def ve(v):
            b.wait(v, "km0", 1)
            b.inc(v.tensor_copy(rmax8[:, :], pB[0:NCH, 0:4]), "km1", 1)
            b.wait(v, "km1", 1)
            b.inc(v.reduce_max(kq8[:, 0:1], rmax8[:, :], axis=AX.X), "km1", 1)

        blk.tensor(pe)
        blk.vector(ve)
    b.barrier()

    with nc.Block() as blk:
        def sc(s):
            b.wait(s, "ldq", 16)
            for i in range(32):
                pc, tt = divmod(i, 4)
                if i >= 2:
                    b.wait(s, "qn1", i - 1)
                b.inc(s.activation(ksq[:, (i % 2) * 512:(i % 2) * 512 + 512],
                                   q_fm[:, pc, tt * 512:tt * 512 + 512],
                                   AF.Square), "qn0", 1)

        def pe(t):
            b.wait(t, "ldq", 16)
            b.wait(t, "ldh", 16)
            for i in range(32):
                b.wait(t, "qn0", i + 1)
                if i >= 2:
                    b.wait(t, "qn2", i - 1)
                mm = t.matmul(pS4[0:2, (i % 2) * 512:(i % 2) * 512 + 512],
                              hsum_sb[:, :],
                              ksq[:, (i % 2) * 512:(i % 2) * 512 + 512],
                              start=True, stop=True)
                b.inc(mm, "qn1", 1)

        def ve(v):
            for i in range(32):
                pc, tt = divmod(i, 4)
                b.wait(v, "qn1", i + 1)
                if tt == 0 and pc >= 1:
                    b.wait(v, "qn3", pc)
                b.inc(v.reduce_max(mx_sb[0:2, tt:tt + 1],
                                   pS4[0:2, (i % 2) * 512:(i % 2) * 512 + 512],
                                   axis=AX.X), "qn2", 1)
                if tt == 3:
                    b.wait(v, "qn2")
                    b.inc(v.reduce_max(qp_sb[0:2, pc:pc + 1], mx_sb[0:2, :],
                                       axis=AX.X), "qn3", 1)

        blk.scalar(sc)
        blk.tensor(pe)
        blk.vector(ve)
    b.barrier()
    with nc.Block() as blk:
        def pe(t):
            b.inc(t.transpose(pB[0:NCH, 0:2], qp_sb[0:2, 0:NCH],
                              ident_f[0:2, 0:2]), "qm0", 1)

        def ve(v):
            b.wait(v, "qm0", 1)
            b.inc(v.tensor_copy(rmax8[:, 0:2], pB[0:NCH, 0:2]), "qm1", 1)
            b.wait(v, "qm1", 1)
            b.inc(v.reduce_max(rmax8[:, 2:3], rmax8[:, 0:2], axis=AX.X),
                  "qm1", 1)
            b.wait(v, "qm1", 2)
            b.inc(v.tensor_tensor(rmax8[:, 3:4], rmax8[:, 2:3], kq8[:, 0:1],
                                  ALU.mult), "qm1", 1)

        blk.tensor(pe)
        blk.vector(ve)
    b.barrier()
    with nc.Block() as blk:
        def sc(s):
            b.inc(s.activation(rmax8[:, 3:4], rmax8[:, 3:4], AF.Ln), "qs0", 1)
            b.wait(s, "qs0", 1)
            b.inc(s.activation(rmax8[:, 3:4], rmax8[:, 3:4], AF.Exp,
                               scale=0.5), "qs0", 1)

        def ve(v):
            b.wait(v, "qs0", 2)
            b.inc(v.tensor_scalar_mul(rmax8[:, 3:4], rmax8[:, 3:4], -0.125),
                  "qs1", 1)

        def pe(t):
            b.wait(t, "qs1", 1)
            b.inc(t.transpose(pB[0:1, 0:NCH], rmax8[0:NCH, 3:4],
                              ident_f[0:NCH, 0:NCH]), "qs2", 1)

        blk.scalar(sc)
        blk.vector(ve)
        blk.tensor(pe)
    b.barrier()
    with nc.Block() as blk:
        def ve(v):
            b.inc(v.tensor_copy(nsh[0:1, :], pB[0:1, 0:NCH]), "qs3", 1)

        def pe(t):
            b.wait(t, "qs3", 1)
            for hp in range(NCH):
                mm = t.matmul(pB[:, hp * 128:hp * 128 + 1],
                              onesrow_f[:1, :], nsh[0:1, hp:hp + 1],
                              start=True, stop=True)
                b.inc(mm, "qs4", 1)

        def ve2(v):
            for hp in range(NCH):
                b.wait(v, "qs4", hp + 1)
                b.inc(v.tensor_copy(biasp[:, hp:hp + 1],
                                    pB[:, hp * 128:hp * 128 + 1]), "qs5", 1)

        blk.vector(ve)
        blk.tensor(pe)
    b.barrier()
    with nc.Block() as blk:
        blk.vector(ve2)
    b.barrier()

    with nc.Block() as blk:
        def sy(e):
            for hp in range(NCH):
                buf = hp % 2
                sname = "kvl%d" % buf
                if hp >= 2:
                    b.wait(e, "pav", (hp - 1) * 4 * 64)
                i = e.dma_start(out=kpair[:, buf, :],
                                in_=io["k_b"].ap()[hp * 128:hp * 128 + 128, :])
                b.inc(i, sname, 16)
                for hh in range(2):
                    srcv = io["v_b"].ap()[:, (2 * hp + hh) * 64:
                                          (2 * hp + hh) * 64 + 64]
                    i = e.dma_start(
                        out=v_aug[:, buf * 2 + hh, :, 0:64],
                        in_=srcv.rearrange("(kt p) d -> p kt d", p=128))
                    b.inc(i, sname, 16)

        def pe(t):
            for hp in range(NCH):
                buf = hp % 2
                b.wait(t, "kvl%d" % buf, (hp // 2 + 1) * 48)
                for qt in range(4):
                    rnd = hp * 4 + qt
                    qsl = slice(qt * 512, qt * 512 + 512)
                    if rnd >= 1:
                        b.wait(t, "fin", rnd)
                    for j in range(64):
                        p = j % 2
                        if rnd * 64 + j - 1 >= 1:
                            b.wait(t, "act", rnd * 64 + j - 1)
                        ksl = slice(j * 128, j * 128 + 128)
                        t.matmul(pS4[:, p * 1024:p * 1024 + 512],
                                 kpair[0:64, buf, ksl],
                                 q_fm[0:64, hp, qsl],
                                 start=True, stop=True)
                        mm = t.matmul(pS4[:, p * 1024 + 512:p * 1024 + 1024],
                                      kpair[64:128, buf, ksl],
                                      q_fm[64:128, hp, qsl],
                                      start=True, stop=True,
                                      tile_position=(64, 0))
                        b.inc(mm, "pes", 1)
                        if j >= 1:
                            pm = (j - 1) % 2
                            b.wait(t, "act", rnd * 64 + j)
                            t.matmul(po[:, 0:512],
                                     v_aug[:, buf * 2, j - 1, :],
                                     probsT[:, pm, 0:512],
                                     start=(j - 1 == 0), stop=False)
                            mm = t.matmul(po[:, 512:1024],
                                          v_aug[:, buf * 2 + 1, j - 1, :],
                                          probsT[:, pm, 512:1024],
                                          start=(j - 1 == 0), stop=False)
                            b.inc(mm, "pav", 1)
                    b.wait(t, "act", rnd * 64 + 64)
                    t.matmul(po[:, 0:512], v_aug[:, buf * 2, 63, :],
                             probsT[:, 1, 0:512], start=False, stop=True)
                    mm = t.matmul(po[:, 512:1024], v_aug[:, buf * 2 + 1, 63, :],
                                  probsT[:, 1, 512:1024],
                                  start=False, stop=True)
                    b.inc(mm, "pav", 1)
                    b.wait(t, "fr0", rnd + 1)
                    t.matmul(pB[0:64, 0:512], onesrow_f[:1, 0:64],
                             rrow[0:1, 0:512], start=True, stop=True)
                    mm = t.matmul(pB[0:64, 512:1024], onesrow_f[:1, 0:64],
                                  rrow[0:1, 512:1024], start=True, stop=True)
                    b.inc(mm, "fr1", 1)

        def sc(s):
            for hp in range(NCH):
                for qt in range(4):
                    rnd = hp * 4 + qt
                    for j in range(64):
                        p = j % 2
                        b.wait(s, "pes", rnd * 64 + j + 1)
                        if rnd * 64 + j - 1 >= 1:
                            b.wait(s, "pav", rnd * 64 + j - 1)
                        i = s.activation(probsT[:, p, :],
                                         pS4[:, p * 1024:p * 1024 + 1024],
                                         AF.Exp, bias=biasp[:, hp:hp + 1],
                                         scale=0.125)
                        b.inc(i, "act", 1)

        def ve(v):
            for hp in range(NCH):
                for qt in range(4):
                    rnd = hp * 4 + qt
                    qsl = slice(qt * 512, qt * 512 + 512)
                    b.wait(v, "pav", (rnd + 1) * 64)
                    v.reciprocal(rrow[0:1, 0:512], po[64:65, 0:512])
                    b.inc(v.reciprocal(rrow[0:1, 512:1024], po[64:65,
                                                               512:1024]),
                          "fr0", 1)
                    b.wait(v, "fr1", rnd + 1)
                    b.inc(v.tensor_copy(rb_sb[:, :], pB[0:64, 0:512]),
                          "fe", 1)
                    b.wait(v, "fe")
                    b.inc(v.tensor_tensor(attn_out[0:64, hp, qsl],
                                          po[0:64, 0:512], rb_sb[:, :],
                                          ALU.mult), "fe", 1)
                    b.wait(v, "fe")
                    b.inc(v.tensor_copy(rb_sb[:, :], pB[0:64, 512:1024]),
                          "fe", 1)
                    b.wait(v, "fe")
                    b.inc(v.tensor_tensor(attn_out[64:128, hp, qsl],
                                          po[0:64, 512:1024], rb_sb[:, :],
                                          ALU.mult), "fin", 1)

        blk.sync(sy)
        blk.tensor(pe)
        blk.scalar(sc)
        blk.vector(ve)
    b.barrier()

    with nc.Block() as blk:
        def gp(e):
            i = e.dma_start(out=wout_sb[:, :, :],
                            in_=io["wout_t"].ap().rearrange(
                                "(c p) n -> p c n", p=128))
            b.inc(i, "wol", 16)
        blk.gpsimd(gp)

        tiles = []
        for ot in range(NCH):
            for tt in range(4):
                kl = [(wout_sb[:, pc, ot * 128:ot * 128 + 128],
                       attn_out[:, pc, tt * 512:tt * 512 + 512])
                      for pc in range(NCH)]
                tiles.append(((ot, tt), kl))

        def wevac(v, info, ps):
            ot, tt = info
            sl = slice(tt * 512, tt * 512 + 512)
            return v.scalar_tensor_tensor(stream[:, ot, sl],
                                          stream[:, ot, sl], 2.0, ps,
                                          ALU.mult, ALU.add)

        def pre_pe(t):
            b.wait(t, "wol", 16)

        def pre_ve(v):
            b.wait(v, "ld", 16)
        mm_chain(b, blk, "wo", tiles, pB, wevac, pre_pe=pre_pe, pre_ve=pre_ve)
    b.barrier()

    emit_postln(b, stream, sq, row2k, stt1, stt2, onescol_f, onesrow_f,
                pS4, pB[:, 0:512], pB[:, 512:1024], "pln1")
    with nc.Block() as blk:
        def sy(e):
            i = e.dma_start(out=io["s1"].ap().rearrange("(c p) t -> p c t",
                                                        p=128),
                            in_=stream[:, :, :])
            b.inc(i, "stB", 16)
            b.wait(e, "stB", 16)

        blk.sync(sy)


def emit_postln(b, stream, sq, row2k, stt1, stt2, onescol_f, onesrow_f,
                psA, psB_mu, psB_rs, pfx):
    nc = b.nc
    s_sq, s_mm, s_ev = pfx + "q", pfx + "m", pfx + "e"
    with nc.Block() as blk:
        def sc(s):
            for i in range(32):
                tt, c = divmod(i, NCH)
                if i >= 2:
                    b.wait(s, s_mm, i - 1)
                b.inc(s.activation(sq[:, (i % 2) * 512:(i % 2) * 512 + 512],
                                   stream[:, c, tt * 512:tt * 512 + 512],
                                   AF.Square), s_sq, 1)

        def pe(t):
            for i in range(32):
                tt, c = divmod(i, NCH)
                b.wait(t, s_sq, i + 1)
                if c == 0 and tt >= 1:
                    b.wait(t, s_ev, tt)
                t.matmul(psA[0:1, 0:512], onescol_f[:, 0:1],
                         stream[:, c, tt * 512:tt * 512 + 512],
                         start=(c == 0), stop=(c == NCH - 1))
                mm = t.matmul(psA[0:1, 512:1024], onescol_f[:, 0:1],
                              sq[:, (i % 2) * 512:(i % 2) * 512 + 512],
                              start=(c == 0), stop=(c == NCH - 1))
                b.inc(mm, s_mm, 1)

        def ve(v):
            for tt in range(4):
                b.wait(v, s_mm, (tt + 1) * NCH)
                v.tensor_scalar_mul(stt1[:1, tt * 512:tt * 512 + 512],
                                    psA[0:1, 0:512], 1.0 / D)
                i = v.tensor_scalar_mul(stt2[:1, tt * 512:tt * 512 + 512],
                                        psA[0:1, 512:1024], 1.0 / D)
                b.inc(i, s_ev, 1)

        blk.scalar(sc)
        blk.tensor(pe)
        blk.vector(ve)
    b.barrier()
    with nc.Block() as blk:
        def ve(v):
            b.inc(v.tensor_tensor(row2k[:1, :], stt1[:1, :], stt1[:1, :],
                                  ALU.mult), pfx + "v", 1)
            b.wait(v, pfx + "v", 1)
            b.inc(v.tensor_tensor(stt2[:1, :], stt2[:1, :], row2k[:1, :],
                                  ALU.subtract), pfx + "v", 1)

        def sc(s):
            b.wait(s, pfx + "v", 2)
            b.inc(s.activation(stt2[:1, :], stt2[:1, :], AF.Ln,
                               bias=PLN_EPS), pfx + "s", 1)
            b.wait(s, pfx + "s", 1)
            b.inc(s.activation(stt2[:1, :], stt2[:1, :], AF.Exp, scale=-0.5),
                  pfx + "s", 1)

        blk.vector(ve)
        blk.scalar(sc)
    b.barrier()
    with nc.Block() as blk:
        def pe(t):
            for tt in range(4):
                sl = slice(tt * 512, tt * 512 + 512)
                if tt >= 1:
                    b.wait(t, pfx + "n", tt * NCH)
                t.matmul(psB_mu[:, 0:512], onesrow_f[:1, :], stt1[:1, sl],
                         start=True, stop=True)
                mm = t.matmul(psB_rs[:, 0:512], onesrow_f[:1, :],
                              stt2[:1, sl], start=True, stop=True)
                b.inc(mm, pfx + "b", 1)

        def ve(v):
            for tt in range(4):
                sl = slice(tt * 512, tt * 512 + 512)
                b.wait(v, pfx + "b", tt + 1)
                for c in range(NCH):
                    b.inc(v.tensor_tensor(stream[:, c, sl], stream[:, c, sl],
                                          psB_mu[:, 0:512], ALU.subtract),
                          pfx + "0", 1)
                    b.wait(v, pfx + "0")
                    b.inc(v.tensor_tensor(stream[:, c, sl], stream[:, c, sl],
                                          psB_rs[:, 0:512], ALU.mult),
                          pfx + "n", 1)

        blk.tensor(pe)
        blk.vector(ve)
    b.barrier()


def build_PB():
    b = B()
    io = {}
    io["x_fm"] = b.din("x_fm", [D, T])
    io["qf"] = b.din("qf", [D, T], BF16)
    io["k_b"] = b.din("k_b", [D, S], BF16)
    io["v_b"] = b.din("v_b", [S, D], BF16)
    io["kqn_all"] = b.din("kqn_all", [4, NCH])
    io["wout_t"] = b.din("wout_t", [D, D])
    io["ident"] = b.din("ident", [128, 128])
    io["hsum"] = b.din("hsum", [128, 2])
    io["s1"] = b.dout("s1", [D, T])
    emit_phase_B(b, io)
    b.nc.finalize()
    return b



CLAMP_MAX = 4.605170185988092


def emit_phase_C(b, io):
    nc = b.nc
    stream = b.sb("streamC", OFF_STREAM, [128, NCH, T], F32)
    so = OFF_SMALL
    onescol_f = b.sb("onescol_fC", so, [128, 1], F32); so += 32
    onesrow_f = b.sb("onesrow_fC", so, [1, 128], F32); so += 512
    onesrow_bf = b.sb("onesrow_bfC", so, [1, 128], BF16); so += 256
    ident_f = b.sb("ident_fC", so, [128, 128], F32); so += 512
    bp_col = b.sb("bp_colC", so, [128, 2], F32); so += 32
    sim_sb = b.sb("sim_sbC", so, [128, NCH], F32); so += 32
    simn_sb = b.sb("simn_sbC", so, [128, NCH], F32); so += 32
    sqs_sb = b.sb("sqs_sbC", so, [128, NCH], F32); so += 32
    temp_sb = b.sb("temp_sbC", so, [1, 1], F32); so += 32
    s_col = b.sb("s_colC", so, [128, 1], F32); so += 32
    scn_sb = b.sb("scn_sbC", so, [1, 4], F32); so += 32
    scnb_sb = b.sb("scnb_sbC", so, [128, 4], F32); so += 32
    rinv = b.sb("rinvC", so, [128, 16], F32); so += 64
    nrm2 = b.sb("nrm2C", so, [128, 1], F32); so += 32
    negm = b.sb("negmC", so, [128, 1], F32); so += 32
    den_c = b.sb("den_cC", so, [128, 1], F32); so += 32
    logit_sb = b.sb("logit_sbC", so, [128, 4], F32); so += 32
    lw_sb = b.sb("lw_sbC", so, [128, 4], F32); so += 32
    msk_sb = b.sb("msk_sbC", so, [128, 4], F32); so += 32
    ret_sb = b.sb("ret_sbC", so, [128, 4], F32); so += 32
    selmap = b.sb("selmapC", so, [4, 15], BF16); so += 32
    bprow_sb = b.sb("bprow_sbC", so, [1, 256], F32); so += 1024
    cstC = b.sb("cstC", so, [128, 8], F32); so += 32
    assert so <= P0, so

    p = P0
    ybf = b.sb("ybfC", p, [128, NCH, T], BF16); p += 32768
    proj_fm = b.sb("proj_fmC", p, [128, 2, T], F32); p += 16384
    retT_sb = b.sb("retTC", p, [4, T], BF16); p += 4096
    dq_sb = b.sb("dqC", p, [15, T], BF16); p += 4096
    sc_sb = b.sb("scC", p, [15, 512], BF16); p += 1024
    sq256 = b.sb("sq256C", p, [128, 256], F32); p += 1024
    W_sb = b.sb("W_sbC", p, [128, NCH, D], BF16); p += 16384
    wp_sb = b.sb("wp_sbC", p, [128, NCH, 256], F32); p += 8192
    wd_sb = b.sb("wd_sbC", p, [128, NCH, 15], BF16); p += 256
    wu_sb = b.sb("wu_sbC", p, [15, D], BF16); p += 2048
    stt1 = b.sb("stt1C", p, [1, T], F32); p += 8192
    stt2 = b.sb("stt2C", p, [1, T], F32); p += 8192
    sq = b.sb("sqC", p, [128, 1024], F32); p += 4096
    row2k = b.sb("row2kC", p, [1, T], F32); p += 8192
    assert p <= SB_TOP, p

    pA = b.psum("pAC")
    pB = b.psum("pBC")
    pSt = b.psum("pStC", (2, 1024))

    b.setup_consts(cstC, [PLN_EPS])

    with nc.Block() as blk:
        def sy(e):
            i = e.dma_start(out=stream[:, :, :],
                            in_=io["s1"].ap().rearrange("(c p) t -> p c t",
                                                        p=128))
            b.inc(i, "ldx", 16)
            i = e.dma_start(out=ident_f[:, :], in_=io["ident"][:, :])
            b.inc(i, "ld", 16)
            i = e.dma_start(out=bp_col[:, :], in_=io["bp_col"][:, :])
            b.inc(i, "ld", 16)
            i = e.dma_start(out=sim_sb[:, :], in_=io["sim_fm"][:, :])
            b.inc(i, "ld", 16)
            i = e.dma_start(out=temp_sb[:, :], in_=io["temp"][:, :])
            b.inc(i, "ld", 16)

        def gp(e):
            e.memset(onescol_f[:, :], 1.0)
            e.memset(onesrow_f[:, :], 1.0)
            e.memset(onesrow_bf[:, :], 1.0)
            b.emit_const_memsets(e)
            i = e.dma_start(out=W_sb[:, :, :],
                            in_=io["W_t"].ap().rearrange("(c p) n -> p c n",
                                                         p=128))
            b.inc(i, "lw", 16)
            i = e.dma_start(out=wp_sb[:, :, :],
                            in_=io["wp_t"].ap().rearrange("(c p) n -> p c n",
                                                          p=128))
            b.inc(i, "lw", 16)
            i = e.dma_start(out=wd_sb[:, :, :],
                            in_=io["wd_t"].ap().rearrange("(c p) n -> p c n",
                                                          p=128))
            b.inc(i, "lw", 16)
            i = e.dma_start(out=wu_sb[:, :], in_=io["wu_t"][:, :])
            b.inc(i, "lw", 16)
            i = e.dma_start(out=selmap[:, :], in_=io["selmap"][:, :])
            b.inc(i, "lw", 16)
            i = e.dma_start(out=bprow_sb[:, :], in_=io["bp_row"][:, :])
            b.inc(i, "lw", 16)

        def ve(v):
            b.wait(v, "ldx", 16)
            for i in range(32):
                tt, c = divmod(i, NCH)
                b.inc(v.tensor_copy(ybf[:, c, tt * 512:tt * 512 + 512],
                                    stream[:, c, tt * 512:tt * 512 + 512]),
                      "yc", 1)

        blk.sync(sy)
        blk.gpsimd(gp)
        blk.vector(ve)
    b.barrier()

    with nc.Block() as blk:
        def pe(t):
            b.wait(t, "lw", 96)
            b.wait(t, "yc", 32)
            for i in range(16):
                bank = i % 2
                if i >= 2:
                    b.wait(t, "c1s", i - 1)
                for c in range(NCH):
                    t.matmul(pA[0:128, bank * 512:bank * 512 + 256],
                             stream[:, c, i * 128:i * 128 + 128],
                             wp_sb[:, c, :],
                             start=(c == 0), stop=False)
                mm = t.matmul(pA[0:128, bank * 512:bank * 512 + 256],
                              onesrow_f[:1, :], bprow_sb[:1, :],
                              start=False, stop=True)
                b.inc(mm, "c1m", 1)

        def sc(s):
            for i in range(16):
                bank = i % 2
                b.wait(s, "c1m", i + 1)
                if i >= 1:
                    b.wait(s, "c1n", i)
                b.inc(s.activation(sq256[:, :],
                                   pA[0:128, bank * 512:bank * 512 + 256],
                                   AF.Square), "c1s", 1)
                b.wait(s, "c1n", i + 1)
                b.inc(s.activation(nrm2[:, 0:1], nrm2[:, 0:1], AF.Ln),
                      "c1l", 1)
                b.wait(s, "c1l")
                b.inc(s.activation(rinv[:, i:i + 1], nrm2[:, 0:1], AF.Exp,
                                   scale=-0.5), "c1r", 1)

        def ve(v):
            for i in range(16):
                b.wait(v, "c1s", i + 1)
                if i >= 1:
                    b.wait(v, "c1r", i)
                b.inc(v.reduce_sum(nrm2[:, 0:1], sq256[:, :], axis=AX.X),
                      "c1n", 1)

        blk.tensor(pe)
        blk.scalar(sc)
        blk.vector(ve)
    b.barrier()

    with nc.Block() as blk:
        tiles = []
        for pt in range(2):
            for tt in range(4):
                kl = [(wp_sb[:, c, pt * 128:pt * 128 + 128],
                       stream[:, c, tt * 512:tt * 512 + 512])
                      for c in range(NCH)]
                tiles.append(((pt, tt), kl))

        def pevac(v, info, ps):
            pt, tt = info
            return v.tensor_scalar(proj_fm[:, pt, tt * 512:tt * 512 + 512],
                                   ps, bp_col[:, pt:pt + 1], None, ALU.add)

        def c2_pre_ve(v):
            b.wait(v, "ld", 64)
        mm_chain(b, blk, "c2", tiles, pB, pevac, pre_ve=c2_pre_ve)

        def sc(s):
            b.wait(s, "ld", 64)
            b.inc(s.activation(sqs_sb[:, :], sim_sb[:, :], AF.Square),
                  "sm0", 1)

        def pe(t):
            b.wait(t, "sm0", 1)
            for c in range(2):
                mm = t.matmul(pSt[0:1, 0:4], onescol_f[:, 0:1],
                              sqs_sb[:, c * 4:c * 4 + 4],
                              start=(c == 0), stop=(c == 1))
            b.inc(mm, "sm1", 1)

        blk.scalar(sc)
        blk.tensor(pe)
    b.barrier()
    with nc.Block() as blk:
        def sc2(s):
            b.inc(s.activation(scn_sb[:1, :], pSt[0:1, 0:4], AF.Ln),
                  "sm2", 1)
            b.wait(s, "sm2", 1)
            b.inc(s.activation(scn_sb[:1, :], scn_sb[:1, :], AF.Exp,
                               scale=-0.5), "sm2", 1)
        blk.scalar(sc2)
    b.barrier()
    with nc.Block() as blk:
        def pe2(t):
            b.inc(t.matmul(pB[0:128, 0:4], onesrow_f[:1, :], scn_sb[:1, :],
                           start=True, stop=True), "sm3", 1)

        def ve_a(v):
            b.wait(v, "sm3", 1)
            b.inc(v.tensor_copy(scnb_sb[:, :], pB[0:128, 0:4]), "sm4", 1)
            b.wait(v, "sm4")
            for c in range(2):
                b.inc(v.tensor_tensor(simn_sb[:, c * 4:c * 4 + 4],
                                      sim_sb[:, c * 4:c * 4 + 4],
                                      scnb_sb[:, :], ALU.mult), "sm4", 1)
            b.wait(v, "ld", 64)
            b.inc(v.tensor_scalar_min(temp_sb[:1, :], temp_sb[:1, :],
                                      CLAMP_MAX), "sm5", 1)

        blk.tensor(pe2)
        blk.vector(ve_a)
    b.barrier()
    with nc.Block() as blk:
        def sc3(s):
            b.inc(s.activation(temp_sb[:1, :], temp_sb[:1, :], AF.Exp),
                  "sm6", 1)
        blk.scalar(sc3)
    b.barrier()
    with nc.Block() as blk:
        def pe3(t):
            b.inc(t.matmul(pB[0:128, 4:5], onesrow_f[:1, :], temp_sb[:1, :],
                           start=True, stop=True), "sm7", 1)

        def ve2(v):
            b.wait(v, "sm7", 1)
            b.inc(v.tensor_copy(s_col[:, :], pB[0:128, 4:5]), "sm8", 1)

        blk.tensor(pe3)
        blk.vector(ve2)
    b.barrier()

    with nc.Block() as blk:
        def pe(t):
            for i in range(16):
                bank = i % 2
                for c in range(2):
                    mm = t.matmul(pA[0:128, bank * 512:bank * 512 + 4],
                                  proj_fm[:, c, i * 128:i * 128 + 128],
                                  simn_sb[:, c * 4:c * 4 + 4],
                                  start=(c == 0), stop=(c == 1))
                b.inc(mm, "c3m", 1)
                b.wait(t, "c3r", i + 1)
                b.inc(t.transpose(pB[0:4, bank * 512:bank * 512 + 128],
                                  ret_sb[:, :], ident_f[:, :]), "c3p", 1)

        def ve(v):
            for i in range(16):
                bank = i % 2
                b.wait(v, "c3m", i + 1)
                b.inc(v.tensor_tensor(nrm2[:, 0:1], rinv[:, i:i + 1],
                                      s_col[:, 0:1], ALU.mult), "c3v", 1)
                b.wait(v, "c3v")
                b.inc(v.tensor_scalar_mul(logit_sb[:, :],
                                          pA[0:128,
                                             bank * 512:bank * 512 + 4],
                                          nrm2[:, 0:1]), "c3v", 1)
                b.wait(v, "c3v")
                b.inc(v.reduce_max(negm[:, 0:1], logit_sb[:, :], axis=AX.X,
                                   negate=True), "c3v", 1)
                b.wait(v, "c3v")
                b.inc(v.tensor_scalar(msk_sb[:, :], logit_sb[:, :],
                                      negm[:, 0:1], 0.0, ALU.add, ALU.is_ge),
                      "c3k", 1)
                b.wait(v, "c3x", i + 1)
                b.inc(v.reduce_sum(den_c[:, 0:1], lw_sb[:, :], axis=AX.X),
                      "c3w", 1)
                b.wait(v, "c3w")
                b.inc(v.reciprocal(den_c[:, 0:1], den_c[:, 0:1]), "c3w", 1)
                b.wait(v, "c3w")
                b.inc(v.tensor_scalar_mul(lw_sb[:, :], lw_sb[:, :],
                                          den_c[:, 0:1]), "c3w", 1)
                b.wait(v, "c3w")
                b.inc(v.tensor_tensor(ret_sb[:, :], lw_sb[:, :],
                                      msk_sb[:, :], ALU.mult), "c3r", 1)
                b.wait(v, "c3p", i + 1)
                b.inc(v.tensor_copy(retT_sb[:, i * 128:i * 128 + 128],
                                    pB[0:4, bank * 512:bank * 512 + 128]),
                      "c3t", 1)

        def sc(s):
            for i in range(16):
                b.wait(s, "c3k", i + 1)
                b.inc(s.activation(lw_sb[:, :], logit_sb[:, :], AF.Exp,
                                   bias=negm[:, 0:1]), "c3x", 1)

        blk.tensor(pe)
        blk.vector(ve)
        blk.scalar(sc)
    b.barrier()

    with nc.Block() as blk:
        def pe(t):
            for tt in range(4):
                bank = tt % 2
                if tt >= 2:
                    b.wait(t, "c4e", 2 * (tt - 1))
                for c in range(NCH):
                    mm = t.matmul(pA[0:15, bank * 512:bank * 512 + 512],
                                  wd_sb[:, c, :],
                                  ybf[:, c, tt * 512:tt * 512 + 512],
                                  start=(c == 0), stop=(c == NCH - 1))
                b.inc(mm, "c4m", 1)
                mm = t.matmul(pB[0:15, bank * 512:bank * 512 + 512],
                              selmap[:, :],
                              retT_sb[:, tt * 512:tt * 512 + 512],
                              start=True, stop=True)
                b.inc(mm, "c4s", 1)

        def ve(v):
            for tt in range(4):
                bank = tt % 2
                b.wait(v, "c4s", tt + 1)
                if tt >= 1:
                    b.wait(v, "c4e", 2 * tt)
                b.inc(v.tensor_copy(sc_sb[:, :],
                                    pB[0:15, bank * 512:bank * 512 + 512]),
                      "c4e", 1)
                b.wait(v, "c4m", tt + 1)
                b.wait(v, "c4e")
                b.inc(v.tensor_tensor(dq_sb[:, tt * 512:tt * 512 + 512],
                                      pA[0:15, bank * 512:bank * 512 + 512],
                                      sc_sb[:, :], ALU.mult), "c4e", 1)

        blk.tensor(pe)
        blk.vector(ve)
    b.barrier()

    with nc.Block() as blk:
        tiles = []
        for ot in range(NCH):
            for tt in range(4):
                kl = [(W_sb[:, c, ot * 128:ot * 128 + 128],
                       ybf[:, c, tt * 512:tt * 512 + 512])
                      for c in range(NCH)]
                kl.append((wu_sb[0:15, ot * 128:ot * 128 + 128],
                           dq_sb[0:15, tt * 512:tt * 512 + 512]))
                tiles.append(((ot, tt), kl))

        def cevac(v, info, ps):
            ot, tt = info
            sl = slice(tt * 512, tt * 512 + 512)
            return v.tensor_tensor(stream[:, ot, sl], stream[:, ot, sl],
                                   ps, ALU.add)

        mm_chain(b, blk, "c5", tiles, pB, cevac)
    b.barrier()

    emit_postln(b, stream, sq, row2k, stt1, stt2, onescol_f, onesrow_f,
                pSt, pB[:, 0:512], pB[:, 512:1024], "pln2")
    with nc.Block() as blk:
        def sy(e):
            i = e.dma_start(out=io["s2"].ap().rearrange("(c p) t -> p c t",
                                                        p=128),
                            in_=stream[:, :, :])
            b.inc(i, "stC", 16)
            b.wait(e, "stC", 16)

        blk.sync(sy)


def build_PC():
    b = B()
    io = {}
    io["s1"] = b.din("s1", [D, T])
    io["W_t"] = b.din("W_t", [D, D])
    io["wp_t"] = b.din("wp_t", [D, 256])
    io["bp_col"] = b.din("bp_col", [128, 2])
    io["bp_row"] = b.din("bp_row", [1, 256])
    io["sim_fm"] = b.din("sim_fm", [128, NCH])
    io["temp"] = b.din("temp", [1, 1])
    io["wd_t"] = b.din("wd_t", [D, 15])
    io["wu_t"] = b.din("wu_t", [15, D])
    io["selmap"] = b.din("selmap", [4, 15])
    io["ident"] = b.din("ident", [128, 128])
    io["s2"] = b.dout("s2", [D, T])
    emit_phase_C(b, io)
    b.nc.finalize()
    return b



NHALF = 2
NCK = 8
_NCAP = None
_X = set()


class SSDCtx:
    pass


def alloc_ssd(b, with_pass2):
    c = SSDCtx()
    so = OFF_SMALL
    c.onescol_f = b.sb("onescol_fD", so, [128, 1], F32); so += 32
    c.onesrow_f = b.sb("onesrow_fD", so, [1, 128], F32); so += 512
    c.ident_f = b.sb("ident_fD", so, [128, 128], F32); so += 512
    c.ident_bf = b.sb("ident_bfD", so, [128, 128], BF16); so += 256
    c.ba2 = b.sb("ba2D", so, [16, 1], F32); so += 32
    c.tot_la = b.sb("tot_laD", so, [16, 1], F32); so += 32
    c.lrow = b.sb("lrowD", so, [1, 16], F32); so += 32
    c.b1_sb = b.sb("b1_sbD", so, [128, 2, 128], F32); so += 1024
    c.tmp_sb = b.sb("tmp_sbD", so, [128, 2], F32); so += 32
    c.e_part = b.sb("e_partD", so, [128, 2], F32); so += 32
    c.lamC = b.sb("lamCD", so, [64, 8], F32); so += 32
    c.k1_sb = b.sb("k1_sbD", so, [128, 2, 64], BF16); so += 256
    c.dl2_sb = b.sb("dl2_sbD", so, [128, 2, 128], F32); so += 1024
    c.d_sb = b.sb("d_sbD", so, [128, 2, 128], BF16); so += 512
    c.at_sb = b.sb("at_sbD", so, [128, 2, 128], BF16); so += 512
    c.qc_sb = b.sb("qc_sbD", so, [64, 4, 128], BF16); so += 1024
    c.q1_sb = b.sb("q1_sbD", so, [64, 2, 128], BF16); so += 512
    c.rinv = b.sb("rinvD", so, [128, 1], F32); so += 32
    c.cstD = b.sb("cstD", so, [128, 8], F32); so += 32
    c.lam_sb = b.sb("lam_sbD", so, [64, 4, 128], F32); so += 2048
    assert so <= P0 + 8192, so

    p = P0 + 8192
    c.ybf = b.sb("ybfD", p, [128, NCH, 1024], BF16); p += 16384
    c.v_aug = b.sb("v_augD", p, [128, NCK, 16, 65], BF16); p += 16640
    c.slab = b.sb("slabD", p, [128, NCH, 1024], BF16)
    c.kTall = b.sb("kTallD", p, [128, NCK, 8, 128], BF16)
    p += 16384
    c.wa_sb = b.sb("wa_sbD", p, [128, NCH, 16], BF16); p += 256
    c.a_sb_off = p
    c.a_sb = b.sb("a_sbD", p, [16, 1024], F32)
    c.elu1 = b.sb("elu1D", p, [128, 512], F32)
    c.elu2 = b.sb("elu2D", p + 2048, [128, 512], F32)
    p += 4096
    c.cum_fm = b.sb("cum_fmD", p, [16, 1024], F32); p += 4096
    c.cum_tm = b.sb("cum_tmD", p, [128, NCK, 16], F32); p += 512
    c.stt = [None, None]
    c.stt[0] = b.sb("stt0D", p, [64, 16, 65], F32); p += 4160
    c.stt[1] = b.sb("stt1D_", p, [64, 16, 65], F32); p += 4160
    c.stt_bf = b.sb("sttbfD", p, [64, 2, 65], BF16); p += 288
    c.lamp = b.sb("lampD", p, [64, 16], F32); p += 64
    c.lamrep = b.sb("lamrepD", p, [64, 16], F32); p += 64
    c.meta_sb = b.sb("meta_sbD", p, [64, 4], F32); p += 64
    c.maskneg = b.sb("masknegD", c.a_sb_off, [128, 128], F32)
    if with_pass2:
        c.scanout = b.sb("scanoutD", p, [128, 16, 1024], BF16)
        c.sacc = b.sb("saccD", p, [64, 16, 65], F32)
        c.sacc2 = b.sb("sacc2D", p + 4160, [64, 16, 65], F32)
        c.sldp = b.sb("sldpD", p + 8320, [64, 16, 65], F32)
        c.sel2 = b.sb("sel2D", p + 12480, [64, 16, 65], F32)
        p += 32768
        c.k_tm = b.sb("k_tmD", p, [128, NCK, 1024], BF16)
        c.scanout_fm = b.sb("scanout_fmD", p, [128, NCH, T], BF16)
        p += 16384
        c.q_fm = b.sb("q_fmD", p, [128, NCH, 1024], BF16); p += 16384
    else:
        c.k_tm = b.sb("k_tmD", p, [128, NCK, 1024], BF16); p += 16384
        c.sacc = c.sldp = None
    assert p <= SB_TOP, p

    c.pM2 = b.psum("pM2D", (128, 1536))
    c.pAT = b.psum("pATD", (128, 1024))
    c.pNS = b.psum("pNSD", (128, 1024))
    c.pX = b.psum("pXD", (128, 512))
    return c


def emit_ssd_qkva(b, c, io, stream, half, with_q):
    nc = b.nc
    hof = half * 1024
    _names = ["yc", "wq", "qm", "qa", "qb", "qx", "qd", "wk", "km", "ka",
              "kb", "kx", "kd", "wv", "vm", "vd", "wa", "am", "ae", "al",
              "cs", "ts", "tt", "ct", "cte"]
    _base = {n: b.getcnt("qk_" + n) for n in _names}

    class _HX:
        def __add__(self, other):
            return "qk_" + other
    hx = _HX()

    _orig_wait = b.wait

    def _qwait(eng, name, val=None):
        if name.startswith("qk_") and val is not None:
            val = _base[name[3:]] + val
        _orig_wait(eng, name, val)

    pQ = c.pM2

    with nc.Block() as blk:
        def ve(v):
            _qwait(v, "ldx", 16)
            for i in range(16):
                tt, cch = divmod(i, NCH)
                sl = slice(hof + tt * 512, hof + tt * 512 + 512)
                b.inc(v.tensor_copy(c.ybf[:, cch, tt * 512:tt * 512 + 512],
                                    stream[:, cch, sl]), hx + "yc", 1)
        blk.vector(ve)
    b.barrier()

    def load_slab(wname, sem):
        def gp(e):
            i = e.dma_start(out=c.slab[:, :, :],
                            in_=io[wname].ap().rearrange("(c p) n -> p c n",
                                                         p=128))
            b.inc(i, sem, 16)
        return gp

    if with_q:
        with nc.Block() as blk:
            blk.gpsimd(load_slab("wq", hx + "wq"))

            def pe(t):
                _qwait(t, hx + "wq", 16)
                _qwait(t, hx + "yc", 16)
                for i in range(16):
                    pc, tt = divmod(i, 2)
                    bank = i % 2
                    if i >= 2:
                        _qwait(t, hx + "qd", i - 1)
                    for cch in range(NCH):
                        mm = t.matmul(pQ[:, bank * 512:bank * 512 + 512],
                                      c.slab[:, cch, pc * 128:pc * 128 + 128],
                                      c.ybf[:, cch, tt * 512:tt * 512 + 512],
                                      start=(cch == 0), stop=(cch == NCH - 1))
                    b.inc(mm, hx + "qm", 1)

            def ve(v):
                for i in range(16):
                    pc, tt = divmod(i, 2)
                    bank = i % 2
                    ps = pQ[:, bank * 512:bank * 512 + 512]
                    _qwait(v, hx + "qm", i + 1)
                    if i >= 1:
                        _qwait(v, hx + "qd", i)
                    b.inc(v.tensor_scalar_max(c.elu1[:, :], ps, 0.0),
                          hx + "qa", 1)
                    _qwait(v, hx + "qa")
                    b.inc(v.tensor_scalar_min(c.elu2[:, :], ps, 0.0),
                          hx + "qb", 1)
                    _qwait(v, hx + "qx", i + 1)
                    b.inc(v.tensor_tensor(
                        c.q_fm[:, pc, tt * 512:tt * 512 + 512],
                        c.elu1[:, :], c.elu2[:, :], ALU.add), hx + "qd", 1)

            def sc(s):
                for i in range(16):
                    _qwait(s, hx + "qb", i + 1)
                    b.inc(s.activation(c.elu2[:, :], c.elu2[:, :], AF.Exp),
                          hx + "qx", 1)

            blk.tensor(pe)
            blk.vector(ve)
            blk.scalar(sc)
        b.barrier()

    with nc.Block() as blk:
        blk.gpsimd(load_slab("wk", hx + "wk"))

        def pe(t):
            _qwait(t, hx + "wk", 16)
            _qwait(t, hx + "yc", 16)
            for i in range(16):
                ck, ph = divmod(i, 2)
                bank = i % 2
                if i >= 2:
                    _qwait(t, hx + "kd", i - 1)
                for cch in range(NCH):
                    mm = t.matmul(pQ[:, bank * 512:bank * 512 + 512],
                                  c.ybf[:, cch, ck * 128:ck * 128 + 128],
                                  c.slab[:, cch, ph * 512:ph * 512 + 512],
                                  start=(cch == 0), stop=(cch == NCH - 1))
                b.inc(mm, hx + "km", 1)

        def ve(v):
            for i in range(16):
                ck, ph = divmod(i, 2)
                bank = i % 2
                ps = pQ[:, bank * 512:bank * 512 + 512]
                _qwait(v, hx + "km", i + 1)
                if i >= 1:
                    _qwait(v, hx + "kd", i)
                b.inc(v.tensor_scalar_max(c.elu1[:, :], ps, 0.0), hx + "ka", 1)
                _qwait(v, hx + "ka")
                b.inc(v.tensor_scalar_min(c.elu2[:, :], ps, 0.0), hx + "kb", 1)
                _qwait(v, hx + "kx", i + 1)
                b.inc(v.tensor_tensor(
                    c.k_tm[:, ck, ph * 512:ph * 512 + 512],
                    c.elu1[:, :], c.elu2[:, :], ALU.add), hx + "kd", 1)

        def sc(s):
            for i in range(16):
                _qwait(s, hx + "kb", i + 1)
                b.inc(s.activation(c.elu2[:, :], c.elu2[:, :], AF.Exp),
                      hx + "kx", 1)

        blk.tensor(pe)
        blk.vector(ve)
        blk.scalar(sc)
    b.barrier()

    with nc.Block() as blk:
        blk.gpsimd(load_slab("wv", hx + "wv"))

        def pe(t):
            _qwait(t, hx + "wv", 16)
            for i in range(16):
                ck, ph = divmod(i, 2)
                bank = i % 2
                if i >= 2:
                    _qwait(t, hx + "vd", i - 1)
                for cch in range(NCH):
                    mm = t.matmul(pQ[:, bank * 512:bank * 512 + 512],
                                  c.ybf[:, cch, ck * 128:ck * 128 + 128],
                                  c.slab[:, cch, ph * 512:ph * 512 + 512],
                                  start=(cch == 0), stop=(cch == NCH - 1))
                b.inc(mm, hx + "vm", 1)

        def ve(v):
            for i in range(16):
                ck, ph = divmod(i, 2)
                bank = i % 2
                ps = pQ[:, bank * 512:bank * 512 + 512]
                _qwait(v, hx + "vm", i + 1)
                dst = c.v_aug[:, ck, ph * 8:ph * 8 + 8, 0:64]
                b.inc(v.tensor_copy(dst, ps.rearrange("p (h d) -> p h d",
                                                      d=64)), hx + "vd", 1)

        blk.tensor(pe)
        blk.vector(ve)
    b.barrier()

    with nc.Block() as blk:
        def gp(e):
            i = e.dma_start(out=c.wa_sb[:, :, :],
                            in_=io["wa"].ap().rearrange("(c p) n -> p c n",
                                                        p=128))
            b.inc(i, hx + "wa", 16)

        def pe(t):
            _qwait(t, hx + "wa", 16)
            _qwait(t, hx + "yc", 16)
            for tt in range(2):
                if tt == 1:
                    _qwait(t, hx + "ae", 1)
                for cch in range(NCH):
                    mm = t.matmul(pQ[0:16, 0:512],
                                  c.wa_sb[:, cch, :],
                                  c.ybf[:, cch, tt * 512:tt * 512 + 512],
                                  start=(cch == 0), stop=(cch == NCH - 1))
                b.inc(mm, hx + "am", 1)

        def sc(s):
            b.wait(s, "ldba", 1)
            for tt in range(2):
                _qwait(s, hx + "am", tt + 1)
                b.inc(s.activation(c.a_sb[:, tt * 512:tt * 512 + 512],
                                   pQ[0:16, 0:512], AF.Sigmoid,
                                   bias=c.ba2[0:16, 0:1]), hx + "ae", 1)
            _qwait(s, hx + "ae", 2)
            b.inc(s.activation(c.a_sb[:, :], c.a_sb[:, :], AF.Ln),
                  hx + "al", 1)

        def ve(v):
            _qwait(v, hx + "al", 1)
            for ck in range(NCK):
                b.inc(v.tensor_tensor_scan(
                    c.cum_fm[:, ck * 128:ck * 128 + 128],
                    c.a_sb[:, ck * 128:ck * 128 + 128],
                    c.a_sb[:, ck * 128:ck * 128 + 128],
                    0.0, ALU.add, ALU.bypass), hx + "cs", 1)
            _qwait(v, hx + "cs", NCK)
            b.inc(v.reduce_sum(c.tmp_sb[0:16, 0:1], c.a_sb[:, :], axis=AX.X),
                  hx + "ts", 1)
            _qwait(v, hx + "ts", 1)
            b.inc(v.tensor_tensor(c.tot_la[:, 0:1], c.tot_la[:, 0:1],
                                  c.tmp_sb[0:16, 0:1], ALU.add), hx + "tt", 1)

        blk.gpsimd(gp)
        blk.tensor(pe)
        blk.scalar(sc)
        blk.vector(ve)
    b.barrier()

    with nc.Block() as blk:
        def pe(t):
            _qwait(t, "ldi", None)
            for ck in range(NCK):
                if ck >= 1:
                    _qwait(t, hx + "cte", ck)
                mm = t.transpose(c.pX[0:128, 0:16],
                                 c.cum_fm[:, ck * 128:ck * 128 + 128],
                                 c.ident_f[0:16, 0:16])
                b.inc(mm, hx + "ct", 1)

        def ve(v):
            for ck in range(NCK):
                _qwait(v, hx + "ct", ck + 1)
                b.inc(v.tensor_copy(c.cum_tm[:, ck, :], c.pX[0:128, 0:16]),
                      hx + "cte", 1)

        blk.tensor(pe)
        blk.vector(ve)
    b.barrier()


def emit_kT_pre(b, c, half):
    nc = b.nc
    base = b.getcnt("ktp"), b.getcnt("kte")
    with nc.Block() as blk:
        def pe(t):
            b.wait(t, "ldb", None)
            for j in range(128):
                ck, h = divmod(j, 16)
                if j >= 1:
                    b.wait(t, "kte", base[1] + j)
                mm = t.matmul(c.pX[0:64, 0:128],
                              c.k_tm[:, ck, h * 64:h * 64 + 64],
                              c.ident_bf[:, :], start=True, stop=True)
                b.inc(mm, "ktp", 1)

        def ve(v):
            for j in range(128):
                ck, h = divmod(j, 16)
                b.wait(v, "ktp", base[0] + j + 1)
                hb = 64 * (h % 2)
                b.inc(v.tensor_copy(c.kTall[hb:hb + 64, ck, h // 2, :],
                                    c.pX[0:64, 0:128]), "kte", 1)

        blk.tensor(pe)
        blk.vector(ve)
    b.barrier()


def emit_ssd_scan(b, c, io, half, pass2):
    nc = b.nc
    if pass2:
        with nc.Block() as blk0:
            def gpm(e):
                i = e.dma_start(out=c.maskneg[:, :], in_=io["maskneg"][:, :])
                b.inc(i, "mkl", 16)
            blk0.gpsimd(gpm)

    nstep = NCK * 16 if _NCAP is None else _NCAP
    nround = nstep + 7
    pe_base = b.getcnt("rpe")
    dv_base = b.getcnt("rdv")
    ac_base = b.getcnt("rac")

    def dv_ops(r):
        n = 0
        if 0 <= r - 6 < nstep:
            n += 4 if pass2 else 1
        if 0 <= r - 4 < nstep:
            n += 4 if pass2 else 1
        if 0 <= r - 2 < nstep:
            n += 4 if pass2 else 1
        if 0 <= r < nstep:
            n += 1
        return max(n, 1)

    def ac_ops(r):
        n = 0
        if 0 <= r - 3 < nstep:
            n += 4 if pass2 else 2
        return max(n, 1)

    dv_cum = [0]
    ac_cum = [0]
    for r in range(nround):
        dv_cum.append(dv_cum[-1] + dv_ops(r))
        ac_cum.append(ac_cum[-1] + ac_ops(r))

    def M2(s):
        return c.pM2[:, (s % 3) * 512:(s % 3) * 512 + 128]

    def AT(s):
        return c.pAT[:, (s % 2) * 512:(s % 2) * 512 + 128]

    def NUM(s):
        return c.pNS[0:128, (s % 2) * 512:(s % 2) * 512 + 65]

    def PST(s):
        return c.pNS[0:64, (s % 2) * 512 + 128:(s % 2) * 512 + 193]

    def chh(s):
        return divmod(s, 16)

    with nc.Block() as blk:
        def pe(t):
            b.wait(t, "ldi", None)
            if pass2:
                b.wait(t, "ldb", None)
            for r in range(nround):
                b.wait(t, "rdv", dv_base + dv_cum[r])
                b.wait(t, "rac", ac_base + ac_cum[r])
                last = None
                s = r - 5
                if 0 <= s < nstep:
                    ck, h = chh(s)
                    if pass2 and "num" not in _X:
                        t.matmul(NUM(s), c.at_sb[:, s % 2, :],
                                 c.v_aug[:, ck, h, :],
                                 start=True, stop=False)
                        t.matmul(NUM(s), c.q1_sb[:, s % 2, :],
                                 c.stt_bf[:, s % 2, :],
                                 start=False, stop=True)
                    last = t.matmul(PST(s), c.k1_sb[:, s % 2, :],
                                    c.v_aug[:, ck, h, :],
                                    start=True, stop=True)
                s = r - 3
                if pass2 and 0 <= s < nstep and "atm" not in _X:
                    ck, h = chh(s)
                    hb = 64 * (h % 2)
                    last = t.matmul(AT(s), c.kTall[hb:hb + 64, ck, h // 2, :],
                                    c.q_fm[hb:hb + 64, h // 2,
                                           ck * 128:ck * 128 + 128],
                                    start=True, stop=True,
                                    tile_position=(hb, 0))
                s = r - 1
                if 0 <= s < nstep:
                    last = t.matmul(M2(s), c.b1_sb[:, s % 2, :],
                                    c.ident_f[:, :], start=True, stop=True)
                if last is None:
                    b.wait(t, "rpe", pe_base + r)
                    last = t.nop()
                b.inc(last, "rpe", 1)
            assert b.getcnt("rpe") - pe_base == nround

        def ve(v):
            if pass2:
                b.wait(v, "mkl", None)

            def I(op):
                b.inc(op, "rdv", 1)

            for r in range(nround):
                b.wait(v, "rpe", pe_base + r)
                b.wait(v, "rac", ac_base + ac_cum[r])
                did = False
                s = r - 6
                if 0 <= s < nstep:
                    did = True
                    ck, h = chh(s)
                    g = half * NCK + ck
                    pin, pout = (g + 1) % 2, g % 2
                    if pass2 and "out" not in _X:
                        b.wait(v, "rdv")
                        I(v.tensor_scalar_add(c.rinv[:, 0:1],
                                              NUM(s)[0:128, 64:65], SSD_EPS))
                        b.wait(v, "rdv")
                        I(v.reciprocal(c.rinv[:, 0:1], c.rinv[:, 0:1]))
                        b.wait(v, "rdv")
                        I(v.tensor_scalar_mul(
                            c.scanout[:, half * 8 + ck, h * 64:h * 64 + 64],
                            NUM(s)[0:128, 0:64], c.rinv[:, 0:1]))
                    elif pass2:
                        I(v.tensor_copy(c.rinv[:, 0:1], c.onescol_f[:, :]))
                        I(v.tensor_copy(c.rinv[:, 0:1], c.onescol_f[:, :]))
                        I(v.tensor_copy(c.scanout[:, half * 8 + ck,
                                                  h * 64:h * 64 + 1],
                                        c.onescol_f[:, :]))
                    I(v.scalar_tensor_tensor(
                        c.stt[pout][:, h, :], c.stt[pin][:, h, :],
                        c.lamC[:, s % 8:s % 8 + 1], PST(s),
                        ALU.mult, ALU.add))
                s = r - 4
                if 0 <= s < nstep:
                    did = True
                    ck, h = chh(s)
                    g = half * NCK + ck
                    pin = (g + 1) % 2
                    if pass2:
                        b.wait(v, "rdv")
                        if "att" not in _X:
                            I(v.tensor_tensor(c.at_sb[:, s % 2, :], AT(s),
                                              c.d_sb[:, s % 2, :], ALU.mult))
                        else:
                            I(v.tensor_copy(c.at_sb[:, s % 2, 0:1],
                                            c.onescol_f[:, :]))
                        I(v.tensor_tensor(c.q1_sb[:, s % 2, :],
                                          c.qc_sb[:, s % 4, :],
                                          c.lam_sb[:, s % 4, :], ALU.mult))
                        I(v.tensor_copy(c.stt_bf[:, s % 2, :],
                                        c.stt[pin][:, h, :]))
                    I(v.tensor_scalar_mul(
                        c.k1_sb[:, s % 2, :],
                        c.k_tm[:, ck, h * 64:h * 64 + 64],
                        c.e_part[:, s % 2:s % 2 + 1]))
                s = r - 2
                if 0 <= s < nstep:
                    did = True
                    ck, h = chh(s)
                    I(v.tensor_scalar(c.tmp_sb[:, s % 2:s % 2 + 1],
                                      M2(s)[0:128, 127:128],
                                      c.cum_tm[:, ck, h:h + 1], None,
                                      ALU.subtract))
                    if pass2:
                        I(v.tensor_scalar(c.dl2_sb[:, s % 2, :], M2(s),
                                          c.cum_tm[:, ck, h:h + 1], None,
                                          ALU.subtract))
                        b.wait(v, "rdv")
                        I(v.tensor_tensor(c.dl2_sb[:, s % 2, :],
                                          c.dl2_sb[:, s % 2, :],
                                          c.maskneg[:, :], ALU.min))
                        hb = 64 * (h % 2)
                        I(v.tensor_copy(
                            c.qc_sb[:, s % 4, :],
                            c.q_fm[hb:hb + 64, h // 2,
                                   ck * 128:ck * 128 + 128]))
                s = r
                if 0 <= s < nstep:
                    did = True
                    ck, h = chh(s)
                    I(v.tensor_copy(
                        c.b1_sb[:, s % 2, :],
                        c.cum_tm[:, ck, h:h + 1].broadcast_to([128, 128])))
                if not did:
                    b.wait(v, "rdv")
                    I(v.nop())
            assert b.getcnt("rdv") - dv_base == dv_cum[nround], \
                (b.getcnt("rdv") - dv_base, dv_cum[nround])

        def sc(s_):
            def I(op):
                b.inc(op, "rac", 1)

            for r in range(nround):
                b.wait(s_, "rpe", pe_base + r)
                b.wait(s_, "rdv", dv_base + dv_cum[r])
                did = False
                s = r - 3
                if 0 <= s < nstep:
                    did = True
                    if pass2:
                        I(s_.activation(c.d_sb[:, s % 2, :],
                                        c.dl2_sb[:, s % 2, :], AF.Exp))
                    I(s_.activation(c.e_part[:, s % 2:s % 2 + 1],
                                    c.tmp_sb[:, s % 2:s % 2 + 1], AF.Exp))
                    I(s_.activation(c.lamC[:, s % 8:s % 8 + 1],
                                    M2(s)[0:64, 127:128], AF.Exp))
                    if pass2:
                        I(s_.activation(c.lam_sb[:, s % 4, :],
                                        M2(s)[0:64, :], AF.Exp))
                if not did:
                    b.wait(s_, "rac")
                    I(s_.nop())
            assert b.getcnt("rac") - ac_base == ac_cum[nround], \
                (b.getcnt("rac") - ac_base, ac_cum[nround])

        blk.tensor(pe)
        blk.vector(ve)
        blk.scalar(sc)
    b.barrier()


def emit_scanout_tr(b, c):
    nc = b.nc
    with nc.Block() as blk:
        def pe(t):
            for j in range(128):
                tt, pc = divmod(j, 8)
                if j >= 1:
                    b.wait(t, "sote", j)
                mm = t.matmul(c.pX[0:128, 0:128],
                              c.scanout[:, tt, pc * 128:pc * 128 + 128],
                              c.ident_bf[:, :], start=True, stop=True)
                b.inc(mm, "sot", 1)

        def ve(v):
            for j in range(128):
                tt, pc = divmod(j, 8)
                b.wait(v, "sot", j + 1)
                b.inc(v.tensor_copy(
                    c.scanout_fm[:, pc, tt * 128:tt * 128 + 128],
                    c.pX[0:128, 0:128]), "sote", 1)

        blk.tensor(pe)
        blk.vector(ve)
    b.barrier()


def emit_phase_D(b, io):
    nc = b.nc
    stream = b.sb("streamD", OFF_STREAM, [128, NCH, T], F32)
    c = alloc_ssd(b, with_pass2=False)
    b.setup_consts(c.cstD, [PLN_EPS])
    with nc.Block() as blk:
        def sy(e):
            i = e.dma_start(out=stream[:, :, :],
                            in_=io["s2"].ap().rearrange("(c p) t -> p c t",
                                                        p=128))
            b.inc(i, "ldx", 16)
            i = e.dma_start(out=c.ident_f[:, :], in_=io["ident"][:, :])
            b.inc(i, "ldi", 16)

        def gp(e):
            e.memset(c.onescol_f[:, :], 1.0)
            e.memset(c.onesrow_f[:, :], 1.0)
            b.emit_const_memsets(e)
            e.memset(c.v_aug[:, :, :, :], 1.0)
            e.memset(c.tot_la[:, :], 0.0)
            e.memset(c.stt[0][:, :, :], 0.0)
            e.memset(c.stt[1][:, :, :], 0.0)
            i = e.dma_start(out=c.ba2[:, :], in_=io["ba_col"][:, :])
            b.inc(i, "ldb", 16)

        def ve(v):
            b.wait(v, "ldb", 16)
            b.inc(v.tensor_scalar_add(c.ba2[:, :], c.ba2[:, :], 2.0),
                  "ldba", 1)

        blk.sync(sy)
        blk.gpsimd(gp)
        blk.vector(ve)
    b.barrier()

    for half in range(NHALF):
        emit_ssd_qkva(b, c, io, stream, half, with_q=False)
        emit_ssd_scan(b, c, io, half, pass2=False)

    with nc.Block() as blk:
        def sc(s):
            b.inc(s.activation(c.tot_la[:, 0:1], c.tot_la[:, 0:1], AF.Exp),
                  "py0", 1)

        def pe(t):
            b.wait(t, "py0", 1)
            b.wait(t, "ldi", 16)
            b.inc(t.transpose(c.pM2[0:1, 0:16], c.tot_la[0:16, 0:1],
                              c.ident_f[0:16, 0:16]), "py1", 1)

        def ve(v):
            b.wait(v, "py1", 1)
            b.inc(v.tensor_copy(c.lrow[0:1, :], c.pM2[0:1, 0:16]), "py2", 1)

        blk.scalar(sc)
        blk.tensor(pe)
        blk.vector(ve)
    b.barrier()
    with nc.Block() as blk:
        def pe(t):
            b.inc(t.matmul(c.pM2[0:64, 512:528], c.onesrow_f[:1, 0:64],
                           c.lrow[0:1, :], start=True, stop=True), "py3", 1)

        def ve(v):
            b.wait(v, "py3", 1)
            b.inc(v.tensor_copy(c.lamrep[:, :], c.pM2[0:64, 512:528]),
                  "py4", 1)

        blk.tensor(pe)
        blk.vector(ve)
    b.barrier()
    with nc.Block() as blk:
        def sy(e):
            i = e.dma_start(out=io["summ"].ap(),
                            in_=c.stt[1][:, :, :].rearrange(
                                "p h n -> p (h n)"))
            b.inc(i, "stD", 16)
            i = e.dma_start(out=io["lamr"][:, :], in_=c.lamrep[:, :])
            b.inc(i, "stD", 16)
            b.wait(e, "stD", 32)

        blk.sync(sy)


def emit_phase_E(b, io):
    nc = b.nc
    stream = b.sb("streamE", OFF_STREAM, [128, NCH, T], F32)
    c = alloc_ssd(b, with_pass2=True)
    b.setup_consts(c.cstD, [PLN_EPS])
    with nc.Block() as blk:
        def sy(e):
            i = e.dma_start(out=stream[:, :, :],
                            in_=io["s2"].ap().rearrange("(c p) t -> p c t",
                                                        p=128))
            b.inc(i, "ldx", 16)
            i = e.dma_start(out=c.ident_f[:, :], in_=io["ident"][:, :])
            b.inc(i, "ldi", 16)
            i = e.dma_start(out=c.meta_sb[:, :], in_=io["meta"][:, :])
            b.inc(i, "ldi", 16)

        def gp(e):
            e.memset(c.onescol_f[:, :], 1.0)
            e.memset(c.onesrow_f[:, :], 1.0)
            b.emit_const_memsets(e)
            e.memset(c.v_aug[:, :, :, :], 1.0)
            e.memset(c.tot_la[:, :], 0.0)
            e.memset(c.stt[1][:, :, :], 0.0)
            e.memset(c.sacc[:, :, :], 0.0)
            i = e.dma_start(out=c.ba2[:, :], in_=io["ba_col"][:, :])
            b.inc(i, "ldb", 16)
            i = e.dma_start(out=c.ident_bf[:, :], in_=io["ident"][:, :])
            b.inc(i, "ldb", 16)

        def ve(v):
            b.wait(v, "ldb", 32)
            b.inc(v.tensor_scalar_add(c.ba2[:, :], c.ba2[:, :], 2.0),
                  "ldba", 1)

        blk.sync(sy)
        blk.gpsimd(gp)
        blk.vector(ve)
    b.barrier()

    if True:
      with nc.Block() as blk:
        def gp(e):
            for p_ in range(4):
                b.wait(e, "pcv", p_ * 17)
                i = e.dma_start(out=c.sldp[:, :, :].rearrange(
                                    "p h n -> p (h n)"),
                                in_=io["summ_all"].ap()[p_ * 64:p_ * 64 + 64,
                                                        :])
                b.inc(i, "pcl", 16)
                i = e.dma_start(out=c.lamp[:, :],
                                in_=io["lam_all"].ap()[p_ * 64:p_ * 64 + 64,
                                                       :])
                b.inc(i, "pcl", 16)

        def ve(v):
            b.wait(v, "ldi", 32)
            for p_ in range(4):
                b.wait(v, "pcl", (p_ + 1) * 32)
                sel_src = c.stt[1] if p_ % 2 == 0 else c.sel2
                sel_dst = c.sel2 if p_ % 2 == 0 else c.stt[1]
                acc_src = c.sacc if p_ % 2 == 0 else c.sacc2
                acc_dst = c.sacc2 if p_ % 2 == 0 else c.sacc
                b.inc(v.scalar_tensor_tensor(
                    sel_dst[:, :, :].rearrange("p h n -> p (h n)"),
                    acc_src[:, :, :].rearrange("p h n -> p (h n)"),
                    c.meta_sb[:, p_:p_ + 1],
                    sel_src[:, :, :].rearrange("p h n -> p (h n)"),
                    ALU.mult, ALU.add), "pcv", 1)
                b.wait(v, "pcv")
                for h in range(16):
                    b.inc(v.scalar_tensor_tensor(
                        acc_dst[:, h, :], acc_src[:, h, :],
                        c.lamp[:, h:h + 1], c.sldp[:, h, :],
                        ALU.mult, ALU.add), "pcv", 1)
                    b.wait(v, "pcv")

        blk.gpsimd(gp)
        blk.vector(ve)
      b.barrier()

    for half in range(NHALF):
        emit_ssd_qkva(b, c, io, stream, half, with_q=True)
        emit_kT_pre(b, c, half)
        emit_ssd_scan(b, c, io, half, pass2=True)
    if "tails" not in _X:
        emit_scanout_tr(b, c)

    if "tails" not in _X:
      with nc.Block() as blk:
        def gp(e):
            i = e.dma_start(out=c.slab[:, :, :],
                            in_=io["wo"].ap().rearrange("(c p) n -> p c n",
                                                        p=128))
            b.inc(i, "wol", 16)
        blk.gpsimd(gp)

        tiles = []
        for ot in range(NCH):
            for tt in range(4):
                kl = [(c.slab[:, pc, ot * 128:ot * 128 + 128],
                       c.scanout_fm[:, pc, tt * 512:tt * 512 + 512])
                      for pc in range(NCH)]
                tiles.append(((ot, tt), kl))

        def wevac(v, info, ps):
            ot, tt = info
            sl = slice(tt * 512, tt * 512 + 512)
            return v.tensor_tensor(stream[:, ot, sl], stream[:, ot, sl],
                                   ps, ALU.add)

        def pre_pe(t):
            b.wait(t, "wol", 16)

        def pre_ve(v):
            b.wait(v, "ldx", 16)
        mm_chain(b, blk, "ew", tiles, c.pM2, wevac, pre_pe=pre_pe,
                 pre_ve=pre_ve)
      b.barrier()

    if "tails" not in _X:
        sq = b.sb("sqE", P0 + 8192 + 32768, [128, 1024], F32)
        stt1 = b.sb("stt1E_", P0 + 8192, [1, T], F32)
        stt2 = b.sb("stt2E_", P0 + 16384, [1, T], F32)
        row2k = b.sb("row2kE", P0 + 24576, [1, T], F32)
        emit_postln(b, stream, sq, row2k, stt1, stt2, c.onescol_f,
                    c.onesrow_f, c.pNS, c.pAT[:, 0:512], c.pM2[:, 0:512],
                    "pln3")
    with nc.Block() as blk:
        def sy(e):
            i = e.dma_start(out=io["s3"].ap().rearrange("(c p) t -> p c t",
                                                        p=128),
                            in_=stream[:, :, :])
            b.inc(i, "stE", 16)
            b.wait(e, "stE", 16)

        blk.sync(sy)


def _mk_maskneg():
    m = np.zeros((128, 128), np.float32)
    for j in range(128):
        m[j, :j] = -1e30
    return m


def build_PD():
    b = B()
    io = {}
    io["s2"] = b.din("s2", [D, T])
    io["wk"] = b.din("wk", [D, D])
    io["wv"] = b.din("wv", [D, D])
    io["wa"] = b.din("wa", [D, H])
    io["ba_col"] = b.din("ba_col", [H, 1])
    io["ident"] = b.din("ident", [128, 128])
    io["summ"] = b.dout("summ", [64, 16 * 65])
    io["lamr"] = b.dout("lamr", [64, 16])
    emit_phase_D(b, io)
    b.nc.finalize()
    return b


def build_PE():
    b = B()
    io = {}
    io["s2"] = b.din("s2", [D, T])
    io["wq"] = b.din("wq", [D, D])
    io["wk"] = b.din("wk", [D, D])
    io["wv"] = b.din("wv", [D, D])
    io["wa"] = b.din("wa", [D, H])
    io["ba_col"] = b.din("ba_col", [H, 1])
    io["wo"] = b.din("wo", [D, D])
    io["ident"] = b.din("ident", [128, 128])
    io["maskneg"] = b.din("maskneg", [128, 128])
    io["meta"] = b.din("meta", [64, 4])
    io["summ_all"] = b.din("summ_all", [256, 16 * 65])
    io["lam_all"] = b.din("lam_all", [256, 16])
    io["s3"] = b.dout("s3", [D, T])
    emit_phase_E(b, io)
    b.nc.finalize()
    return b



def emit_phase_F(b, io):
    nc = b.nc
    stream = b.sb("streamF", OFF_STREAM, [128, NCH, T], F32)
    so = OFF_SMALL
    onescol_f = b.sb("onescol_fF", so, [128, 1], F32); so += 32
    onesrow_f = b.sb("onesrow_fF", so, [1, 128], F32); so += 512
    ident_f = b.sb("ident_fF", so, [128, 128], F32); so += 512
    wr_sb = b.sb("wr_sbF", so, [128, NCH, 8], F32); so += 256
    wg_sb = b.sb("wg_sbF", so, [128, NCH, 8], F32); so += 256
    negm = b.sb("negmF", so, [128, 1], F32); so += 32
    den_c = b.sb("den_cF", so, [128, 1], F32); so += 32
    rp_sb = b.sb("rp_sbF", so, [128, 8], F32); so += 32
    gw_sb = b.sb("gw_sbF", so, [128, 8], F32); so += 32
    cw_sb = b.sb("cw_sbF", so, [128, 8], F32); so += 32
    cwt = b.sb("cwtF", so, [128, 8], F32); so += 32
    cstF = b.sb("cstF", so, [128, 8], F32); so += 32
    assert so <= P0, so

    p = P0
    ybf = b.sb("ybfF", p, [128, NCH, T], BF16); p += 32768
    hsl = b.sb("hslF", p, [128, NCH, T], BF16); p += 32768
    cw_tm = b.sb("cw_tmF", p, [128, 16, 8], F32); p += 512
    cwrow = b.sb("cwrowF", p, [1, T], F32); p += 8192
    cwb = b.sb("cwbF", p, [128, T], F32); p += 8192
    gs_sb = b.sb("gs_sbF", p, [128, 2, 512], BF16); p += 2048
    uc_sb = b.sb("uc_sbF", p, [128, 2, 512], BF16); p += 2048
    wgate = b.sb("wgateF", p, [128, 2, NCH, 512], BF16)
    stt1 = b.sb("stt1F", p, [1, T], F32)
    stt2 = b.sb("stt2F", p + 8192, [1, T], F32)
    p += 16384
    wup = b.sb("wupF", p, [128, 2, NCH, 512], BF16)
    row2k = b.sb("row2kF", p, [1, T], F32)
    sq = b.sb("sqF", p + 8192, [128, 1024], F32)
    p += 16384
    wdown = b.sb("wdownF", p, [128, 2, 4, D], BF16); p += 16384
    assert p <= SB_TOP, p

    pA = b.psum("pAF")
    pB = b.psum("pBF")
    pC = b.psum("pCF")
    pD = b.psum("pDF", (128, 512))
    b.setup_consts(cstF, [PLN_EPS])

    with nc.Block() as blk:
        def sy(e):
            i = e.dma_start(out=stream[:, :, :],
                            in_=io["s3"].ap().rearrange("(c p) t -> p c t",
                                                        p=128))
            b.inc(i, "ldx", 16)
            i = e.dma_start(out=ident_f[:, :], in_=io["ident"][:, :])
            b.inc(i, "ldk", 16)

        def gp(e):
            e.memset(onescol_f[:, :], 1.0)
            e.memset(onesrow_f[:, :], 1.0)
            b.emit_const_memsets(e)
            i = e.dma_start(out=wr_sb[:, :, :],
                            in_=io["w_router"].ap().rearrange(
                                "(c p) n -> p c n", p=128))
            b.inc(i, "ldr", 16)
            i = e.dma_start(out=wg_sb[:, :, :],
                            in_=io["w_gating"].ap().rearrange(
                                "(c p) n -> p c n", p=128))
            b.inc(i, "ldr", 16)

        def ve(v):
            b.wait(v, "ldx", 16)
            for i in range(32):
                tt, cc = divmod(i, NCH)
                b.inc(v.tensor_copy(ybf[:, cc, tt * 512:tt * 512 + 512],
                                    stream[:, cc, tt * 512:tt * 512 + 512]),
                      "yc", 1)

        blk.sync(sy)
        blk.gpsimd(gp)
        blk.vector(ve)
    b.barrier()

    with nc.Block() as blk:
        def pe(t):
            b.wait(t, "ldr", 32)
            b.wait(t, "ldk", 16)
            for i in range(16):
                bank = i % 2
                for cc in range(NCH):
                    t.matmul(pA[0:128, bank * 512:bank * 512 + 8],
                             stream[:, cc, i * 128:i * 128 + 128],
                             wr_sb[:, cc, :],
                             start=(cc == 0), stop=(cc == NCH - 1))
                for cc in range(NCH):
                    mm = t.matmul(pA[0:128, bank * 512 + 8:bank * 512 + 16],
                                  stream[:, cc, i * 128:i * 128 + 128],
                                  wg_sb[:, cc, :],
                                  start=(cc == 0), stop=(cc == NCH - 1))
                b.inc(mm, "f0m", 1)
                b.wait(t, "f0c", i + 1)

        def ve(v):
            def chain(op):
                b.inc(op, "f0v", 1)
                b.wait(v, "f0v")
            for i in range(16):
                bank = i % 2
                b.wait(v, "f0m", i + 1)
                chain(v.reduce_max(negm[:, 0:1],
                                   pA[0:128, bank * 512:bank * 512 + 8],
                                   axis=AX.X, negate=True))
                b.inc(v.tensor_scalar(rp_sb[:, :],
                                      pA[0:128, bank * 512:bank * 512 + 8],
                                      negm[:, 0:1], None, ALU.add), "f0k", 1)
                b.wait(v, "f0x", 2 * i + 1)
                chain(v.reduce_sum(den_c[:, 0:1], rp_sb[:, :], axis=AX.X))
                chain(v.reciprocal(den_c[:, 0:1], den_c[:, 0:1]))
                chain(v.tensor_scalar_mul(rp_sb[:, :], rp_sb[:, :],
                                          den_c[:, 0:1]))
                chain(v.reduce_max(negm[:, 0:1],
                                   pA[0:128,
                                      bank * 512 + 8:bank * 512 + 16],
                                   axis=AX.X, negate=True))
                b.inc(v.tensor_scalar(gw_sb[:, :],
                                      pA[0:128,
                                         bank * 512 + 8:bank * 512 + 16],
                                      negm[:, 0:1], None, ALU.add), "f0k", 1)
                b.wait(v, "f0x", 2 * i + 2)
                chain(v.reduce_sum(den_c[:, 0:1], gw_sb[:, :], axis=AX.X))
                chain(v.reciprocal(den_c[:, 0:1], den_c[:, 0:1]))
                chain(v.tensor_scalar_mul(gw_sb[:, :], gw_sb[:, :],
                                          den_c[:, 0:1]))
                chain(v.tensor_tensor(cw_sb[:, :], rp_sb[:, :], gw_sb[:, :],
                                      ALU.mult))
                chain(v.reduce_sum(den_c[:, 0:1], cw_sb[:, :], axis=AX.X))
                chain(v.tensor_scalar_add(den_c[:, 0:1], den_c[:, 0:1],
                                          1e-9))
                chain(v.reciprocal(den_c[:, 0:1], den_c[:, 0:1]))
                b.inc(v.tensor_scalar_mul(cw_tm[:, i, :], cw_sb[:, :],
                                          den_c[:, 0:1]), "f0c", 1)

        def sc(s):
            for i in range(16):
                b.wait(s, "f0k", 2 * i + 1)
                b.inc(s.activation(rp_sb[:, :], rp_sb[:, :], AF.Exp),
                      "f0x", 1)
                b.wait(s, "f0k", 2 * i + 2)
                b.inc(s.activation(gw_sb[:, :], gw_sb[:, :], AF.Exp,
                                   scale=10.0), "f0x", 1)

        blk.tensor(pe)
        blk.vector(ve)
        blk.scalar(sc)
    b.barrier()

    NSLAB = 4
    with nc.Block() as blk:
        def gp(e_):
            for e in range(8):
                for fs in range(NSLAB):
                    for half in range(2):
                        idx = (e * NSLAB + fs) * 2 + half
                        buf = idx % 2
                        if idx >= 2:
                            b.wait(e_, "fhm", (idx - 1) * 32)
                        f0 = fs * 1024 + half * 512
                        i = e_.dma_start(
                            out=wgate[:, buf, :, :],
                            in_=io["w_gate"].ap()[e, :, f0:f0 + 512]
                            .rearrange("(c p) n -> p c n", p=128))
                        b.inc(i, "fwl%d" % buf, 16)
                        i = e_.dma_start(
                            out=wup[:, buf, :, :],
                            in_=io["w_up"].ap()[e, :, f0:f0 + 512]
                            .rearrange("(c p) n -> p c n", p=128))
                        b.inc(i, "fwl%d" % buf, 16)
                        if e * NSLAB + fs >= 1:
                            b.wait(e_, "fde", (e * NSLAB + fs) * 32)
                        i = e_.dma_start(
                            out=wdown[:, half, :, :],
                            in_=io["w_down"].ap()[e, f0:f0 + 512, :]
                            .rearrange("(c p) n -> p c n", p=128))
                        b.inc(i, "fwd%d" % half, 16)

        def pe(t):
            b.wait(t, "ldk", 16)
            for e in range(8):
                for i in range(16):
                    j = e * 16 + i
                    if j >= 1:
                        b.wait(t, "cre", j)
                    mm = t.transpose(pD[0:1, (j % 2) * 128:(j % 2) * 128 + 128],
                                     cw_tm[:, i, e:e + 1], ident_f[:, :])
                    b.inc(mm, "crt", 1)
                b.wait(t, "cre", (e + 1) * 16)
                b.wait(t, "fde", e * NSLAB * 32)
                for tt in range(4):
                    j = e * 4 + tt
                    if j >= 2:
                        b.wait(t, "cbv", j - 1)
                    mm = t.matmul(pC[:, (tt % 2) * 512:(tt % 2) * 512 + 512],
                                  onesrow_f[:1, :],
                                  cwrow[0:1, tt * 512:tt * 512 + 512],
                                  start=True, stop=True)
                    b.inc(mm, "cbm", 1)
                for fs in range(NSLAB):
                    for half in range(2):
                        idx = (e * NSLAB + fs) * 2 + half
                        buf = idx % 2
                        b.wait(t, "fwl%d" % buf, (idx // 2 + 1) * 32)
                        for ft in range(4):
                            for tt in range(4):
                                for gu in range(2):
                                    j = idx * 32 + (ft * 4 + tt) * 2 + gu
                                    j2 = j // 2
                                    wt = wgate if gu == 0 else wup
                                    ps = pA if j2 % 2 == 0 else pB
                                    bank = gu
                                    if gu == 0 and j2 >= 2:
                                        b.wait(t, "fhe", j2 - 1)
                                    for cc in range(NCH):
                                        mm = t.matmul(
                                            ps[:, bank * 512:bank * 512 + 512],
                                            wt[:, buf, cc,
                                               ft * 128:ft * 128 + 128],
                                            ybf[:, cc,
                                                tt * 512:tt * 512 + 512],
                                            start=(cc == 0),
                                            stop=(cc == NCH - 1))
                                    b.inc(mm, "fhm", 1)
                    sidx = e * NSLAB + fs
                    b.wait(t, "fhe", (sidx + 1) * 32)
                    b.wait(t, "fwd0", (sidx + 1) * 16)
                    b.wait(t, "fwd1", (sidx + 1) * 16)
                    if fs == 0:
                        b.wait(t, "cbv", (e + 1) * 4)
                    for ot in range(NCH):
                        for tt in range(4):
                            j = sidx * 32 + ot * 4 + tt
                            bank = j % 2
                            if j >= 2:
                                b.wait(t, "fde", j - 1)
                            for cc in range(NCH):
                                mm = t.matmul(
                                    pC[:, bank * 512:bank * 512 + 512],
                                    wdown[:, cc // 4, cc % 4,
                                          ot * 128:ot * 128 + 128],
                                    hsl[:, cc, tt * 512:tt * 512 + 512],
                                    start=(cc == 0), stop=(cc == NCH - 1))
                            b.inc(mm, "fdm", 1)

        def ve(v):
            b.wait(v, "ldx", 16)
            for e in range(8):
                for i in range(16):
                    j = e * 16 + i
                    b.wait(v, "crt", j + 1)
                    b.inc(v.tensor_copy(cwrow[0:1, i * 128:i * 128 + 128],
                                        pD[0:1, (j % 2) * 128:
                                           (j % 2) * 128 + 128]), "cre", 1)
                for tt in range(4):
                    j = e * 4 + tt
                    b.wait(v, "cbm", j + 1)
                    b.inc(v.tensor_copy(cwb[:, tt * 512:tt * 512 + 512],
                                        pC[:, (tt % 2) * 512:
                                           (tt % 2) * 512 + 512]), "cbv", 1)
                for fs in range(NSLAB):
                    for half in range(2):
                        idx = (e * NSLAB + fs) * 2 + half
                        for ft in range(4):
                            for tt in range(4):
                                j2 = idx * 16 + ft * 4 + tt
                                jg = idx * 32 + (ft * 4 + tt) * 2
                                ps = pA if (jg // 2) % 2 == 0 else pB
                                par = j2 % 2
                                fc = half * 4 + ft
                                b.wait(v, "fhm", jg + 2)
                                b.wait(v, "fhx", j2 + 1)
                                b.inc(v.tensor_tensor(
                                    uc_sb[:, par, :],
                                    ps[:, 512:1024],
                                    cwb[:, tt * 512:tt * 512 + 512],
                                    ALU.mult), "fhv", 1)
                                b.wait(v, "fhv")
                                b.inc(v.tensor_tensor(
                                    hsl[:, fc, tt * 512:tt * 512 + 512],
                                    gs_sb[:, par, :], uc_sb[:, par, :],
                                    ALU.mult), "fhe", 1)
                    sidx = e * NSLAB + fs
                    for ot in range(NCH):
                        for tt in range(4):
                            j = sidx * 32 + ot * 4 + tt
                            bank = j % 2
                            b.wait(v, "fdm", j + 1)
                            sl = slice(tt * 512, tt * 512 + 512)
                            b.inc(v.tensor_tensor(
                                stream[:, ot, sl], stream[:, ot, sl],
                                pC[:, bank * 512:bank * 512 + 512],
                                ALU.add), "fde", 1)

        def sc(s):
            for e in range(8):
                for fs in range(NSLAB):
                    for half in range(2):
                        idx = (e * NSLAB + fs) * 2 + half
                        for ft in range(4):
                            for tt in range(4):
                                j2 = idx * 16 + ft * 4 + tt
                                jg = idx * 32 + (ft * 4 + tt) * 2
                                ps = pA if (jg // 2) % 2 == 0 else pB
                                par = j2 % 2
                                b.wait(s, "fhm", jg + 1)
                                if j2 >= 2:
                                    b.wait(s, "fhe", j2 - 1)
                                b.inc(s.activation(gs_sb[:, par, :],
                                                   ps[:, 0:512], AF.Silu),
                                      "fhx", 1)

        blk.gpsimd(gp)
        blk.tensor(pe)
        blk.vector(ve)
        blk.scalar(sc)
    b.barrier()

    emit_postln(b, stream, sq, row2k, stt1, stt2, onescol_f, onesrow_f,
                pA, pB[:, 0:512], pB[:, 512:1024], "pln4")
    with nc.Block() as blk:
        def sy(e):
            i = e.dma_start(out=io["out_fm"].ap().rearrange(
                                "(c p) t -> p c t", p=128),
                            in_=stream[:, :, :])
            b.inc(i, "stF", 16)
            b.wait(e, "stF", 16)

        blk.sync(sy)


def build_PF():
    b = B()
    io = {}
    io["s3"] = b.din("s3", [D, T])
    io["w_router"] = b.din("w_router", [D, 8])
    io["w_gating"] = b.din("w_gating", [D, 8])
    io["w_gate"] = b.din("w_gate", [8, D, 4096])
    io["w_up"] = b.din("w_up", [8, D, 4096])
    io["w_down"] = b.din("w_down", [8, 4096, D])
    io["ident"] = b.din("ident", [128, 128])
    io["out_fm"] = b.dout("out_fm", [D, T])
    emit_phase_F(b, io)
    b.nc.finalize()
    return b


def build_BC():
    b = B()
    io = {}
    io["x_fm"] = b.din("x_fm", [D, T])
    io["qf"] = b.din("qf", [D, T], BF16)
    io["k_b"] = b.din("k_b", [D, S], BF16)
    io["v_b"] = b.din("v_b", [S, D], BF16)
    io["kqn_all"] = b.din("kqn_all", [4, NCH])
    io["wout_t"] = b.din("wout_t", [D, D])
    io["ident"] = b.din("ident", [128, 128])
    io["hsum"] = b.din("hsum", [128, 2])
    io["s1"] = b.dint("s1", [D, T])
    for n, sh in [("W_t", [D, D]), ("wp_t", [D, 256]), ("bp_col", [128, 2]),
                  ("bp_row", [1, 256]), ("sim_fm", [128, NCH]),
                  ("temp", [1, 1]), ("wd_t", [D, 15]), ("wu_t", [15, D]),
                  ("selmap", [4, 15])]:
        io[n] = b.din(n, sh)
    io["s2"] = b.dout("s2", [D, T])
    b.ns = "B_"
    emit_phase_B(b, io)
    b.next_phase("C_")
    emit_phase_C(b, io)
    b.nc.finalize()
    return b


def build_BCD():
    b = B()
    io = {}
    io["x_fm"] = b.din("x_fm", [D, T])
    io["qf"] = b.din("qf", [D, T], BF16)
    io["k_b"] = b.din("k_b", [D, S], BF16)
    io["v_b"] = b.din("v_b", [S, D], BF16)
    io["kqn_all"] = b.din("kqn_all", [4, NCH])
    io["wout_t"] = b.din("wout_t", [D, D])
    io["ident"] = b.din("ident", [128, 128])
    io["hsum"] = b.din("hsum", [128, 2])
    io["s1"] = b.dint("s1", [D, T])
    for n, sh in [("W_t", [D, D]), ("wp_t", [D, 256]), ("bp_col", [128, 2]),
                  ("bp_row", [1, 256]), ("sim_fm", [128, NCH]),
                  ("temp", [1, 1]), ("wd_t", [D, 15]), ("wu_t", [15, D]),
                  ("selmap", [4, 15])]:
        io[n] = b.din(n, sh)
    io["s2"] = b.dout("s2", [D, T])
    for n, sh in [("wk", [D, D]), ("wv", [D, D]), ("wa", [D, H]),
                  ("ba_col", [H, 1])]:
        io[n] = b.din(n, sh)
    io["summ"] = b.dout("summ", [64, 16 * 65])
    io["lamr"] = b.dout("lamr", [64, 16])
    b.ns = "B_"
    emit_phase_B(b, io)
    b.next_phase("C_")
    emit_phase_C(b, io)
    b.next_phase("D_")
    emit_phase_D(b, io)
    b.nc.finalize()
    return b


def build_EF():
    b = B()
    io = {}
    io["s2"] = b.din("s2", [D, T])
    for n, sh in [("wq", [D, D]), ("wk", [D, D]), ("wv", [D, D]),
                  ("wa", [D, H]), ("ba_col", [H, 1]), ("wo", [D, D]),
                  ("ident", [128, 128]), ("maskneg", [128, 128]),
                  ("meta", [64, 4]), ("summ_all", [256, 16 * 65]),
                  ("lam_all", [256, 16])]:
        io[n] = b.din(n, sh)
    io["s3"] = b.dint("s3", [D, T])
    for n, sh in [("w_router", [D, 8]), ("w_gating", [D, 8]),
                  ("w_gate", [8, D, 4096]), ("w_up", [8, D, 4096]),
                  ("w_down", [8, 4096, D])]:
        io[n] = b.din(n, sh)
    io["out_fm"] = b.dout("out_fm", [D, T])
    b.ns = "E_"
    emit_phase_E(b, io)
    b.next_phase("F_")
    emit_phase_F(b, io)
    b.nc.finalize()
    return b


def _mk_selmap():
    sel = np.zeros((4, 15), np.float32)
    r0 = 0
    for e, r in enumerate([1, 2, 4, 8]):
        sel[e, r0:r0 + r] = 1.0
        r0 += r
    return sel


def kernel(**inputs):
    g, pre = _prep_shared(inputs)
    consts = [_host_consts(core) for core in range(8)]
    x = g["L0_attn_x"].astype(np.float32)
    maskneg = _mk_maskneg()
    selmap = _mk_selmap()
    ba_col = _c(g["L1_attn_ba"].astype(np.float32).reshape(H, 1))

    def shard(arr):
        out = []
        for core in range(NCORES):
            bcore, pos0 = core // 4, (core % 4) * T
            out.append(_c(arr[bcore, pos0:pos0 + T, :].T))
        return out

    x_fm = shard(x)

    bA = build_PA()
    mapsA = []
    for core in range(NCORES):
        m = {"x_fm": x_fm[core]}
        for n in ["wdq_t", "wdkv_t", "upq_t", "upk_t", "upv_t",
                  "l0_gamma", "l0_beta"]:
            m[n] = pre[n]
        for n in ["cos2", "sin2", "p2t", "ident", "hsum"]:
            m[n] = consts[core][n]
        mapsA.append(m)
    resA = run_bass_kernel_spmd(bA.nc, mapsA, core_ids=list(range(NCORES)))
    rA = resA.results

    bBCD = build_BCD()
    mapsBCD = []
    for core in range(NCORES):
        bcore = core // 4
        grp = range(4 * bcore, 4 * bcore + 4)
        mapsBCD.append(dict(
            x_fm=x_fm[core], qf=_c(rA[core]["qf"]),
            k_b=_c(np.concatenate([rA[i]["kf"] for i in grp], axis=1)),
            v_b=_c(np.concatenate([rA[i]["vt"] for i in grp], axis=0)),
            kqn_all=_c(np.concatenate([rA[i]["kqn"] for i in grp], 0)),
            wout_t=pre["wout_t"], ident=consts[core]["ident"],
            hsum=consts[core]["hsum"],
            W_t=pre["W_t"], wp_t=pre["wp_t"], bp_col=pre["bp_col"],
            bp_row=pre["bp_row"], sim_fm=pre["sim_fm"], temp=pre["temp"],
            wd_t=pre["wd_t"], wu_t=pre["wu_t"], selmap=selmap,
            wk=pre["wk"], wv=pre["wv"], wa=pre["wa"], ba_col=ba_col))
    resD = run_bass_kernel_spmd(bBCD.nc, mapsBCD, core_ids=list(range(NCORES)))
    s2 = [_c(resD.results[core]["s2"]) for core in range(NCORES)]

    bEF = build_EF()
    mapsEF = []
    for core in range(NCORES):
        grp = range(4 * (core // 4), 4 * (core // 4) + 4)
        mapsEF.append(dict(
            s2=s2[core], wq=pre["wq"], wk=pre["wk"], wv=pre["wv"],
            wa=pre["wa"], ba_col=ba_col, wo=pre["wo"],
            ident=consts[core]["ident"], maskneg=maskneg,
            meta=consts[core]["meta"],
            summ_all=_c(np.concatenate(
                [resD.results[i]["summ"] for i in grp], 0)),
            lam_all=_c(np.concatenate(
                [resD.results[i]["lamr"] for i in grp], 0)),
            w_router=pre["w_router"], w_gating=pre["w_gating"],
            w_gate=pre["w_gate"], w_up=pre["w_up"], w_down=pre["w_down"]))
    resEF = run_bass_kernel_spmd(bEF.nc, mapsEF, core_ids=list(range(NCORES)))

    out = np.empty((2, S, D), np.float32)
    for core in range(NCORES):
        bcore, pos0 = core // 4, (core % 4) * T
        out[bcore, pos0:pos0 + T, :] = resEF.results[core]["out_fm"].T
    return out
